# version 01
# baseline (speedup 1.0000x reference)
import contextlib
import numpy as np
import ml_dtypes

import concourse.bass as bass
import concourse.mybir as mybir
from concourse.bass_utils import run_bass_kernel_spmd

F32 = mybir.dt.float32
BF16 = mybir.dt.bfloat16
F8 = mybir.dt.float8e4
AF = mybir.ActivationFunctionType
ALU = mybir.AluOpType
AX = mybir.AxisListType

NCORES = 8
T = 2048
HALO = 3072
TT = HALO + T
NT = T // 128
NTT = TT // 128
NG = TT // 512
D = 1024
DI = 2816
H = 16
HD = 64
S = 256
ASCALE = float(HD ** -0.5)
ESHIFT = -4.0

OFFSETS_ALL = sorted(set((1 << j) * t for j in range(11) for t in range(4)))
LOCAL_OFFS = [o for o in OFFSETS_ALL if o <= 128]
LARGE_OFFS = [o for o in OFFSETS_ALL if o > 128]
LARGE_DELTA = [o // 128 for o in LARGE_OFFS]
NLO = len(LARGE_OFFS)
NW = (NLO + 3) // 4

WQ_SH = DI // NCORES


class Prog:
    ENGINES = ("sync", "vector", "scalar", "tensor", "gpsimd")

    def __init__(self, nc):
        self.nc = nc
        self.items = {e: [] for e in self.ENGINES}
        self.es = contextlib.ExitStack()
        self.sems = {}
        self.cnt = {}

    def sbuf(self, name, shape, dtype, es=None):
        return (es or self.es).enter_context(self.nc.sbuf_tensor(name, shape, dtype))

    def psum(self, name, shape, dtype=F32):
        return self.es.enter_context(self.nc.psum_tensor(name, shape, dtype))

    def sem(self, name):
        if name not in self.sems:
            self.sems[name] = self.es.enter_context(self.nc.semaphore(name))
            self.cnt[name] = 0
        return self.sems[name]

    def inc(self, name, k=1):
        self.sem(name)
        self.cnt[name] += k
        return self.cnt[name]

    def val(self, name):
        return self.cnt.get(name, 0)

    def emit(self, engine, fn):
        self.items[engine].append(fn)

    def sync_run(self, engine, fn, name=None):
        name = name or f"sr_{engine}"
        tgt = self.run(engine, fn, inc=name)
        self.wait(engine, name, tgt)
        return tgt

    def fence(self, engine, ap):
        if not hasattr(self, "fscr"):
            self.fscr = {}
        if engine not in self.fscr:
            self.fscr[engine] = self.sbuf(f"fscr_{engine}", [1, 2], F32)
        scr = self.fscr[engine]
        name = f"fence_{engine}"
        tgt = self.inc(name)
        s = self.sems[name]
        def _f(e, ap=ap, scr=scr, s=s, tgt=tgt, engine=engine):
            cp = e.copy if engine == "scalar" else e.tensor_copy
            cp(scr[0:1, 0:1], ap)
            cp(scr[0:1, 1:2], scr[0:1, 0:1]).then_inc(s, 1)
            e.wait_ge(s, tgt)
        self.emit(engine, _f)

    def wait(self, engine, sem_name, val):
        s = self.sem(sem_name)
        self.emit(engine, lambda e, s=s, val=val: e.wait_ge(s, val))

    def run(self, engine, fn, inc=None):
        if inc is None:
            self.emit(engine, fn)
            return None
        tgt = self.inc(inc)
        s = self.sems[inc]
        def wrapped(e, fn=fn, s=s):
            ins = fn(e)
            ins.then_inc(s, 1)
        self.emit(engine, wrapped)
        return tgt

    def dma(self, eng, out, in_, sem_name, wait=None, transpose=False):
        tgt = self.inc(sem_name, 16)
        s = self.sems[sem_name]
        if wait is not None:
            self.wait(eng, wait[0], wait[1])
        if transpose:
            self.emit(eng, lambda e, out=out, in_=in_, s=s:
                      e.dma_start_transpose(out, in_).then_inc(s, 16))
        else:
            self.emit(eng, lambda e, out=out, in_=in_, s=s:
                      e.dma_start(out=out, in_=in_).then_inc(s, 16))
        return tgt

    def finalize(self):
        nc = self.nc
        items = self.items
        with self.es, nc.Block() as block:
            @block.sync
            def _(q):
                for f in items["sync"]:
                    f(q)

            @block.vector
            def _(v):
                for f in items["vector"]:
                    f(v)

            @block.scalar
            def _(s):
                for f in items["scalar"]:
                    f(s)

            @block.tensor
            def _(t):
                for f in items["tensor"]:
                    f(t)

            @block.gpsimd
            def _(g):
                for f in items["gpsimd"]:
                    f(g)
        return nc


def emit_fp4_quant(p, in_b, out_b, scr, rows, W, done_sem):
    nb = W // 16
    bm, scl, invb = scr["bm"], scr["scl"], scr["invb"]
    t, ta, c, d, e_t, sg = scr["t"], scr["ta"], scr["c"], scr["d"], scr["e"], scr["sg"]
    bx = scr["bx"]
    R = rows

    p.sync_run("vector", lambda v: v.reduce_max(
        bm[:R, :nb], in_b.rearrange("p (a b) -> p a b", b=16),
        axis=AX.X, apply_absolute_value=True))
    p.sync_run("vector", lambda v: v.tensor_scalar(
        scl[:R, :nb], bm[:R, :nb], 1.0 / 6.0, 1e-12, ALU.mult, ALU.max))
    p.sync_run("vector", lambda v: v.reciprocal(bm[:R, :nb], scl[:R, :nb]))
    p.sync_run("vector", lambda v: v.tensor_copy(invb[:R, :nb], bm[:R, :nb]))

    def expand_inv(v):
        iv = invb[:R, :nb].unsqueeze(2).broadcast_to([R, nb, 16])
        return v.tensor_copy(bx[:R, :W].rearrange("p (a b) -> p a b", b=16), iv)
    p.sync_run("vector", expand_inv)
    t_tgt = p.run("vector", lambda v: v.tensor_tensor(t[:R, :W], in_b, bx[:R, :W],
                                                      ALU.mult), inc="q_t")
    p.wait("scalar", "q_t", t_tgt)
    sg_tgt = p.run("scalar", lambda s: s.activation(sg[:R, :W], t[:R, :W], AF.Sign),
                   inc="q_sg")
    p.wait("vector", "q_t", t_tgt)

    p.sync_run("vector", lambda v: (
        v.tensor_tensor(ta[:R, :W], t[:R, :W], t[:R, :W], ALU.mult),
        v.tensor_copy(invb[:R, :nb], scl[:R, :nb]))[-1])
    p.sync_run("vector", lambda v: (
        v.tensor_scalar(c[:R, :W], ta[:R, :W], 0.0625, None, ALU.is_gt),
        v.tensor_scalar(d[:R, :W], ta[:R, :W], 6.25, None, ALU.is_gt),
        v.tensor_scalar(e_t[:R, :W], ta[:R, :W], 25.0, 2.0, ALU.is_gt, ALU.mult))[-1])
    p.sync_run("vector", lambda v: (
        v.scalar_tensor_tensor(c[:R, :W], ta[:R, :W], 0.5625, c[:R, :W],
                               ALU.is_gt, ALU.add),
        v.scalar_tensor_tensor(d[:R, :W], ta[:R, :W], 12.25, d[:R, :W],
                               ALU.is_gt, ALU.add))[-1])
    p.sync_run("vector", lambda v: v.scalar_tensor_tensor(
        c[:R, :W], ta[:R, :W], 1.5625, c[:R, :W], ALU.is_gt, ALU.add))
    p.sync_run("vector", lambda v: (
        v.scalar_tensor_tensor(c[:R, :W], ta[:R, :W], 3.0625, c[:R, :W],
                               ALU.is_gt, ALU.add),
        v.tensor_tensor(e_t[:R, :W], d[:R, :W], e_t[:R, :W], ALU.add))[-1])
    p.sync_run("vector", lambda v: v.scalar_tensor_tensor(
        c[:R, :W], c[:R, :W], 0.5, e_t[:R, :W], ALU.mult, ALU.add))

    def expand_scl(v):
        sv = invb[:R, :nb].unsqueeze(2).broadcast_to([R, nb, 16])
        return v.tensor_copy(bx[:R, :W].rearrange("p (a b) -> p a b", b=16), sv)
    p.sync_run("vector", expand_scl)
    p.wait("vector", "q_sg", sg_tgt)
    p.sync_run("vector", lambda v: v.tensor_tensor(c[:R, :W], c[:R, :W], sg[:R, :W],
                                                   ALU.mult))
    return p.run("vector", lambda v: v.tensor_tensor(out_b, c[:R, :W], bx[:R, :W],
                                                     ALU.mult), inc=done_sem)


def make_fp4_scratch(p, maxw=DI):
    return dict(
        bm=p.sbuf("q_bm", [128, maxw // 16], F32),
        scl=p.sbuf("q_scl", [128, maxw // 16], F32),
        invb=p.sbuf("q_invb", [128, maxw // 16], BF16),
        t=p.sbuf("q_t", [128, maxw], BF16),
        ta=p.sbuf("q_ta", [128, maxw], BF16),
        c=p.sbuf("q_c", [128, maxw], BF16),
        d=p.sbuf("q_d", [128, maxw], BF16),
        e=p.sbuf("q_e", [128, maxw], BF16),
        sg=p.sbuf("q_sg", [128, maxw], BF16),
        bx=p.sbuf("q_bx", [128, maxw], BF16),
    )


def build_a1():
    nc = bass.Bass("TRN2", target_bir_lowering=False)
    xpad = nc.dram_tensor("xpad", [TT, D], F32, kind="ExternalInput").ap()
    maskcol = nc.dram_tensor("maskcol", [128, NTT], F32, kind="ExternalInput").ap()
    wg_sh = nc.dram_tensor("wg_sh", [WQ_SH, D], F32, kind="ExternalInput").ap()
    wu_sh = nc.dram_tensor("wu_sh", [WQ_SH, D], F32, kind="ExternalInput").ap()
    wd_sh = nc.dram_tensor("wd_sh", [128, DI], F32, kind="ExternalInput").ap()
    posb = nc.dram_tensor("posb", [44, H], F32, kind="ExternalInput").ap()
    indic = nc.dram_tensor("indic", [44, 24], BF16, kind="ExternalInput").ap()

    xnb_d = nc.dram_tensor("xnb", [TT, D], BF16, kind="ExternalOutput").ap()
    xq_d = nc.dram_tensor("xq", [T, D], BF16, kind="ExternalOutput").ap()
    wqg_d = nc.dram_tensor("wqg", [WQ_SH, D], BF16, kind="ExternalOutput").ap()
    wqu_d = nc.dram_tensor("wqu", [WQ_SH, D], BF16, kind="ExternalOutput").ap()
    wqd_d = nc.dram_tensor("wqd", [128, DI], BF16, kind="ExternalOutput").ap()
    wdrev_d = nc.dram_tensor("wdrev", [H, 384], BF16, kind="ExternalOutput").ap()
    lnw8_d = nc.dram_tensor("lnw8", [H, NLO], F32, kind="ExternalOutput").ap()

    p = Prog(nc)
    xs = [p.sbuf(f"xs{i}", [128, D], F32) for i in range(2)]
    mcol = p.sbuf("mcol", [128, NTT], F32)
    stat = p.sbuf("stat", [128, 8], F32)
    epsc = p.sbuf("epsc", [128, 1], F32)
    sq = p.sbuf("sq", [128, D], F32)
    xnb = [p.sbuf(f"xnb{i}", [128, D], BF16) for i in range(2)]
    xqb = p.sbuf("xqb", [128, D], BF16)
    scr = make_fp4_scratch(p)
    wst = p.sbuf("wst", [128, DI], BF16)
    wqb = p.sbuf("wqb", [128, DI], BF16)
    pb = p.sbuf("pb", [44, H], F32)
    pbe = p.sbuf("pbe", [44, H], BF16)
    ind_s = p.sbuf("ind_s", [44, 24], BF16)
    ws16 = p.sbuf("ws16", [16, 24], F32)
    lnw = p.sbuf("lnw", [16, 24], F32)
    wdrev = p.sbuf("wdrev_s", [16, 384], BF16)
    pspb = p.psum("pspb", [16, 24], F32)

    mtot = p.dma("sync", mcol[:], maskcol, "ld_m")
    ep = p.run("vector", lambda v: v.memset(epsc[:], 1e-5), inc="v_eps")
    p.wait("scalar", "v_eps", ep)

    for ti in range(NTT):
        buf = ti % 2
        own = ti >= HALO // 128
        w = ("st_xnb", 16 * (ti - 1)) if ti >= 2 else None
        ldv = p.dma("sync", xs[buf][:], xpad[ti * 128:(ti + 1) * 128, :], "ld_x",
                    wait=w)
        p.wait("vector", "ld_x", ldv)
        if ti == 0:
            p.wait("vector", "ld_m", mtot)

        p.emit("vector", lambda v, x=xs[buf]: v.reduce_sum(stat[:, 0:1], x[:], axis=AX.X))
        p.fence("vector", stat[0:1, 0:1])
        p.emit("vector", lambda v: v.tensor_scalar_mul(stat[:, 0:1], stat[:, 0:1], 1.0 / D))
        p.fence("vector", stat[0:1, 0:1])
        ct = p.run("vector", lambda v, x=xs[buf]: v.tensor_scalar(
            x[:], x[:], stat[:, 0:1], None, ALU.subtract), inc="v_ln")

        p.wait("scalar", "v_ln", ct)
        p.emit("scalar", lambda s: s.memzero(stat[:, 2:3]))
        p.emit("scalar", lambda s, x=xs[buf]: s.activation(
            sq[:], x[:], AF.Square, accum_out=stat[:, 2:3]))
        p.fence("scalar", stat[0:1, 2:3])
        if p.val("sr_vector"):
            p.wait("scalar", "sr_vector", p.val("sr_vector"))
        rt = p.run("scalar", lambda s: s.activation(
            stat[:, 4:5], stat[:, 2:3], AF.Sqrt, scale=1.0 / D, bias=epsc[:, 0:1]),
            inc="s_rs")

        p.wait("vector", "s_rs", rt)
        p.emit("vector", lambda v: v.reciprocal(stat[:, 5:6], stat[:, 4:5]))
        p.fence("vector", stat[0:1, 5:6])

        def norm(v, x=xs[buf], xo=xnb[buf], ti=ti):
            return v.tensor_scalar(xo[:], x[:], stat[:, 5:6], mcol[:, ti:ti + 1],
                                   ALU.mult, ALU.mult)
        nt_ = p.run("vector", norm, inc="v_nm")
        p.dma("gpsimd", xnb_d[ti * 128:(ti + 1) * 128, :], xnb[buf][:], "st_xnb",
              wait=("v_nm", nt_))
        if own:
            oi = ti - HALO // 128
            if oi > 0:
                p.wait("vector", "st_xq", 16 * oi)
            qd = emit_fp4_quant(p, xnb[buf][:], xqb[:], scr, 128, D, "q_done")
            p.dma("gpsimd", xq_d[oi * 128:(oi + 1) * 128, :], xqb[:], "st_xq",
                  wait=("q_done", qd))

    wjobs = [(wg_sh, wqg_d, WQ_SH, D), (wu_sh, wqu_d, WQ_SH, D), (wd_sh, wqd_d, 128, DI)]
    nst = 0
    for wi, (src, dst, rows, W) in enumerate(wjobs):
        r0 = 0
        while r0 < rows:
            r = min(128, rows - r0)
            wt = ("st_wq", 16 * nst) if nst > 0 else None
            ldw = p.dma("gpsimd", wst[0:r, 0:W], src[r0:r0 + r, :], "ld_w", wait=wt)
            p.wait("vector", "ld_w", ldw)
            if nst > 0:
                p.wait("vector", "st_wq", 16 * nst)
            qd = emit_fp4_quant(p, wst[0:r, 0:W], wqb[0:r, 0:W], scr, r, W, "q_done")
            p.dma("gpsimd", dst[r0:r0 + r, :], wqb[0:r, 0:W], "st_wq",
                  wait=("q_done", qd))
            nst += 1
            r0 += r

    p.dma("sync", pb[:], posb, "ld_pb")
    pbt = p.dma("sync", ind_s[:], indic, "ld_pb")
    p.wait("scalar", "ld_pb", pbt)
    pe = p.run("scalar", lambda s: s.activation(pbe[:], pb[:], AF.Exp), inc="s_pb")
    p.wait("tensor", "s_pb", pe)
    mmw = p.run("tensor", lambda t: t.matmul(pspb[:], pbe[:], ind_s[:],
                                             start=True, stop=True), inc="t_pb")
    p.wait("vector", "t_pb", mmw)

    def wsum(v):
        v.tensor_copy(ws16[:], pspb[:])
        v.memset(wdrev[:], 0.0)
        for gi, off in enumerate(LOCAL_OFFS):
            v.tensor_copy(wdrev[:, 128 + off:129 + off], ws16[:, gi:gi + 1])
        return v.tensor_copy(wdrev[:, 0:1], wdrev[:, 0:1])
    vw = p.run("vector", wsum, inc="v_pb")
    p.wait("scalar", "v_pb", vw)
    lw = p.run("scalar", lambda s: s.activation(lnw[:, 0:24], ws16[:, 0:24], AF.Ln),
               inc="s_pb")
    p.dma("gpsimd", wdrev_d, wdrev[:], "st_pb", wait=("v_pb", vw))
    p.dma("gpsimd", lnw8_d, lnw[:, len(LOCAL_OFFS):24], "st_pb", wait=("s_pb", lw))

    for sname in ["st_xnb", "st_xq", "st_wq", "st_pb"]:
        if p.val(sname):
            p.wait("sync", sname, p.val(sname))
    return p.finalize()


def build_a2():
    nc = bass.Bass("TRN2", target_bir_lowering=False)
    xnb_d = nc.dram_tensor("xnb", [TT, D], BF16, kind="ExternalInput").ap()
    wqkv_d = nc.dram_tensor("w_qkv", [D, 3 * D], F32, kind="ExternalInput").ap()
    wgate_d = nc.dram_tensor("w_gate", [D, D], F32, kind="ExternalInput").ap()
    bq_r = nc.dram_tensor("bq_r", [1, D], F32, kind="ExternalInput").ap()
    bk_r = nc.dram_tensor("bk_r", [1, D], F32, kind="ExternalInput").ap()
    bv_r = nc.dram_tensor("bv_r", [1, D], F32, kind="ExternalInput").ap()
    bg_r = nc.dram_tensor("bg_r", [1, D], F32, kind="ExternalInput").ap()
    vrow = nc.dram_tensor("validrow", [1, TT], F32, kind="ExternalInput").ap()

    q_d = nc.dram_tensor("q_fm", [D, T], BF16, kind="ExternalOutput").ap()
    k_d = nc.dram_tensor("k_fm", [D, TT], BF16, kind="ExternalOutput").ap()
    v_d = nc.dram_tensor("v_tm", [TT, D], BF16, kind="ExternalOutput").ap()
    g_d = nc.dram_tensor("gate_fm", [D, T], BF16, kind="ExternalOutput").ap()

    p = Prog(nc)
    wstage = p.sbuf("wstage", [128, 3 * D], F32)
    wqkv = p.sbuf("wqkv", [128, 8 * 3 * D], BF16)
    wgate = p.sbuf("wgate", [128, 8 * D], BF16)
    brow = p.sbuf("brow", [1, 4 * D], F32)
    vr = p.sbuf("vr", [1, TT], F32)
    xt = [[p.sbuf(f"xt{b}_{kc}", [128, 512], BF16) for kc in range(8)]
          for b in range(2)]
    ev = [p.sbuf(f"ev{i}", [128, 512], BF16) for i in range(4)]
    ps = [p.psum(f"ps{i}", [128, 512], F32) for i in range(4)]

    ncast = 0
    for kc in range(8):
        wt = ("cast_w", ncast) if ncast else None
        lw = p.dma("sync", wstage[:, 0:3 * D], wqkv_d[kc * 128:(kc + 1) * 128, :],
                   "ld_w", wait=wt)
        p.wait("vector", "ld_w", lw)
        p.run("vector", lambda v, kc=kc: v.tensor_copy(
            wqkv[:, kc * 3 * D:(kc + 1) * 3 * D], wstage[:, 0:3 * D]), inc="cast_w")
        ncast += 1
    for kc in range(8):
        lw = p.dma("sync", wstage[:, 0:D], wgate_d[kc * 128:(kc + 1) * 128, :],
                   "ld_w", wait=("cast_w", ncast))
        p.wait("vector", "ld_w", lw)
        p.run("vector", lambda v, kc=kc: v.tensor_copy(
            wgate[:, kc * D:(kc + 1) * D], wstage[:, 0:D]), inc="cast_w")
        ncast += 1
    wtot = p.val("cast_w")

    for i, dr in enumerate([bq_r, bk_r, bv_r, bg_r]):
        p.dma("sync", brow[0:1, i * D:(i + 1) * D], dr, "ld_b")
    p.dma("sync", vr[:], vrow, "ld_b")
    btot = p.val("ld_b")

    ev_tgt = [0, 0, 0, 0]
    mm_hist = []
    pslot = [0]

    def do_mm(mms_fn, dst_ap, func, sl=None):
        sl = pslot[0] % 4 if sl is None else sl
        pslot[0] += 1
        nev = p.val("ev_" + str(sl))
        if nev:
            p.wait("tensor", "ev_" + str(sl), nev)
        mt = p.run("tensor", lambda t: mms_fn(t, sl), inc="mm_done")
        p.wait("scalar", "mm_done", mt)
        if ev_tgt[sl]:
            p.wait("scalar", "st_out", ev_tgt[sl])
        et = p.run("scalar", lambda s, sl=sl: s.activation(ev[sl][:], ps[sl][:], func),
                   inc="ev_" + str(sl))
        ev_tgt[sl] = p.dma("gpsimd", dst_ap, ev[sl][:], "st_out",
                           wait=("ev_" + str(sl), et))

    for g in range(NG):
        buf = g % 2
        own = g >= HALO // 512
        tok0 = g * 512
        ot0 = tok0 - HALO
        for kc in range(8):
            wt = None
            if g >= 2 and kc == 0:
                wt = ("mm_done", mm_hist[g - 2])
            p.dma("scalar", xt[buf][kc][:],
                  xnb_d[tok0:tok0 + 512, kc * 128:(kc + 1) * 128], "ld_xt",
                  wait=wt, transpose=True)
        xtot = p.val("ld_xt")
        p.wait("tensor", "ld_xt", xtot)
        if g == 0:
            p.wait("tensor", "ld_b", btot)
            p.wait("tensor", "cast_w", wtot)

        vrs = vr[0:1, tok0:tok0 + 512]

        def fm_set(wbase, wsb, brow_off, n_out, dst_of, func):
            for oc in range(n_out):
                def mms(t, sl, oc=oc, wbase=wbase, wsb=wsb, brow_off=brow_off,
                        buf=buf, vrs=vrs):
                    for kc in range(8):
                        t.matmul(ps[sl][:],
                                 wsb[:, kc * wbase[1] + wbase[0] + oc * 128:
                                     kc * wbase[1] + wbase[0] + (oc + 1) * 128],
                                 xt[buf][kc][:], start=(kc == 0), stop=False)
                    return t.matmul(ps[sl][:],
                                    brow[0:1, brow_off + oc * 128:
                                         brow_off + (oc + 1) * 128],
                                    vrs, start=False, stop=True)
                do_mm(mms, dst_of(oc), func)

        if own:
            fm_set((0, 3 * D), wqkv, 0 * D, 8,
                   lambda oc: q_d[oc * 128:(oc + 1) * 128, ot0:ot0 + 512], AF.Copy)
        fm_set((D, 3 * D), wqkv, 1 * D, 8,
               lambda oc: k_d[oc * 128:(oc + 1) * 128, tok0:tok0 + 512], AF.Copy)
        if own:
            fm_set((0, D), wgate, 3 * D, 8,
                   lambda oc: g_d[oc * 128:(oc + 1) * 128, ot0:ot0 + 512], AF.Sigmoid)
        for tl in range(4):
            trow = tok0 + tl * 128
            for nch in range(2):
                def mmv(t, sl, tl=tl, nch=nch, buf=buf, trow=trow):
                    for kc in range(8):
                        t.matmul(ps[sl][:], xt[buf][kc][:, tl * 128:(tl + 1) * 128],
                                 wqkv[:, kc * 3 * D + 2 * D + nch * 512:
                                      kc * 3 * D + 2 * D + (nch + 1) * 512],
                                 start=(kc == 0), stop=False)
                    return t.matmul(ps[sl][:], vr[0:1, trow:trow + 128],
                                    brow[0:1, 2 * D + nch * 512:2 * D + (nch + 1) * 512],
                                    start=False, stop=True)
                do_mm(mmv, v_d[trow:trow + 128, nch * 512:(nch + 1) * 512], AF.Copy)
        mm_hist.append(p.val("mm_done"))

    p.wait("sync", "st_out", p.val("st_out"))
    return p.finalize()



RING_V = 26

def build_a3(stage='all', nobias=False):
    nc = bass.Bass("TRN2", target_bir_lowering=False)
    q_dr = nc.dram_tensor("q_fm", [D, T], BF16, kind="ExternalInput").ap()
    k_dr = nc.dram_tensor("k_fm", [D, TT], BF16, kind="ExternalInput").ap()
    v_dr = nc.dram_tensor("v_tm", [TT, D], BF16, kind="ExternalInput").ap()
    wmask_dr = nc.dram_tensor("wmask_full", [128, 2 * 16 * 128], BF16,
                              kind="ExternalInput").ap()
    lnw8_dr = nc.dram_tensor("lnw8", [H, NLO], F32, kind="ExternalInput").ap()
    onehot_dr = nc.dram_tensor("onehot", [128, 16 * 16], BF16, kind="ExternalInput").ap()
    indc_dr = nc.dram_tensor("indc", [128, 8 * 16], BF16, kind="ExternalInput").ap()
    indzl_dr = nc.dram_tensor("indzl", [128, 16], F32, kind="ExternalInput").ap()
    identf_dr = nc.dram_tensor("identf", [128, 128], F32, kind="ExternalInput").ap()

    glo_dr = nc.dram_tensor("glo_fm", [D, T], BF16, kind="ExternalOutput").ap()
    olg_dr = nc.dram_tensor("olg_tm", [T, D], BF16, kind="ExternalOutput").ap()
    z_dr = nc.dram_tensor("z_all", [H, T], F32, kind="ExternalOutput").ap()

    p = Prog(nc)
    HT = HALO // 128
    a_sb2 = p.sbuf("a_sb2", [128, NW * T], F32)
    lnwm4 = p.sbuf("lnwm4", [128, NLO], F32)
    ebias = p.sbuf("ebias", [128, 1], F32)
    oneh = p.sbuf("oneh", [128, 16 * 16], BF16)
    indc = p.sbuf("indc_s", [128, 8 * 16], BF16)
    indzl = p.sbuf("indzl_s", [128, 16], F32)
    identf = p.sbuf("identf_s", [128, 128], F32)
    wmask = p.sbuf("wmask", [128, 2 * 16 * 128], BF16)
    zl_sb = p.sbuf("zl_sb", [16, T], F32)
    zrow_sb = p.sbuf("zrow_sb", [16, 128], F32)
    kes = contextlib.ExitStack()
    k_sb = [p.sbuf(f"k{c}", [128, TT], BF16, es=kes) for c in range(8)]
    qd = [[p.sbuf(f"qd{b}_{c}", [128, 512], BF16, es=kes) for c in range(8)]
          for b in range(2)]
    prod = [p.sbuf(f"prod{b}", [128, 512], BF16, es=kes) for b in range(2)]

    ps_sc = [p.psum(f"psc{i}", [128, 512], F32) for i in range(2)]
    ps_s = [p.psum(f"pss{i}", [128, 512], F32) for i in range(2)]
    ps_o = [p.psum(f"pso{i}", [128, 512], F32) for i in range(2)]
    ps_z = p.psum("psz", [16, 512], F32)
    ps_tp = p.psum("pstp", [128, 512], F32)

    import os as _os
    _skips = set(_os.environ.get("A3SKIP", "").split(","))
    if "k" not in _skips:
        for c in range(8):
            p.dma("sync", k_sb[c][:], k_dr[c * 128:(c + 1) * 128, :], "ld_k")
    if "c" not in _skips:
        p.dma("sync", oneh[:], onehot_dr, "ld_c")
        p.dma("sync", indc[:], indc_dr, "ld_c")
        p.dma("sync", indzl[:], indzl_dr, "ld_c")
        p.dma("sync", identf[:], identf_dr, "ld_c")
        p.dma("sync", wmask[:], wmask_dr, "ld_c")
    mz = p.run("vector", lambda v: v.memset(lnwm4[:], 0.0), inc="v_c")
    if "lnw" not in _skips:
        for qd_ in range(4):
            p.dma("gpsimd", lnwm4[32 * qd_:32 * qd_ + 16, 0:NLO], lnw8_dr, "ld_cg",
                  wait=(("v_c", mz) if qd_ == 0 else None))
    ctot = p.val("ld_c")
    p.wait("vector", "ld_c", ctot)
    p.wait("vector", "ld_cg", p.val("ld_cg"))
    cprep = p.run("vector", lambda v: (
        v.memset(ebias[:], ESHIFT),
        v.memset(a_sb2[:], 0.0),
        v.tensor_scalar(lnwm4[:], lnwm4[:], ESHIFT, None, ALU.add))[-1], inc="v_c")

    if stage == 'loads':
        for snm in ["ld_k", "ld_c", "ld_cg", "v_c"]:
            if p.val(snm):
                p.wait("sync", snm, p.val(snm))
        kes.close()
        return p.finalize()
    ktot = p.val("ld_k")
    p.wait("tensor", "ld_k", ktot)
    p.wait("vector", "ld_k", ktot)
    p.wait("tensor", "v_c", cprep)
    p.wait("scalar", "v_c", cprep)
    sc_hist = {}
    for w in range(NW):
        now = min(4, NLO - 4 * w)
        for n in range(4):
            bk = (w * 4 + n) % 2
            for c in range(8):
                wt = None
                if (w * 4 + n) >= 2 and c == 0:
                    wt = ("sc_mm", sc_hist[(w * 4 + n) - 2])
                p.dma("scalar", qd[bk][c][:],
                      q_dr[c * 128:(c + 1) * 128, n * 512:(n + 1) * 512],
                      f"ld_qd{bk}", wait=wt)
            qt = p.val(f"ld_qd{bk}")
            p.wait("vector", f"ld_qd{bk}", qt)
            p.wait("tensor", "sc_ev", p.val("sc_ev"))
            for oi in range(now):
                off = LARGE_OFFS[w * 4 + oi]
                for c in range(8):
                    pb = (oi * 8 + c) % 2
                    if p.val("sc_mm") >= 2:
                        p.wait("vector", "sc_mm", p.val("sc_mm") - 1)
                    def pr(v, c=c, off=off, n=n, bk=bk, pb=pb):
                        return v.tensor_tensor(
                            prod[pb][:], qd[bk][c][:],
                            k_sb[c][:, HALO - off + n * 512: HALO - off + (n + 1) * 512],
                            ALU.mult)
                    pt = p.run("vector", pr, inc="pr_v")
                    p.wait("tensor", "pr_v", pt)

                    def mm(t, c=c, oi=oi, pb=pb):
                        return t.matmul(ps_sc[0][32 * oi:32 * oi + 16, :],
                                        indc[:, c * 16:(c + 1) * 16], prod[pb][:],
                                        start=(c == 0), stop=(c == 7),
                                        tile_position=(0, 32 * oi))
                    p.run("tensor", mm, inc="sc_mm")
            sc_hist_mark = True
            sc_hist[w * 4 + n] = p.val("sc_mm")
            if stage == 'pre0':
                p.wait("sync", "sc_mm", p.val("sc_mm"))
                kes.close()
                return p.finalize()
            p.wait("scalar", "sc_mm", p.val("sc_mm"))
            for oi in range(now):
                def ex(s, w=w, n=n, oi=oi):
                    kw = {}
                    if not nobias:
                        kw["bias"] = lnwm4[32 * oi:32 * oi + 16,
                                           w * 4 + oi:w * 4 + oi + 1]
                    return s.activation(
                        a_sb2[32 * oi:32 * oi + 16,
                              w * T + n * 512:w * T + (n + 1) * 512],
                        ps_sc[0][32 * oi:32 * oi + 16, :], AF.Exp,
                        scale=ASCALE, **kw)
                p.run("scalar", ex, inc="sc_ev")

    if stage == 'pre':
        p.wait("sync", "sc_ev", p.val("sc_ev"))
        kes.close()
        return p.finalize()
    p.wait("tensor", "sc_ev", p.val("sc_ev"))
    for n in range(4):
        if p.val("zl_ev"):
            p.wait("tensor", "zl_ev", p.val("zl_ev"))
        def zmm(t, n=n):
            last = None
            for w in range(NW):
                last = t.matmul(ps_sc[1][0:16, :], indzl[:, 0:16],
                                a_sb2[:, w * T + n * 512:w * T + (n + 1) * 512],
                                start=(w == 0), stop=(w == NW - 1))
            return last
        zt = p.run("tensor", zmm, inc="zl_mm")
        p.wait("vector", "zl_mm", zt)
        p.run("vector", lambda v, n=n: v.tensor_copy(
            zl_sb[:, n * 512:(n + 1) * 512], ps_sc[1][0:16, :]), inc="zl_ev")

    if stage == 'zl':
        p.wait("sync", "zl_ev", p.val("zl_ev"))
        kes.close()
        return p.finalize()
    kes.close()
    q_sb = [p.sbuf(f"q{c}", [128, T], BF16) for c in range(8)]
    v_ring = p.sbuf("v_ring", [128, RING_V * D], BF16)
    kr = [p.sbuf(f"kr{c}", [128, 3 * 128], BF16) for c in range(8)]
    A_sb = p.sbuf("A_sb", [128, 2 * 2048], BF16)
    a_cmp = [p.sbuf(f"acmp{o}", [128, 16], BF16) for o in range(NLO)]
    stripes = p.sbuf("stripes", [128, NLO * D], BF16)
    og_tm = p.sbuf("og_tm", [128, D], BF16)
    v64_ring = p.sbuf("v64_ring", [128, 3 * D], BF16)
    glo_sb = [p.sbuf(f"glo{c}", [128, T], BF16) for c in range(8)]
    sc_fin = p.val("sc_mm")
    p.wait("sync", "sc_mm", sc_fin)
    p.wait("gpsimd", "sc_mm", sc_fin)
    for c in range(8):
        p.dma("sync", q_sb[c][:], q_dr[c * 128:(c + 1) * 128, :], "ld_q")
    for s in range(HT):
        p.dma("gpsimd", v_ring[:, (s % RING_V) * D:(s % RING_V + 1) * D],
              v_dr[s * 128:(s + 1) * 128, :], "ld_v_pre")
    for c in range(8):
        p.dma("sync", kr[c][:, (23 % 3) * 128:(23 % 3 + 1) * 128],
              k_dr[c * 128:(c + 1) * 128, 23 * 128:24 * 128], "ld_kr_pre")
    qtot = p.val("ld_q")
    p.wait("tensor", "ld_q", qtot)
    p.wait("tensor", "ld_v_pre", p.val("ld_v_pre"))
    p.wait("tensor", "ld_kr_pre", p.val("ld_kr_pre"))
    p.wait("tensor", "zl_ev", p.val("zl_ev"))

    tile_done = []
    for i in range(NT):
        it = HT + i
        wt = ("tile", tile_done[i - 2]) if i >= 2 else None
        p.dma("gpsimd", v_ring[:, (it % RING_V) * D:(it % RING_V + 1) * D],
              v_dr[it * 128:(it + 1) * 128, :], f"ld_v{i % 3}", wait=wt)
        p.dma("gpsimd", v64_ring[:, (i % 3) * D:(i % 3 + 1) * D],
              v_dr[it * 128 - 192:it * 128 - 64, :], f"ld_v{i % 3}")
        for c in range(8):
            p.dma("sync", kr[c][:, (it % 3) * 128:(it % 3 + 1) * 128],
                  k_dr[c * 128:(c + 1) * 128, it * 128:(it + 1) * 128],
                  f"ld_kr{i % 3}", wait=wt if c == 0 else None)
        p.wait("tensor", f"ld_v{i % 3}", p.val(f"ld_v{i % 3}"))
        p.wait("tensor", f"ld_kr{i % 3}", p.val(f"ld_kr{i % 3}"))

        def apos(h, cti):
            hg, j = divmod(h, 4)
            return (j % 2) * 2048 + hg * 512 + cti * 256 + (j // 2) * 128

        first_z = True
        for hg in range(4 if "locmm" not in _skips else 0):
            for par in range(2):
                p.wait("tensor", "s_ev", p.val("s_ev"))

                def smm(t, hg=hg, par=par, it=it, i=i):
                    last = None
                    pb = 64 * par
                    for cti, ct in enumerate([it - 1, it]):
                        for jj in range(2):
                            h = hg * 4 + 2 * jj + par
                            c = h // 2
                            last = t.matmul(
                                ps_s[par][:, cti * 256 + jj * 128:
                                          cti * 256 + (jj + 1) * 128],
                                kr[c][pb:pb + 64, (ct % 3) * 128:(ct % 3 + 1) * 128],
                                q_sb[c][pb:pb + 64, i * 128:(i + 1) * 128],
                                start=True, stop=True)
                    return last
                st = p.run("tensor", smm, inc="s_mm")
                if "lexp" in _skips:
                    continue
                p.wait("scalar", "s_mm", st)

                def ex(s, hg=hg, par=par):
                    return s.activation(
                        A_sb[:, par * 2048 + hg * 512: par * 2048 + (hg + 1) * 512],
                        ps_s[par][:], AF.Exp, scale=ASCALE, bias=ebias[:, 0:1])
                et = p.run("scalar", ex, inc="s_ev")
                p.wait("vector", "s_ev", et)

                def msk(v, hg=hg, par=par):
                    abase = par * 2048 + hg * 512
                    return v.tensor_tensor(A_sb[:, abase:abase + 512],
                                           A_sb[:, abase:abase + 512],
                                           wmask[:, abase:abase + 512], ALU.mult)
                mt = p.run("vector", msk, inc="s_msk")
                p.wait("tensor", "s_msk", mt)

            if "lvz" in _skips or "lexp" in _skips:
                continue

            def vmm(t, hg=hg, it=it):
                last = None
                for j in range(4):
                    h = hg * 4 + j
                    bank = h // 8
                    pb = 64 * (h % 2)
                    fs = ((h // 2) % 4) * 128
                    for cti, ct in enumerate([it - 1, it]):
                        kwargs = dict(start=(cti == 0), stop=(cti == 1))
                        if pb:
                            kwargs["tile_position"] = (0, pb)
                        last = t.matmul(
                            ps_o[bank][pb:pb + 64, fs:fs + 128],
                            v_ring[:, (ct % RING_V) * D + h * 64:
                                   (ct % RING_V) * D + (h + 1) * 64],
                            A_sb[:, apos(h, cti):apos(h, cti) + 128],
                            **kwargs)
                return last
            p.run("tensor", vmm, inc="v_mm")

            def zmm2(t, hg=hg, it=it, first_z=first_z):
                last = None
                for j in range(4):
                    h = hg * 4 + j
                    for cti in range(2):
                        last = t.matmul(
                            ps_z[0:16, 0:128], oneh[:, h * 16:(h + 1) * 16],
                            A_sb[:, apos(h, cti):apos(h, cti) + 128],
                            start=(first_z and j == 0 and cti == 0),
                            stop=(hg == 3 and j == 3 and cti == 1))
                return last
            p.run("tensor", zmm2, inc="z_mm")
            first_z = False

        skip_large = "large" in _skips
        def tpm(t, i=i):
            last = None
            for w in range(NW):
                last = t.transpose(ps_tp[:, w * 128:(w + 1) * 128],
                                   a_sb2[:, w * T + i * 128:w * T + (i + 1) * 128],
                                   identf[:])
            return last
        if skip_large:
            p.run("vector", lambda v: v.memset(og_tm[:], 0.0), inc="olg_v")
        p.wait("tensor", "a_ev", p.val("a_ev")) if not skip_large else None
        tpt = p.run("tensor", tpm, inc="tp_mm") if not skip_large else None
        p.wait("vector", "tp_mm", tpt) if not skip_large else None

        aevt = [p.inc("a_ev") for _ in range(NLO)] if not skip_large else []
        asem = p.sems["a_ev"] if not skip_large else None

        def aev(v, asem=asem):
            for o in range(NLO):
                w, oi = divmod(o, 4)
                v.tensor_copy(a_cmp[o][:],
                              ps_tp[:, w * 128 + 32 * oi:w * 128 + 32 * oi + 16]) \
                    .then_inc(asem, 1)
        if not skip_large:
            p.emit("vector", aev)
            p.wait("vector", "a_ev", aevt[-1])

        def vlg(v, it=it):
            last = None
            for o, off in enumerate(LARGE_OFFS):
                if off % 128 == 64:
                    s64 = (it - HALO // 128) % 3
                    av = a_cmp[o][:].unsqueeze(2).broadcast_to([128, 16, 64])
                    last = v.tensor_tensor(stripes[:, o * D:(o + 1) * D]
                                           .rearrange("p (a b) -> p a b", b=64),
                                           v64_ring[:, s64 * D:(s64 + 1) * D]
                                           .rearrange("p (a b) -> p a b", b=64),
                                           av, ALU.mult)
                    continue
                dl = off // 128
                slot = (it - dl) % RING_V
                av = a_cmp[o][:].unsqueeze(2).broadcast_to([128, 16, 64])
                last = v.tensor_tensor(stripes[:, o * D:(o + 1) * D]
                                       .rearrange("p (a b) -> p a b", b=64),
                                       v_ring[:, slot * D:(slot + 1) * D]
                                       .rearrange("p (a b) -> p a b", b=64),
                                       av, ALU.mult)
            return last
        if skip_large:
            pass
        else:
            p.sync_run("vector", vlg)
        if not skip_large:
            p.sync_run("vector", lambda v: v.tensor_tensor(
                stripes[:, 0:4 * D], stripes[:, 0:4 * D], stripes[:, 4 * D:8 * D],
                ALU.add))
            p.sync_run("vector", lambda v: v.tensor_tensor(
                stripes[:, 0:2 * D], stripes[:, 0:2 * D], stripes[:, 2 * D:4 * D],
                ALU.add))
            p.sync_run("vector", lambda v: v.tensor_tensor(
                stripes[:, 0:D], stripes[:, 0:D], stripes[:, D:2 * D], ALU.add))
            lt = p.run("vector", lambda v: v.tensor_tensor(
                og_tm[:], stripes[:, 0:D], stripes[:, 8 * D:9 * D], ALU.add),
                inc="olg_v")
        else:
            lt = p.val("olg_v")
        p.dma("gpsimd", olg_dr[i * 128:(i + 1) * 128, :], og_tm[:], "st_olg",
              wait=("olg_v", lt))
        p.wait("vector", "st_olg", p.val("st_olg"))

        if p.val("v_mm"):
            p.wait("vector", "v_mm", p.val("v_mm"))
        if p.val("z_mm"):
            p.wait("vector", "z_mm", p.val("z_mm"))

        def oev(v, i=i):
            if p.val("v_mm"):
                for bank in range(2):
                    for j in range(4):
                        c = bank * 4 + j
                        v.tensor_copy(glo_sb[c][:, i * 128:(i + 1) * 128],
                                      ps_o[bank][:, j * 128:(j + 1) * 128])
            if p.val("z_mm"):
                return v.tensor_copy(zrow_sb[:], ps_z[0:16, 0:128])
            return v.tensor_copy(zrow_sb[:], zrow_sb[:])
        ot = p.run("vector", oev, inc="o_ev")
        p.wait("vector", "o_ev", ot)

        def zadd(v, i=i):
            return v.tensor_tensor(zl_sb[:, i * 128:(i + 1) * 128],
                                   zl_sb[:, i * 128:(i + 1) * 128],
                                   zrow_sb[:], ALU.add)
        p.run("vector", zadd, inc="tile")
        tile_done.append(p.val("tile"))
        p.wait("tensor", "tile", tile_done[-1])
        if stage == 'tile1' and i == 0:
            break
        if stage == 'tile4' and i == 3:
            break

    fin = p.val("tile")
    for c in range(8):
        p.dma("gpsimd", glo_dr[c * 128:(c + 1) * 128, :], glo_sb[c][:], "st_out",
              wait=(("tile", fin) if c == 0 else None))
    p.dma("gpsimd", z_dr, zl_sb[:], "st_out")
    p.wait("sync", "st_out", p.val("st_out"))
    p.wait("sync", "st_olg", p.val("st_olg"))
    return p.finalize()



NI = DI // 512 + (1 if DI % 512 else 0)

def build_a4():
    nc = bass.Bass("TRN2", target_bir_lowering=False)
    xq_dr = nc.dram_tensor("xq_tm", [T, D], BF16, kind="ExternalInput").ap()
    wqg_dr = nc.dram_tensor("wqgT", [D, DI], BF16, kind="ExternalInput").ap()
    wqu_dr = nc.dram_tensor("wquT", [D, DI], BF16, kind="ExternalInput").ap()
    wqd_dr = nc.dram_tensor("wqdT", [DI, D], BF16, kind="ExternalInput").ap()
    g0_dr = nc.dram_tensor("g0_fm", [D, T], BF16, kind="ExternalOutput").ap()
    hq_dr = nc.dram_tensor("hq_tm", [T, DI], BF16, kind="Internal").ap()

    p = Prog(nc)
    xt = [p.sbuf(f"xqf{c}", [128, T], BF16) for c in range(8)]
    scr = make_fp4_scratch(p)
    h_sb = [p.sbuf(f"h{b}", [128, DI], BF16) for b in range(2)]
    sgt = p.sbuf("sgt", [128, 512], BF16)
    hq_sb = [p.sbuf(f"hq{b}", [128, DI], BF16) for b in range(2)]
    ps = [p.psum(f"ps{i}", [128, 512], F32) for i in range(4)]

    kes = contextlib.ExitStack()
    wg = [p.sbuf(f"wg{c}", [128, DI], BF16, es=kes) for c in range(8)]
    wu = [p.sbuf(f"wu{c}", [128, DI], BF16, es=kes) for c in range(8)]

    for c in range(8):
        p.dma("scalar", wg[c][:], wqg_dr[c * 128:(c + 1) * 128, :], "ld_w")
        p.dma("scalar", wu[c][:], wqu_dr[c * 128:(c + 1) * 128, :], "ld_w")
    for c in range(8):
        p.dma("sync", xt[c][:], xq_dr[:, c * 128:(c + 1) * 128], "ld_x",
              transpose=True)
    p.wait("tensor", "ld_w", p.val("ld_w"))
    p.wait("tensor", "ld_x", p.val("ld_x"))


    for ti in range(NT):
        buf = ti % 2
        if ti >= 2:
            p.wait("scalar", f"st_hq{buf}", p.val(f"st_hq{buf}"))
            p.wait("vector", f"st_hq{buf}", p.val(f"st_hq{buf}"))
        for ic in range(NI):
            iw = min(512, DI - ic * 512)
            sl = (ti * NI + ic) % 2
            nev = p.val(f"ev4_{sl}")
            if nev:
                p.wait("tensor", f"ev4_{sl}", nev)

            def mmg(t, ti=ti, ic=ic, sl=sl, iw=iw):
                for c in range(8):
                    t.matmul(ps[sl][:, 0:iw], wg[c][:, ic * 512:ic * 512 + iw],
                             xt[c][:, ti * 128:(ti + 1) * 128].rearrange(
                                 "p m -> p m") if False else
                             xt[c][:, ti * 128:(ti + 1) * 128],
                             start=(c == 0), stop=(c == 7)) if False else None
                last = None
                for c in range(8):
                    last = t.matmul(ps[sl][:, 0:iw],
                                    xt[c][:, ti * 128:(ti + 1) * 128],
                                    wg[c][:, ic * 512:ic * 512 + iw],
                                    start=(c == 0), stop=(c == 7))
                for c in range(8):
                    last = t.matmul(ps[2 + sl][:, 0:iw],
                                    xt[c][:, ti * 128:(ti + 1) * 128],
                                    wu[c][:, ic * 512:ic * 512 + iw],
                                    start=(c == 0), stop=(c == 7))
                return last
            mt = p.run("tensor", mmg, inc="mm4")
            p.wait("scalar", "mm4", mt)
            for s2 in range(2):
                if p.val(f"ev4_{s2}"):
                    p.wait("scalar", f"ev4_{s2}", p.val(f"ev4_{s2}"))
            et = p.run("scalar", lambda s, sl=sl, iw=iw: s.activation(
                sgt[:, 0:iw], ps[sl][:, 0:iw], AF.Silu), inc="sl4")
            p.wait("vector", "sl4", et)
            ht = p.run("vector", lambda v, sl=sl, ic=ic, iw=iw, buf=buf:
                       v.tensor_tensor(h_sb[buf][:, ic * 512:ic * 512 + iw],
                                       sgt[:, 0:iw], ps[2 + sl][:, 0:iw],
                                       ALU.mult), inc=f"ev4_{sl}")
        for s2 in range(2):
            if p.val(f"ev4_{s2}"):
                p.wait("vector", f"ev4_{s2}", p.val(f"ev4_{s2}"))
        qd = emit_fp4_quant(p, h_sb[buf][:], hq_sb[buf][:], scr, 128, DI,
                            f"q4_{buf}")
        p.dma("gpsimd", hq_dr[ti * 128:(ti + 1) * 128, :], hq_sb[buf][:],
              f"st_hq{buf}", wait=(f"q4_{buf}", qd))

    for b2 in range(2):
        p.wait("sync", f"st_hq{b2}", p.val(f"st_hq{b2}"))
        p.wait("scalar", f"st_hq{b2}", p.val(f"st_hq{b2}"))
    p.wait("sync", "mm4", p.val("mm4"))
    p.wait("scalar", "mm4", p.val("mm4"))
    kes.close()
    NIC = (DI + 127) // 128
    wd = [p.sbuf(f"wd{icc}", [128, D], BF16) for icc in range(NIC)]
    ht = [[p.sbuf(f"ht{b}_{icc}", [128, 512], BF16) for icc in range(NIC)]
          for b in range(2)]
    for icc in range(NIC):
        p.dma("scalar", wd[icc][:],
              wqd_dr[icc * 128:(icc + 1) * 128, :], "ld_wd")
    p.wait("tensor", "ld_wd", p.val("ld_wd"))
    g0_sb = [p.sbuf(f"g0_sb{i}", [128, 512], BF16) for i in range(2)]

    mm_hist4 = []
    for g in range(4):
        buf = g % 2
        for icc in range(NIC):
            wt = None
            if g >= 2 and icc == 0:
                wt = ("mm4", mm_hist4[g - 2])
            elif icc >= 8:
                wt = (f"ld_h{buf}", 16 * (p.val(f"ld_h{buf}") // 16 - 7))
            p.dma("scalar", ht[buf][icc][:],
                  hq_dr[g * 512:(g + 1) * 512, icc * 128:(icc + 1) * 128],
                  f"ld_h{buf}", wait=wt, transpose=True)
        p.wait("tensor", f"ld_h{buf}", p.val(f"ld_h{buf}"))
        for oc in range(8):
            sl = oc % 2
            nev = p.val(f"ev4b_{sl}")
            if nev:
                p.wait("tensor", f"ev4b_{sl}", nev)

            def mmd(t, g=g, oc=oc, sl=sl, buf=buf):
                last = None
                for icc in range(NIC):
                    last = t.matmul(ps[sl][:],
                                    wd[icc][:, oc * 128:(oc + 1) * 128],
                                    ht[buf][icc][:],
                                    start=(icc == 0), stop=(icc == NIC - 1))
                return last
            mt = p.run("tensor", mmd, inc="mm4")
            p.wait("scalar", "mm4", mt)
            if p.val(f"st_g0_{sl}"):
                p.wait("scalar", f"st_g0_{sl}", p.val(f"st_g0_{sl}"))
            et = p.run("scalar", lambda s, sl=sl: s.activation(
                g0_sb[sl][:], ps[sl][:], AF.Silu), inc=f"ev4b_{sl}")
            p.dma("gpsimd", g0_dr[oc * 128:(oc + 1) * 128, g * 512:(g + 1) * 512],
                  g0_sb[sl][:], f"st_g0_{sl}", wait=(f"ev4b_{sl}", p.val(f"ev4b_{sl}")))
        mm_hist4.append(p.val("mm4"))

    for sl in range(2):
        p.wait("sync", f"st_g0_{sl}", p.val(f"st_g0_{sl}"))
    return p.finalize()


def build_a5():
    nc = bass.Bass("TRN2", target_bir_lowering=False)
    glo_dr = nc.dram_tensor("glo_fm", [D, T], BF16, kind="ExternalInput").ap()
    olg_dr = nc.dram_tensor("olg_tm", [T, D], BF16, kind="ExternalInput").ap()
    z_dr = nc.dram_tensor("z_all", [H, T], F32, kind="ExternalInput").ap()
    gate_dr = nc.dram_tensor("gate_fm", [D, T], BF16, kind="ExternalInput").ap()
    g0_dr = nc.dram_tensor("g0_fm", [D, T], BF16, kind="ExternalInput").ap()
    wout_dr = nc.dram_tensor("w_out", [D, D], F32, kind="ExternalInput").ap()
    bout_dr = nc.dram_tensor("b_out_r", [1, D], F32, kind="ExternalInput").ap()
    ones_dr = nc.dram_tensor("ones_r", [1, 512], F32, kind="ExternalInput").ap()
    indh_dr = nc.dram_tensor("indh", [16, 8 * 128], F32, kind="ExternalInput").ap()
    x_dr = nc.dram_tensor("x_own", [T, D], F32, kind="ExternalInput").ap()
    x1_dr = nc.dram_tensor("x1", [T, D], F32, kind="ExternalOutput").ap()
    sg_dr = nc.dram_tensor("sg_fm", [D, T], BF16, kind="Internal").ap()

    p = Prog(nc)
    glo = [p.sbuf(f"glo{c}", [128, T], BF16) for c in range(8)]
    olgT = [p.sbuf(f"olgT{c}", [128, T], BF16) for c in range(8)]
    gate = [p.sbuf(f"gate{c}", [128, T], BF16) for c in range(8)]
    g0 = [p.sbuf(f"g0_{c}", [128, T], BF16) for c in range(8)]
    zi = p.sbuf("zi", [16, T], F32)
    indh = p.sbuf("indh_s", [16, 8 * 128], F32)
    wstage = p.sbuf("wstage", [128, D], F32)
    wout = p.sbuf("wout", [128, 8 * D], BF16)
    bout = p.sbuf("bout", [1, D], F32)
    onesr = p.sbuf("onesr", [1, 512], F32)
    gn = p.sbuf("gn", [128, 512], BF16)
    sev = [p.sbuf(f"sev{i}", [128, 512], BF16) for i in range(2)]
    xs = [p.sbuf(f"xs{b}", [128, D], F32) for b in range(2)]
    sgt = [p.sbuf(f"sgt{b}", [128, D], BF16) for b in range(2)]
    x1b = [p.sbuf(f"x1b{b}", [128, D], F32) for b in range(2)]
    ps_b = p.psum("psb", [128, 512], F32)
    ps_w = [p.psum(f"psw{i}", [128, 512], F32) for i in range(2)]

    for c in range(8):
        p.dma("sync", glo[c][:], glo_dr[c * 128:(c + 1) * 128, :], "ld_a")
        p.dma("sync", gate[c][:], gate_dr[c * 128:(c + 1) * 128, :], "ld_a")
        p.dma("sync", g0[c][:], g0_dr[c * 128:(c + 1) * 128, :], "ld_a")
        p.dma("scalar", olgT[c][:], olg_dr[:, c * 128:(c + 1) * 128], "ld_aT",
              transpose=True)
    p.dma("sync", zi[:], z_dr, "ld_a")
    p.dma("sync", indh[:], indh_dr, "ld_a")
    p.dma("sync", bout[:], bout_dr, "ld_a")
    p.dma("sync", onesr[:], ones_dr, "ld_a")
    ncast = 0
    for kc in range(8):
        lw = p.dma("sync", wstage[:], wout_dr[kc * 128:(kc + 1) * 128, :], "ld_w",
                   wait=(("cast_w", ncast) if ncast else None))
        p.wait("vector", "ld_w", lw)
        p.run("vector", lambda v, kc=kc: v.tensor_copy(
            wout[:, kc * D:(kc + 1) * D], wstage[:]), inc="cast_w")
        ncast += 1
    p.wait("vector", "ld_a", p.val("ld_a"))
    p.wait("vector", "ld_aT", p.val("ld_aT"))
    p.sync_run("vector", lambda v: v.reciprocal(zi[:], zi[:]))
    vz = p.run("vector", lambda v: v.tensor_copy(zi[0:1, 0:1], zi[0:1, 0:1]),
               inc="v5")
    p.wait("tensor", "v5", vz)
    p.wait("tensor", "cast_w", p.val("cast_w"))

    for c in range(8):
        for n in range(4):
            def bz(t, c=c, n=n):
                return t.matmul(ps_b[:], indh[:, c * 128:(c + 1) * 128],
                                zi[:, n * 512:(n + 1) * 512], start=True, stop=True)
            bt = p.run("tensor", bz, inc="b_mm")
            p.wait("vector", "b_mm", bt)

            sl_ = slice(n * 512, (n + 1) * 512)
            p.sync_run("vector", lambda v, c=c, sl=sl_: v.tensor_tensor(
                gn[:], glo[c][:, sl], olgT[c][:, sl], ALU.add))
            p.sync_run("vector", lambda v: v.tensor_tensor(
                gn[:], gn[:], ps_b[:], ALU.mult))
            nt_ = p.run("vector", lambda v, c=c, sl=sl_: v.tensor_tensor(
                olgT[c][:, sl], gn[:], gate[c][:, sl], ALU.mult), inc="n_ev")
            p.wait("tensor", "n_ev", nt_)
            p.wait("vector", "n_ev", nt_)

    for n in range(4):
        for oc in range(8):
            sl = oc % 2
            if p.val(f"ev5_{sl}"):
                p.wait("tensor", f"ev5_{sl}", p.val(f"ev5_{sl}"))

            def mmo(t, n=n, oc=oc, sl=sl):
                for kc in range(8):
                    t.matmul(ps_w[sl][:], wout[:, kc * D + oc * 128:
                                               kc * D + (oc + 1) * 128],
                             olgT[kc][:, n * 512:(n + 1) * 512],
                             start=(kc == 0), stop=False)
                return t.matmul(ps_w[sl][:], bout[0:1, oc * 128:(oc + 1) * 128],
                                onesr[0:1, 0:512], start=False, stop=True)
            mt = p.run("tensor", mmo, inc="mm5")
            p.wait("vector", "mm5", mt)
            if p.val(f"st5_{sl}"):
                p.wait("vector", f"st5_{sl}", p.val(f"st5_{sl}"))

            p.sync_run("vector", lambda v, sl=sl: v.tensor_copy(sev[sl][:],
                                                                 ps_w[sl][:]))
            et = p.run("vector", lambda v, n=n, oc=oc, sl=sl: v.tensor_tensor(
                sev[sl][:], sev[sl][:], g0[oc][:, n * 512:(n + 1) * 512], ALU.mult),
                inc=f"ev5_{sl}")
            p.dma("gpsimd", sg_dr[oc * 128:(oc + 1) * 128, n * 512:(n + 1) * 512],
                  sev[sl][:], f"st5_{sl}", wait=(f"ev5_{sl}", p.val(f"ev5_{sl}")))

    for sl in range(2):
        p.wait("sync", f"st5_{sl}", p.val(f"st5_{sl}"))
        p.wait("scalar", f"st5_{sl}", p.val(f"st5_{sl}"))

    for ti in range(NT):
        buf = ti % 2
        wt = (f"st_x1_{buf}", p.val(f"st_x1_{buf}")) if ti >= 2 else None
        p.dma("sync", xs[buf][:], x_dr[ti * 128:(ti + 1) * 128, :], f"ld_x{buf}",
              wait=wt)
        p.dma("scalar", sgt[buf][:], sg_dr[:, ti * 128:(ti + 1) * 128],
              f"ld_sg{buf}", wait=wt, transpose=True)
        p.wait("vector", f"ld_x{buf}", p.val(f"ld_x{buf}"))
        p.wait("vector", f"ld_sg{buf}", p.val(f"ld_sg{buf}"))
        at = p.run("vector", lambda v, buf=buf: v.tensor_tensor(
            x1b[buf][:], xs[buf][:], sgt[buf][:], ALU.add), inc="x1v")
        p.dma("gpsimd", x1_dr[ti * 128:(ti + 1) * 128, :], x1b[buf][:],
              f"st_x1_{buf}", wait=("x1v", at))
    for b in range(2):
        p.wait("sync", f"st_x1_{b}", p.val(f"st_x1_{b}"))
    return p.finalize()


def build_b():
    nc = bass.Bass("TRN2", target_bir_lowering=False)
    x1_dr = nc.dram_tensor("x1", [T, D], F32, kind="ExternalInput").ap()
    win_dr = nc.dram_tensor("W_in", [D, 3 * S], F32, kind="ExternalInput").ap()
    bin_dr = nc.dram_tensor("b_in_r", [1, 3 * S], F32, kind="ExternalInput").ap()
    wx_dr = nc.dram_tensor("W_x", [S, S], F32, kind="ExternalInput").ap()
    bx_dr = nc.dram_tensor("b_x_r", [1, S], F32, kind="ExternalInput").ap()
    wdt_dr = nc.dram_tensor("W_dt", [S, S], F32, kind="ExternalInput").ap()
    bdt_dr = nc.dram_tensor("b_dt_r", [1, S], F32, kind="ExternalInput").ap()
    wnorm_dr = nc.dram_tensor("w_norm_r", [1, D], F32, kind="ExternalInput").ap()

    xn1_dr = nc.dram_tensor("xn1_tm", [T, D], F32, kind="ExternalOutput").ap()
    xn1b_dr = nc.dram_tensor("xn1b", [T, D], BF16, kind="ExternalOutput").ap()
    xr_dr = nc.dram_tensor("xr_b", [T, D], BF16, kind="Internal").ap()
    sigg_dr = nc.dram_tensor("sigg", [S, T], BF16, kind="ExternalOutput").ap()
    hs_dr = nc.dram_tensor("hs_loc", [S, T], F32, kind="ExternalOutput").ap()
    pp_dr = nc.dram_tensor("pp", [S, T], F32, kind="ExternalOutput").ap()
    ends_dr = nc.dram_tensor("ends", [S, 2], F32, kind="ExternalOutput").ap()
    amax_dr = nc.dram_tensor("amax", [1, 1], F32, kind="ExternalOutput").ap()

    p = Prog(nc)
    xs = [p.sbuf(f"xs{b}", [128, D], F32) for b in range(2)]
    xnb = [p.sbuf(f"xnb{b}", [128, D], BF16) for b in range(2)]
    xnf = [p.sbuf(f"xnf{b}", [128, D], F32) for b in range(2)]
    stat = p.sbuf("stat", [128, 8], F32)
    sq = [p.sbuf(f"sq{b}", [128, D], F32) for b in range(2)]
    xrb = [p.sbuf(f"xrb{b}", [128, D], BF16) for b in range(2)]
    epsc = p.sbuf("epsc", [128, 1], F32)
    eps5 = p.sbuf("eps5", [128, 1], F32)
    wnr = p.sbuf("wnr", [1, D], F32)
    wnb = p.sbuf("wnb", [128, D], F32)
    amx = p.sbuf("amx", [128, 2], F32)
    amr = p.sbuf("amr", [1, 128], F32)
    identf = p.sbuf("identf2", [128, 128], F32)
    win = p.sbuf("win", [128, 8 * 3 * S], BF16)
    wx = p.sbuf("wx", [128, 2 * S], BF16)
    wdt = p.sbuf("wdt", [128, 2 * S], BF16)
    brows = p.sbuf("brows", [1, 3 * S + 2 * S], F32)
    wstage = p.sbuf("wstage", [128, 3 * S], F32)
    xnt = [[p.sbuf(f"xnt{b}_{c}", [128, 512], BF16) for c in range(8)]
           for b in range(2)]
    uzg = [p.sbuf(f"uzg{j}", [128, T], BF16) for j in range(6)]
    ut = [p.sbuf(f"ut{j}", [128, T], F32) for j in range(2)]
    dt_ = [p.sbuf(f"dt{j}", [128, T], F32) for j in range(2)]
    bb = [p.sbuf(f"bb{j}", [128, T], F32) for j in range(2)]
    hs = [p.sbuf(f"hs{j}", [128, T], F32) for j in range(2)]
    pp = [p.sbuf(f"pp{j}", [128, T], F32) for j in range(2)]
    sgg = p.sbuf("sgg", [128, T], BF16)
    ends = p.sbuf("ends_s", [128, 4], F32)
    ps = [p.psum(f"ps{i}", [128, 512], F32) for i in range(3)]

    idf_dr = nc.dram_tensor("identf", [128, 128], F32, kind="ExternalInput").ap()

    p.dma("sync", wnr[:], wnorm_dr, "ld_c")
    p.dma("sync", idf_dr_s := identf[:], idf_dr, "ld_c")
    p.dma("sync", brows[0:1, 0:3 * S], bin_dr, "ld_c")
    p.dma("sync", brows[0:1, 3 * S:4 * S], bx_dr, "ld_c")
    p.dma("sync", brows[0:1, 4 * S:5 * S], bdt_dr, "ld_c")
    ncast = 0
    for kc in range(8):
        lw = p.dma("sync", wstage[:], win_dr[kc * 128:(kc + 1) * 128, :], "ld_w",
                   wait=(("cast_w", ncast) if ncast else None))
        p.wait("vector", "ld_w", lw)
        p.run("vector", lambda v, kc=kc: v.tensor_copy(
            win[:, kc * 3 * S:(kc + 1) * 3 * S], wstage[:]), inc="cast_w")
        ncast += 1
    for kc in range(2):
        lw = p.dma("sync", wstage[:, 0:S], wx_dr[kc * 128:(kc + 1) * 128, :],
                   "ld_w", wait=("cast_w", ncast))
        p.wait("vector", "ld_w", lw)
        p.run("vector", lambda v, kc=kc: v.tensor_copy(
            wx[:, kc * S:(kc + 1) * S], wstage[:, 0:S]), inc="cast_w")
        ncast += 1
        lw = p.dma("sync", wstage[:, 0:S], wdt_dr[kc * 128:(kc + 1) * 128, :],
                   "ld_w", wait=("cast_w", ncast))
        p.wait("vector", "ld_w", lw)
        p.run("vector", lambda v, kc=kc: v.tensor_copy(
            wdt[:, kc * S:(kc + 1) * S], wstage[:, 0:S]), inc="cast_w")
        ncast += 1
    p.wait("vector", "ld_c", p.val("ld_c"))
    one_c = p.sbuf("one_c", [1, 1], F32)
    mz = p.sync_run("vector", lambda v: (v.memset(one_c[:], 1.0),
                                         v.memset(epsc[:], 1e-8),
                                         v.memset(eps5[:], 1e-5),
                                         v.memset(amx[:], 0.0),
                                         v.memset(stat[:], 0.0))[-1], name="v_c")
    p.wait("tensor", "v_c", mz)
    p.wait("scalar", "v_c", mz)
    for n in range(2):
        def wb(t, n=n):
            return t.matmul(ps[2][:], one_c[0:1, 0:1].broadcast_to([1, 128])
                            if False else one_c[0:1, 0:1],
                            wnr[0:1, n * 512:(n + 1) * 512],
                            start=True, stop=True)
        pass
    ones128 = p.sbuf("ones128", [1, 128], F32)
    ones512 = p.sbuf("ones512", [1, 512], F32)
    mo = p.sync_run("vector", lambda v: (v.memset(ones128[:], 1.0),
                                         v.memset(ones512[:], 1.0))[-1], name="v_c")
    p.wait("tensor", "v_c", mo)
    for n in range(2):
        def wb(t, n=n):
            return t.matmul(ps[2][:], ones128[0:1, :],
                            wnr[0:1, n * 512:(n + 1) * 512], start=True, stop=True)
        bt = p.run("tensor", wb, inc="wb_mm")
        p.wait("vector", "wb_mm", bt)
        vt = p.run("vector", lambda v, n=n: v.tensor_copy(
            wnb[:, n * 512:(n + 1) * 512], ps[2][:]), inc="wb_ev")
        p.wait("tensor", "wb_ev", p.val("wb_ev"))

    for ti in range(NT):
        buf = ti % 2
        wt = (f"st_xn{buf}", p.val(f"st_xn{buf}")) if ti >= 2 else None
        ldv = p.dma("sync", xs[buf][:], x1_dr[ti * 128:(ti + 1) * 128, :],
                    f"ld_x{buf}", wait=wt)
        p.wait("vector", f"ld_x{buf}", ldv)
        p.sync_run("vector", lambda v, x=xs[buf]: v.reduce_sum(
            stat[:, 0:1], x[:], axis=AX.X))
        p.sync_run("vector", lambda v: v.tensor_scalar_mul(
            stat[:, 0:1], stat[:, 0:1], 1.0 / D))
        ct = p.run("vector", lambda v, x=xs[buf]: v.tensor_scalar(
            x[:], x[:], stat[:, 0:1], None, ALU.subtract), inc="v_ln")
        p.wait("scalar", "v_ln", ct)
        if ti >= 2:
            p.wait("scalar", f"st_xr{buf}", p.val(f"st_xr{buf}"))
        p.sync_run("scalar", lambda s: s.memzero(stat[:, 2:3]))
        p.sync_run("scalar", lambda s, x=xs[buf], q=sq[buf]: s.activation(
            q[:], x[:], AF.Square, accum_out=stat[:, 2:3]))
        if p.val("sr_vector"):
            p.wait("scalar", "sr_vector", p.val("sr_vector"))
        rt = p.run("scalar", lambda s: s.activation(
            stat[:, 4:5], stat[:, 2:3], AF.Sqrt, scale=1.0 / D, bias=eps5[:, 0:1]),
            inc="s_rs")
        p.wait("vector", "s_rs", rt)
        p.sync_run("vector", lambda v: v.reciprocal(stat[:, 5:6], stat[:, 4:5]))
        p.sync_run("vector", lambda v, x=xs[buf], xf=xnf[buf]: v.tensor_scalar(
            xf[:], x[:], stat[:, 5:6], None, ALU.mult))
        at = p.run("vector", lambda v, xf=xnf[buf]: v.reduce_max(
            stat[:, 6:7], xf[:], axis=AX.X, apply_absolute_value=True), inc="v_am")
        p.wait("vector", "v_am", at)
        p.sync_run("vector", lambda v: v.tensor_tensor(
            amx[:, 0:1], amx[:, 0:1], stat[:, 6:7], ALU.max))
        nt_ = p.run("vector", lambda v, xf=xnf[buf], xb=xnb[buf]:
                    v.tensor_copy(xb[:], xf[:]), inc="v_nm")
        p.dma("gpsimd", xn1_dr[ti * 128:(ti + 1) * 128, :], xnf[buf][:],
              f"st_xn{buf}", wait=("v_nm", nt_))
        p.dma("gpsimd", xn1b_dr[ti * 128:(ti + 1) * 128, :], xnb[buf][:],
              f"st_xn{buf}")
        p.wait("scalar", "sr_vector", p.val("sr_vector"))
        p.sync_run("scalar", lambda s: s.memzero(stat[:, 3:4]))
        p.sync_run("scalar", lambda s, xf=xnf[buf], q=sq[buf]: s.activation(
            q[:], xf[:], AF.Square, accum_out=stat[:, 3:4]))
        rt2 = p.run("scalar", lambda s: s.activation(
            stat[:, 7:8], stat[:, 3:4], AF.Sqrt, scale=1.0 / D, bias=epsc[:, 0:1]),
            inc="s_rs")
        p.wait("vector", "s_rs", rt2)
        p.sync_run("vector", lambda v: v.reciprocal(stat[:, 1:2], stat[:, 7:8]))
        p.sync_run("vector", lambda v, xf=xnf[buf], q=sq[buf]: v.tensor_scalar(
            q[:], xf[:], stat[:, 1:2], None, ALU.mult))
        p.sync_run("vector", lambda v, q=sq[buf]: v.tensor_tensor(
            q[:], q[:], wnb[:], ALU.mult))
        xr_ = p.run("vector", lambda v, q=sq[buf], xb=xrb[buf]:
                    v.tensor_copy(xb[:], q[:]), inc="v_xr")
        p.dma("gpsimd", xr_dr[ti * 128:(ti + 1) * 128, :], xrb[buf][:],
              f"st_xr{buf}", wait=("v_xr", xr_))
    amt = p.val("sr_vector")
    p.wait("tensor", "sr_vector", amt)
    at2 = p.run("tensor", lambda t: t.matmul(ps[2][0:1, 0:128], amx[:, 0:1],
                                             identf[:], start=True, stop=True),
                inc="am_mm")
    p.wait("vector", "am_mm", at2)
    p.sync_run("vector", lambda v: v.reduce_max(amr[0:1, 0:1],
                                                ps[2][0:1, 0:128], axis=AX.X))
    av = p.run("vector", lambda v: v.tensor_copy(amr[0:1, 1:2], amr[0:1, 0:1]),
               inc="v_amf")
    p.dma("gpsimd", amax_dr, amr[0:1, 0:1], "st_am", wait=("v_amf", av))

    xn_hist = []
    for g in range(4):
        buf = g % 2
        if g == 0:
            for b2 in range(2):
                p.wait("tensor", f"st_xr{b2}", p.val(f"st_xr{b2}"))
                p.wait("scalar", f"st_xr{b2}", p.val(f"st_xr{b2}"))
        for c in range(8):
            wt = None
            if g >= 2 and c == 0:
                wt = ("mm_b", xn_hist[g - 2])
            p.dma("scalar", xnt[buf][c][:],
                  xr_dr[g * 512:(g + 1) * 512, c * 128:(c + 1) * 128],
                  f"ld_xt{buf}", wait=wt, transpose=True)
        p.wait("tensor", f"ld_xt{buf}", p.val(f"ld_xt{buf}"))
        p.wait("tensor", "cast_w", ncast)
        for j in range(6):
            sl = j % 2
            if p.val(f"evb_{sl}"):
                p.wait("tensor", f"evb_{sl}", p.val(f"evb_{sl}"))

            def mmi(t, g=g, j=j, sl=sl, buf=buf):
                oc = j
                for c in range(8):
                    t.matmul(ps[sl][:],
                             win[:, c * 3 * S + oc * 128:
                                 c * 3 * S + (oc + 1) * 128],
                             xnt[buf][c][:], start=(c == 0), stop=False)
                return t.matmul(ps[sl][:], brows[0:1, oc * 128:(oc + 1) * 128],
                                ones512[0:1, :], start=False, stop=True)
            mt = p.run("tensor", mmi, inc="mm_b")
            p.wait("scalar", "mm_b", mt)
            func = AF.Sigmoid if j >= 4 else AF.Copy

            def ev(s, g=g, j=j, sl=sl, func=func):
                return s.activation(uzg[j][:, g * 512:(g + 1) * 512], ps[sl][:],
                                    func)
            p.run("scalar", ev, inc=f"evb_{sl}")
        xn_hist.append(p.val("mm_b"))
    for sl in range(2):
        p.wait("tensor", f"evb_{sl}", p.val(f"evb_{sl}"))

    ev_all = [p.val("evb_0"), p.val("evb_1")]
    p.wait("gpsimd", "evb_0", ev_all[0])
    p.wait("gpsimd", "evb_1", ev_all[1])
    for j in range(2):
        p.dma("gpsimd", sigg_dr[j * 128:(j + 1) * 128, :], uzg[4 + j][:], "st_sg")

    p.wait("tensor", "sr_vector", p.val("sr_vector"))
    for g in range(4):
        for j in range(2):
            for which in range(2):
                sl = 2 if which == 0 else 0
                if p.val(f"evc_{sl}"):
                    p.wait("tensor", f"evc_{sl}", p.val(f"evc_{sl}"))

                def mmx(t, g=g, j=j, which=which, sl=sl):
                    w = wx if which == 0 else wdt
                    srcv = uzg[0:2] if which == 0 else uzg[2:4]
                    boff = (3 * S if which == 0 else 4 * S) + j * 128
                    for c in range(2):
                        t.matmul(ps[sl][:],
                                 w[:, c * S + j * 128:c * S + (j + 1) * 128],
                                 srcv[c][:, g * 512:(g + 1) * 512],
                                 start=(c == 0), stop=False)
                    return t.matmul(ps[sl][:], brows[0:1, boff:boff + 128],
                                    ones512[0:1, :], start=False, stop=True)
                mt = p.run("tensor", mmx, inc="mm_b")
                p.wait("scalar", "mm_b", mt)

                def evx(s, g=g, j=j, which=which, sl=sl):
                    if which == 0:
                        return s.activation(ut[j][:, g * 512:(g + 1) * 512],
                                            ps[sl][:], AF.Tanh)
                    return s.activation(dt_[j][:, g * 512:(g + 1) * 512],
                                        ps[sl][:], AF.Sigmoid)
                p.run("scalar", evx, inc=f"evc_{sl}")
    for sl in (0, 2):
        p.wait("vector", f"evc_{sl}", p.val(f"evc_{sl}"))

    for j in range(2):
        p.sync_run("vector", lambda v, j=j: v.tensor_tensor(
            bb[j][:], dt_[j][:], ut[j][:], ALU.mult))
        p.sync_run("vector", lambda v, j=j: v.tensor_scalar(
            dt_[j][:], dt_[j][:], -1.0, 1.0, ALU.mult, ALU.add))
        p.sync_run("vector", lambda v, j=j: v.tensor_tensor_scan(
            hs[j][:], dt_[j][:], bb[j][:], 0.0, ALU.mult, ALU.add))
        p.sync_run("vector", lambda v, j=j: v.tensor_tensor_scan(
            pp[j][:], dt_[j][:], dt_[j][:], 1.0, ALU.mult, ALU.bypass))
        et = p.run("vector", lambda v, j=j: (
            v.tensor_copy(ends[:, 2 * j:2 * j + 1], pp[j][:, T - 1:T]),
            v.tensor_copy(ends[:, 2 * j + 1:2 * j + 2], hs[j][:, T - 1:T]))[-1],
            inc="v_end")
        p.dma("gpsimd", hs_dr[j * 128:(j + 1) * 128, :], hs[j][:], "st_s",
              wait=("v_end", p.val("v_end")))
        p.dma("gpsimd", pp_dr[j * 128:(j + 1) * 128, :], pp[j][:], "st_s")
    p.dma("gpsimd", ends_dr[0:128, :], ends[:, 0:2], "st_s")
    p.dma("gpsimd", ends_dr[128:256, :], ends[:, 2:4], "st_s")
    p.wait("sync", "st_s", p.val("st_s"))
    p.wait("sync", "st_sg", p.val("st_sg"))
    p.wait("sync", "st_am", p.val("st_am"))
    for b in range(2):
        p.wait("sync", f"st_xn{b}", p.val(f"st_xn{b}"))
        p.wait("sync", f"st_xr{b}", p.val(f"st_xr{b}"))
    return p.finalize()


def build_c():
    nc = bass.Bass("TRN2", target_bir_lowering=False)
    x1_dr = nc.dram_tensor("x1", [T, D], F32, kind="ExternalInput").ap()
    xn1_dr = nc.dram_tensor("xn1_tm", [T, D], F32, kind="ExternalInput").ap()
    xn1b_dr = nc.dram_tensor("xn1b", [T, D], BF16, kind="ExternalInput").ap()
    hs_dr = nc.dram_tensor("hs_loc", [S, T], F32, kind="ExternalInput").ap()
    pp_dr = nc.dram_tensor("pp", [S, T], F32, kind="ExternalInput").ap()
    sigg_dr = nc.dram_tensor("sigg", [S, T], BF16, kind="ExternalInput").ap()
    endsp_dr = nc.dram_tensor("ends_prev", [S, 6], F32, kind="ExternalInput").ap()
    amax8_dr = nc.dram_tensor("amax8", [1, 8], F32, kind="ExternalInput").ap()
    wout1_dr = nc.dram_tensor("W_out1", [S, D], F32, kind="ExternalInput").ap()
    bout1_dr = nc.dram_tensor("b_out1_r", [1, D], F32, kind="ExternalInput").ap()
    wfT_dr = nc.dram_tensor("wf_T", [D, D], F32, kind="ExternalInput").ap()
    bf_dr = nc.dram_tensor("b_f_r", [1, D], F32, kind="ExternalInput").ap()
    ones_dr = nc.dram_tensor("ones_r", [1, 512], F32, kind="ExternalInput").ap()
    out_dr = nc.dram_tensor("out", [T, D], F32, kind="ExternalOutput").ap()
    xdq_dr = nc.dram_tensor("xdq_tm", [T, D], BF16, kind="Internal").ap()
    sg1_dr = nc.dram_tensor("sg1_fm", [D, T], BF16, kind="Internal").ap()

    p = Prog(nc)
    hs = [p.sbuf(f"hs{j}", [128, T], F32) for j in range(2)]
    pp = [p.sbuf(f"pp{j}", [128, T], F32) for j in range(2)]
    sgg = [p.sbuf(f"sgg{j}", [128, T], BF16) for j in range(2)]
    yy = [p.sbuf(f"yy{j}", [128, T], BF16) for j in range(2)]
    endsp = p.sbuf("endsp", [128, 12], F32)
    hin = p.sbuf("hin", [128, 2], F32)
    am8 = p.sbuf("am8", [1, 8], F32)
    scl8 = p.sbuf("scl8", [1, 4], F32)
    scol = p.sbuf("scol", [128, 2], F32)
    wstage = p.sbuf("wstage", [128, D], F32)
    wo1 = p.sbuf("wo1", [128, 2 * D], BF16)
    wfq = [p.sbuf(f"wfq{c}", [128, D], BF16) for c in range(8)]
    bo1 = p.sbuf("bo1", [1, D], F32)
    bfr = p.sbuf("bfr", [1, D], F32)
    onesr = p.sbuf("onesr", [1, 512], F32)
    onesc = p.sbuf("onesc", [1, 128], F32)
    s1f = [p.sbuf(f"s1f{c}", [128, T], BF16) for c in range(8)]
    xnf1 = [p.sbuf(f"xnf1{c}", [128, T], BF16) for c in range(8)]
    xst = [p.sbuf(f"xst{b}", [128, D], F32) for b in range(2)]
    f8t = p.sbuf("f8t", [128, D], F8)
    xdqb = [p.sbuf(f"xdqb{b}", [128, D], BF16) for b in range(2)]
    wst8 = p.sbuf("wst8", [128, D], F32)
    amw = p.sbuf("amw", [128, 2], F32)
    amwr = p.sbuf("amwr", [1, 128], F32)
    identf = p.sbuf("identfc", [128, 128], F32)
    xq8 = [[p.sbuf(f"xq8{b}_{c}", [128, 512], BF16) for c in range(8)]
           for b in range(2)]
    g1ev = [p.sbuf(f"g1ev{i}", [128, 512], BF16) for i in range(2)]
    sg1t = [p.sbuf(f"sg1t{b}", [128, D], BF16) for b in range(2)]
    outb = [p.sbuf(f"outb{b}", [128, D], F32) for b in range(2)]
    idf_dr = nc.dram_tensor("identf", [128, 128], F32, kind="ExternalInput").ap()
    ps = [p.psum(f"ps{i}", [128, 512], F32) for i in range(3)]

    for j in range(2):
        p.dma("sync", hs[j][:], hs_dr[j * 128:(j + 1) * 128, :], "ld_a")
        p.dma("sync", pp[j][:], pp_dr[j * 128:(j + 1) * 128, :], "ld_a")
        p.dma("sync", sgg[j][:], sigg_dr[j * 128:(j + 1) * 128, :], "ld_a")
    p.dma("sync", endsp[0:128, 0:6], endsp_dr[0:128, :], "ld_a")
    p.dma("sync", endsp[0:128, 6:12], endsp_dr[128:256, :], "ld_a")
    p.dma("sync", am8[:], amax8_dr, "ld_a")
    p.dma("sync", bo1[:], bout1_dr, "ld_a")
    p.dma("sync", bfr[:], bf_dr, "ld_a")
    p.dma("sync", onesr[:], ones_dr, "ld_a")
    p.dma("sync", identf[:], idf_dr, "ld_a")
    ncast = 0
    for kc in range(2):
        lw = p.dma("sync", wstage[:], wout1_dr[kc * 128:(kc + 1) * 128, :], "ld_w",
                   wait=(("cast_w", ncast) if ncast else None))
        p.wait("vector", "ld_w", lw)
        p.run("vector", lambda v, kc=kc: v.tensor_copy(
            wo1[:, kc * D:(kc + 1) * D], wstage[:]), inc="cast_w")
        ncast += 1
    p.wait("vector", "ld_a", p.val("ld_a"))
    mz = p.sync_run("vector", lambda v: (v.memset(onesc[:], 1.0),
                                         v.memset(amw[:], 0.0))[-1])

    for j in range(2):
        sl6 = slice(6 * j, 6 * j + 6)
        e = endsp[:, 6 * j:6 * j + 6]
        p.sync_run("vector", lambda v, j=j, e=e: v.tensor_copy(
            hin[:, j:j + 1], e[:, 5:6]))
        p.sync_run("vector", lambda v, j=j, e=e: v.tensor_tensor(
            hin[:, j:j + 1], hin[:, j:j + 1], e[:, 2:3], ALU.mult))
        p.sync_run("vector", lambda v, j=j, e=e: v.tensor_tensor(
            hin[:, j:j + 1], hin[:, j:j + 1], e[:, 3:4], ALU.add))
        p.sync_run("vector", lambda v, j=j, e=e: v.tensor_tensor(
            hin[:, j:j + 1], hin[:, j:j + 1], e[:, 0:1], ALU.mult))
        p.sync_run("vector", lambda v, j=j, e=e: v.tensor_tensor(
            hin[:, j:j + 1], hin[:, j:j + 1], e[:, 1:2], ALU.add))
        p.sync_run("vector", lambda v, j=j: v.scalar_tensor_tensor(
            hs[j][:], pp[j][:], hin[:, j:j + 1], hs[j][:], ALU.mult, ALU.add))
        yt = p.run("vector", lambda v, j=j: v.tensor_tensor(
            yy[j][:], hs[j][:], sgg[j][:], ALU.mult), inc="v_y")
    p.wait("tensor", "v_y", p.val("v_y"))
    p.wait("tensor", "cast_w", ncast)

    for c in range(8):
        p.dma("scalar", xnf1[c][:], xn1b_dr[:, c * 128:(c + 1) * 128], "ld_xT",
              transpose=True)
    p.wait("vector", "ld_xT", p.val("ld_xT"))
    for n in range(4):
        for oc in range(8):
            sl = oc % 2
            if p.val(f"evo_{sl}"):
                p.wait("tensor", f"evo_{sl}", p.val(f"evo_{sl}"))

            def mmo(t, n=n, oc=oc, sl=sl):
                for j in range(2):
                    t.matmul(ps[sl][:], wo1[:, j * D + oc * 128:
                                            j * D + (oc + 1) * 128],
                             yy[j][:, n * 512:(n + 1) * 512],
                             start=(j == 0), stop=False)
                return t.matmul(ps[sl][:], bo1[0:1, oc * 128:(oc + 1) * 128],
                                onesr[0:1, 0:512], start=False, stop=True)
            mt = p.run("tensor", mmo, inc="mm_c")
            p.wait("vector", "mm_c", mt)
            et = p.run("vector", lambda v, n=n, oc=oc, sl=sl: v.tensor_tensor(
                s1f[oc][:, n * 512:(n + 1) * 512], ps[sl][:],
                xnf1[oc][:, n * 512:(n + 1) * 512], ALU.add), inc=f"evo_{sl}")

    p.sync_run("vector", lambda v: v.reduce_max(scl8[0:1, 0:1], am8[0:1, :],
                                                axis=AX.X))
    p.sync_run("vector", lambda v: v.tensor_scalar(
        scl8[0:1, 1:2], scl8[0:1, 0:1], 1e-12, None, ALU.max))
    p.sync_run("vector", lambda v: v.reciprocal(scl8[0:1, 2:3], scl8[0:1, 1:2]))
    p.sync_run("vector", lambda v: v.tensor_scalar(
        scl8[0:1, 2:3], scl8[0:1, 2:3], 224.0, None, ALU.mult))
    sc8 = p.sync_run("vector", lambda v: v.tensor_scalar(
        scl8[0:1, 3:4], scl8[0:1, 1:2], 1.0 / 224.0, None, ALU.mult))
    p.wait("tensor", "sr_vector", sc8)
    bt = p.run("tensor", lambda t: t.matmul(ps[2][:, 0:2], onesc[0:1, :],
                                            scl8[0:1, 2:4], start=True, stop=True),
               inc="mm_c")
    p.wait("vector", "mm_c", p.val("mm_c"))
    p.sync_run("vector", lambda v: v.tensor_copy(scol[:], ps[2][:, 0:2]))

    for ti in range(NT):
        buf = ti % 2
        wt = (f"st_xd{buf}", p.val(f"st_xd{buf}")) if ti >= 2 else None
        ldv = p.dma("sync", xst[buf][:], xn1_dr[ti * 128:(ti + 1) * 128, :],
                    f"ld_xq{buf}", wait=wt)
        p.wait("vector", f"ld_xq{buf}", ldv)
        if p.val("v_q8"):
            p.wait("vector", "v_q8", p.val("v_q8"))
        p.sync_run("vector", lambda v, buf=buf: v.tensor_scalar(
            f8t[:], xst[buf][:], scol[:, 0:1], None, ALU.mult))
        qt = p.run("vector", lambda v, buf=buf: v.tensor_scalar(
            xdqb[buf][:], f8t[:], scol[:, 1:2], None, ALU.mult), inc="v_q8")
        p.dma("gpsimd", xdq_dr[ti * 128:(ti + 1) * 128, :], xdqb[buf][:],
              f"st_xd{buf}", wait=("v_q8", qt))

    for kc in range(8):
        lw = p.dma("sync", wst8[:], wfT_dr[kc * 128:(kc + 1) * 128, :], "ld_w8",
                   wait=(("w8_ev", p.val("w8_ev")) if kc else None))
        p.wait("vector", "ld_w8", lw)
        p.sync_run("vector", lambda v: v.reduce_max(amw[:, 1:2], wst8[:],
                                                    axis=AX.X,
                                                    apply_absolute_value=True))
        p.run("vector", lambda v: v.tensor_tensor(
            amw[:, 0:1], amw[:, 0:1], amw[:, 1:2], ALU.max), inc="w8_ev")
        p.wait("vector", "w8_ev", p.val("w8_ev"))
    aw = p.val("w8_ev")
    p.wait("tensor", "w8_ev", aw)
    p.run("tensor", lambda t: t.matmul(ps[2][0:1, 0:128], amw[:, 0:1], identf[:],
                                       start=True, stop=True), inc="mm_c")
    p.wait("vector", "mm_c", p.val("mm_c"))
    p.sync_run("vector", lambda v: v.reduce_max(scl8[0:1, 0:1],
                                                ps[2][0:1, 0:128], axis=AX.X))
    p.sync_run("vector", lambda v: v.tensor_scalar(
        scl8[0:1, 1:2], scl8[0:1, 0:1], 1e-12, None, ALU.max))
    p.sync_run("vector", lambda v: v.reciprocal(scl8[0:1, 2:3], scl8[0:1, 1:2]))
    p.sync_run("vector", lambda v: v.tensor_scalar(
        scl8[0:1, 2:3], scl8[0:1, 2:3], 224.0, None, ALU.mult))
    sw8 = p.sync_run("vector", lambda v: v.tensor_scalar(
        scl8[0:1, 3:4], scl8[0:1, 1:2], 1.0 / 224.0, None, ALU.mult))
    p.wait("tensor", "sr_vector", sw8)
    p.run("tensor", lambda t: t.matmul(ps[2][:, 0:2], onesc[0:1, :],
                                       scl8[0:1, 2:4], start=True, stop=True),
          inc="mm_c")
    p.wait("vector", "mm_c", p.val("mm_c"))
    p.sync_run("vector", lambda v: v.tensor_copy(scol[:], ps[2][:, 0:2]))
    for kc in range(8):
        wt8 = ("w8q_ev", p.val("w8q_ev")) if kc else ("w8_ev", p.val("w8_ev"))
        lw = p.dma("sync", wst8[:], wfT_dr[kc * 128:(kc + 1) * 128, :], "ld_w8b",
                   wait=wt8)
        p.wait("vector", "ld_w8b", lw)
        if kc == 0 and p.val("v_q8"):
            p.wait("vector", "v_q8", p.val("v_q8"))
        if p.val("w8q_ev"):
            p.wait("vector", "w8q_ev", p.val("w8q_ev"))
        p.sync_run("vector", lambda v: v.tensor_scalar(
            f8t[:], wst8[:], scol[:, 0:1], None, ALU.mult))
        p.run("vector", lambda v, kc=kc: v.tensor_scalar(
            wfq[kc][:], f8t[:], scol[:, 1:2], None, ALU.mult), inc="w8q_ev")
    p.wait("tensor", "w8q_ev", p.val("w8q_ev"))

    xq_hist = []
    for g in range(4):
        buf = g % 2
        if g == 0:
            for b2 in range(2):
                p.wait("tensor", f"st_xd{b2}", p.val(f"st_xd{b2}"))
                p.wait("scalar", f"st_xd{b2}", p.val(f"st_xd{b2}"))
        for c in range(8):
            wt = None
            if g >= 2 and c == 0:
                wt = ("mm_c", xq_hist[g - 2])
            p.dma("scalar", xq8[buf][c][:],
                  xdq_dr[g * 512:(g + 1) * 512, c * 128:(c + 1) * 128],
                  f"ld_q8{buf}", wait=wt, transpose=True)
        p.wait("tensor", f"ld_q8{buf}", p.val(f"ld_q8{buf}"))
        for oc in range(8):
            sl = oc % 2
            if p.val(f"evg_{sl}"):
                p.wait("tensor", f"evg_{sl}", p.val(f"evg_{sl}"))

            def mmf(t, g=g, oc=oc, sl=sl, buf=buf):
                for c in range(8):
                    t.matmul(ps[sl][:], wfq[c][:, oc * 128:(oc + 1) * 128],
                             xq8[buf][c][:], start=(c == 0), stop=False)
                return t.matmul(ps[sl][:], bfr[0:1, oc * 128:(oc + 1) * 128],
                                onesr[0:1, 0:512], start=False, stop=True)
            mt = p.run("tensor", mmf, inc="mm_c")
            p.wait("scalar", "mm_c", mt)
            if p.val(f"st_sg1_{sl}"):
                p.wait("scalar", f"st_sg1_{sl}", p.val(f"st_sg1_{sl}"))
            et = p.run("scalar", lambda s, sl=sl: s.activation(
                g1ev[sl][:], ps[sl][:], AF.Silu), inc=f"evg_{sl}")
            p.wait("vector", f"evg_{sl}", p.val(f"evg_{sl}"))
            st2 = p.run("vector", lambda v, g=g, oc=oc, sl=sl: v.tensor_tensor(
                g1ev[sl][:], g1ev[sl][:], s1f[oc][:, g * 512:(g + 1) * 512],
                ALU.mult), inc=f"evg_{sl}")
            p.dma("gpsimd", sg1_dr[oc * 128:(oc + 1) * 128, g * 512:(g + 1) * 512],
                  g1ev[sl][:], f"st_sg1_{sl}", wait=(f"evg_{sl}", st2))
        xq_hist.append(p.val("mm_c"))
    for sl in range(2):
        p.wait("sync", f"st_sg1_{sl}", p.val(f"st_sg1_{sl}"))
        p.wait("scalar", f"st_sg1_{sl}", p.val(f"st_sg1_{sl}"))

    for ti in range(NT):
        buf = ti % 2
        wt = (f"st_o{buf}", p.val(f"st_o{buf}")) if ti >= 2 else None
        p.dma("sync", xst[buf][:], x1_dr[ti * 128:(ti + 1) * 128, :],
              f"ld_f{buf}", wait=wt)
        p.dma("scalar", sg1t[buf][:], sg1_dr[:, ti * 128:(ti + 1) * 128],
              f"ld_g{buf}", wait=wt, transpose=True)
        p.wait("vector", f"ld_f{buf}", p.val(f"ld_f{buf}"))
        p.wait("vector", f"ld_g{buf}", p.val(f"ld_g{buf}"))
        at = p.run("vector", lambda v, buf=buf: v.tensor_tensor(
            outb[buf][:], xst[buf][:], sg1t[buf][:], ALU.add), inc="v_o")
        p.dma("gpsimd", out_dr[ti * 128:(ti + 1) * 128, :], outb[buf][:],
              f"st_o{buf}", wait=("v_o", at))
    for b in range(2):
        p.wait("sync", f"st_o{b}", p.val(f"st_o{b}"))
    return p.finalize()


def host_prep(inp):
    x = np.asarray(inp["L0_attn_x"])
    groups = LOCAL_OFFS + LARGE_OFFS
    perm = []
    indic = np.zeros((44, 24), np.float32)
    all_offs = [(1 << j) * t for j in range(11) for t in range(4)]
    r = 0
    for gi, off in enumerate(groups):
        for k, o in enumerate(all_offs):
            if o == off:
                perm.append(k)
                indic[r, gi] = 1.0
                r += 1
    assert r == 44
    posb_perm = np.ascontiguousarray(np.asarray(inp["L0_attn_pos_bias"])[perm])
    indic_b = indic.astype(ml_dtypes.bfloat16)

    wg = np.asarray(inp["L0_ffn_w_gate"])
    wu = np.asarray(inp["L0_ffn_w_up"])
    wd = np.asarray(inp["L0_ffn_w_down"])
    b_qkv = np.asarray(inp["L0_attn_b_qkv"])
    b_gate = np.asarray(inp["L0_attn_b_gate"])

    per_core = []
    for c in range(NCORES):
        seq, chunk = divmod(c, 4)
        start = chunk * T
        vstart = max(0, start - HALO)
        pad = HALO - (start - vstart)
        xp = np.zeros((TT, D), np.float32)
        xp[pad:] = x[seq, vstart:start + T]
        valid = np.zeros(TT, np.float32)
        valid[pad:] = 1.0
        per_core.append(dict(
            xpad=xp,
            maskcol=np.ascontiguousarray(valid.reshape(NTT, 128).T),
            validrow=valid.reshape(1, TT).copy(),
            wg_sh=np.ascontiguousarray(wg[c * WQ_SH:(c + 1) * WQ_SH]),
            wu_sh=np.ascontiguousarray(wu[c * WQ_SH:(c + 1) * WQ_SH]),
            wd_sh=np.ascontiguousarray(wd[c * 128:(c + 1) * 128]),
            posb=posb_perm,
            indic=indic_b,
            bq_r=np.ascontiguousarray(b_qkv[None, 0:D]),
            bk_r=np.ascontiguousarray(b_qkv[None, D:2 * D]),
            bv_r=np.ascontiguousarray(b_qkv[None, 2 * D:3 * D]),
            bg_r=np.ascontiguousarray(b_gate[None, :]),
        ))
    return per_core


def _make_consts():
    onehot = np.zeros((128, 16 * 16), np.float32)
    for h in range(16):
        onehot[:, h * 16 + h] = 1.0
    indc = np.zeros((128, 8 * 16), np.float32)
    for c in range(8):
        indc[0:64, c * 16 + 2 * c] = 1.0
        indc[64:128, c * 16 + 2 * c + 1] = 1.0
    indzl = np.zeros((128, 16), np.float32)
    for q in range(4):
        for h in range(16):
            indzl[32 * q + h, h] = 1.0
    identf = np.eye(128, dtype=np.float32)
    indh = np.zeros((16, 8 * 128), np.float32)
    for c in range(8):
        for h in (2 * c, 2 * c + 1):
            for dd in range(64):
                indh[h, c * 128 + (h % 2) * 64 + dd] = 1.0
    return dict(onehot=onehot.astype(ml_dtypes.bfloat16),
                indc=indc.astype(ml_dtypes.bfloat16), indzl=indzl,
                identf=identf, indh=indh)


def _wmask_from_wdrev(wdrev):
    wm = np.zeros((128, 2 * 16 * 128), dtype=wdrev.dtype)
    cc, mm_ = np.meshgrid(np.arange(128), np.arange(128), indexing="ij")
    for h in range(16):
        hg, j = divmod(h, 4)
        par, jj = j % 2, j // 2
        for cti, base in [(0, 256), (1, 128)]:
            idx = np.clip(base + mm_ - cc, 0, 383)
            pos = par * 2048 + hg * 512 + cti * 256 + jj * 128
            wm[:, pos:pos + 128] = wdrev[h][idx]
    return wm


def kernel(**inputs):
    from concourse.bass_utils import run_bass_kernel_spmd
    inp = inputs
    C = _make_consts()
    pc = host_prep(inp)
    cores = list(range(8))

    def launch(build, maps):
        return run_bass_kernel_spmd(build(), maps, core_ids=cores).results

    r1 = launch(build_a1,
                [{k: pc[c][k] for k in ["xpad", "maskcol", "wg_sh", "wu_sh",
                                        "wd_sh", "posb", "indic"]}
                 for c in range(8)])
    wq = np.ascontiguousarray(np.asarray(inp["L0_attn_w_qkv"], np.float32))
    wg = np.ascontiguousarray(np.asarray(inp["L0_attn_w_gate"], np.float32))
    r2 = launch(build_a2,
                [dict(xnb=r1[c]["xnb"], w_qkv=wq, w_gate=wg,
                      bq_r=pc[c]["bq_r"], bk_r=pc[c]["bk_r"],
                      bv_r=pc[c]["bv_r"], bg_r=pc[c]["bg_r"],
                      validrow=pc[c]["validrow"]) for c in range(8)])
    r3 = launch(build_a3,
                [dict(q_fm=r2[c]["q_fm"], k_fm=r2[c]["k_fm"], v_tm=r2[c]["v_tm"],
                      wmask_full=_wmask_from_wdrev(r1[c]["wdrev"]),
                      lnw8=r1[c]["lnw8"], onehot=C["onehot"], indc=C["indc"],
                      indzl=C["indzl"], identf=C["identf"]) for c in range(8)])
    wqgT = np.ascontiguousarray(np.concatenate([r1[c]["wqg"] for c in range(8)], 0).T)
    wquT = np.ascontiguousarray(np.concatenate([r1[c]["wqu"] for c in range(8)], 0).T)
    wqdT = np.ascontiguousarray(np.concatenate([r1[c]["wqd"] for c in range(8)], 0).T)
    r4 = launch(build_a4,
                [dict(xq_tm=r1[c]["xq"], wqgT=wqgT, wquT=wquT, wqdT=wqdT)
                 for c in range(8)])
    ones512 = np.ones((1, 512), np.float32)
    x = np.asarray(inp["L0_attn_x"], np.float32)
    xown = [np.ascontiguousarray(x[c // 4, (c % 4) * T:(c % 4 + 1) * T])
            for c in range(8)]
    w_out0 = np.ascontiguousarray(np.asarray(inp["L0_attn_w_out"], np.float32))
    b_out0 = np.asarray(inp["L0_attn_b_out"], np.float32)
    r5 = launch(build_a5,
                [dict(glo_fm=r3[c]["glo_fm"], olg_tm=r3[c]["olg_tm"],
                      z_all=r3[c]["z_all"], gate_fm=r2[c]["gate_fm"],
                      g0_fm=r4[c]["g0_fm"], w_out=w_out0,
                      b_out_r=b_out0[None, :],
                      ones_r=ones512, indh=C["indh"], x_own=xown[c])
                 for c in range(8)])
    rB = launch(build_b,
                [dict(x1=r5[c]["x1"],
                      W_in=np.ascontiguousarray(np.asarray(inp["L1_attn_W_in"], np.float32)),
                      b_in_r=np.asarray(inp["L1_attn_b_in"], np.float32)[None, :],
                      W_x=np.ascontiguousarray(np.asarray(inp["L1_attn_W_x"], np.float32)),
                      b_x_r=np.asarray(inp["L1_attn_b_x"], np.float32)[None, :],
                      W_dt=np.ascontiguousarray(np.asarray(inp["L1_attn_W_dt"], np.float32)),
                      b_dt_r=np.asarray(inp["L1_attn_b_dt"], np.float32)[None, :],
                      w_norm_r=np.asarray(inp["L1_attn_w_norm"], np.float32)[None, :],
                      identf=C["identf"]) for c in range(8)])
    amax8 = np.array([[rB[c]["amax"][0, 0] for c in range(8)]], np.float32)
    maps_c = []
    for c in range(8):
        seq, ch = divmod(c, 4)
        ends_prev = np.zeros((S, 6), np.float32)
        for jprev in range(1, 4):
            cp = c - jprev
            if cp // 4 == seq and cp >= 0:
                ends_prev[:, 2 * (jprev - 1):2 * jprev] = rB[cp]["ends"]
        maps_c.append(dict(
            x1=r5[c]["x1"], xn1_tm=rB[c]["xn1_tm"], xn1b=rB[c]["xn1b"],
            hs_loc=rB[c]["hs_loc"], pp=rB[c]["pp"], sigg=rB[c]["sigg"],
            ends_prev=ends_prev, amax8=amax8,
            W_out1=np.ascontiguousarray(np.asarray(inp["L1_attn_W_out"], np.float32)),
            b_out1_r=np.asarray(inp["L1_attn_b_out"], np.float32)[None, :],
            wf_T=np.ascontiguousarray(np.asarray(inp["L1_ffn_weight"], np.float32).T),
            b_f_r=np.asarray(inp["L1_ffn_bias"], np.float32)[None, :],
            ones_r=ones512, identf=C["identf"]))
    rC = launch(build_c, maps_c)
    out = np.zeros((2, 8192, 1024), np.float32)
    for c in range(8):
        seq, ch = divmod(c, 4)
        out[seq, ch * T:(ch + 1) * T] = rC[c]["out"]
    return out


# version 02
# speedup vs baseline: 1.0052x; 1.0052x over previous
import contextlib
import numpy as np
import ml_dtypes

import concourse.bass as bass
import concourse.mybir as mybir
from concourse.bass_utils import run_bass_kernel_spmd

F32 = mybir.dt.float32
BF16 = mybir.dt.bfloat16
F8 = mybir.dt.float8e4
AF = mybir.ActivationFunctionType
ALU = mybir.AluOpType
AX = mybir.AxisListType

NCORES = 8
T = 2048
HALO = 3072
TT = HALO + T
NT = T // 128
NTT = TT // 128
NG = TT // 512
D = 1024
DI = 2816
H = 16
HD = 64
S = 256
ASCALE = float(HD ** -0.5)
ESHIFT = -4.0

OFFSETS_ALL = sorted(set((1 << j) * t for j in range(11) for t in range(4)))
LOCAL_OFFS = [o for o in OFFSETS_ALL if o <= 128]
LARGE_OFFS = [o for o in OFFSETS_ALL if o > 128]
LARGE_DELTA = [o // 128 for o in LARGE_OFFS]
NLO = len(LARGE_OFFS)
NW = (NLO + 3) // 4

WQ_SH = DI // NCORES


class Prog:
    ENGINES = ("sync", "vector", "scalar", "tensor", "gpsimd")

    def __init__(self, nc):
        self.nc = nc
        self.items = {e: [] for e in self.ENGINES}
        self.es = contextlib.ExitStack()
        self.sems = {}
        self.cnt = {}

    def sbuf(self, name, shape, dtype, es=None):
        return (es or self.es).enter_context(self.nc.sbuf_tensor(name, shape, dtype))

    def psum(self, name, shape, dtype=F32):
        return self.es.enter_context(self.nc.psum_tensor(name, shape, dtype))

    def sem(self, name):
        if name not in self.sems:
            self.sems[name] = self.es.enter_context(self.nc.semaphore(name))
            self.cnt[name] = 0
        return self.sems[name]

    def inc(self, name, k=1):
        self.sem(name)
        self.cnt[name] += k
        return self.cnt[name]

    def val(self, name):
        return self.cnt.get(name, 0)

    def emit(self, engine, fn):
        self.items[engine].append(fn)

    def sync_run(self, engine, fn, name=None):
        name = name or f"sr_{engine}"
        tgt = self.run(engine, fn, inc=name)
        self.wait(engine, name, tgt)
        return tgt

    def fence(self, engine, ap):
        if not hasattr(self, "fscr"):
            self.fscr = {}
        if engine not in self.fscr:
            self.fscr[engine] = self.sbuf(f"fscr_{engine}", [1, 2], F32)
        scr = self.fscr[engine]
        name = f"fence_{engine}"
        tgt = self.inc(name)
        s = self.sems[name]
        def _f(e, ap=ap, scr=scr, s=s, tgt=tgt, engine=engine):
            cp = e.copy if engine == "scalar" else e.tensor_copy
            cp(scr[0:1, 0:1], ap)
            cp(scr[0:1, 1:2], scr[0:1, 0:1]).then_inc(s, 1)
            e.wait_ge(s, tgt)
        self.emit(engine, _f)

    def wait(self, engine, sem_name, val):
        s = self.sem(sem_name)
        self.emit(engine, lambda e, s=s, val=val: e.wait_ge(s, val))

    def run(self, engine, fn, inc=None):
        if inc is None:
            self.emit(engine, fn)
            return None
        tgt = self.inc(inc)
        s = self.sems[inc]
        def wrapped(e, fn=fn, s=s):
            ins = fn(e)
            ins.then_inc(s, 1)
        self.emit(engine, wrapped)
        return tgt

    def dma(self, eng, out, in_, sem_name, wait=None, transpose=False):
        tgt = self.inc(sem_name, 16)
        s = self.sems[sem_name]
        if wait is not None:
            self.wait(eng, wait[0], wait[1])
        if transpose:
            self.emit(eng, lambda e, out=out, in_=in_, s=s:
                      e.dma_start_transpose(out, in_).then_inc(s, 16))
        else:
            self.emit(eng, lambda e, out=out, in_=in_, s=s:
                      e.dma_start(out=out, in_=in_).then_inc(s, 16))
        return tgt

    def finalize(self):
        nc = self.nc
        items = self.items
        with self.es, nc.Block() as block:
            @block.sync
            def _(q):
                for f in items["sync"]:
                    f(q)

            @block.vector
            def _(v):
                for f in items["vector"]:
                    f(v)

            @block.scalar
            def _(s):
                for f in items["scalar"]:
                    f(s)

            @block.tensor
            def _(t):
                for f in items["tensor"]:
                    f(t)

            @block.gpsimd
            def _(g):
                for f in items["gpsimd"]:
                    f(g)
        return nc


def emit_fp4_quant(p, in_b, out_b, scr, rows, W, done_sem):
    nb = W // 16
    bm, scl, invb = scr["bm"], scr["scl"], scr["invb"]
    t, ta, c, d, e_t, sg = scr["t"], scr["ta"], scr["c"], scr["d"], scr["e"], scr["sg"]
    bx = scr["bx"]
    R = rows

    p.sync_run("vector", lambda v: v.reduce_max(
        bm[:R, :nb], in_b.rearrange("p (a b) -> p a b", b=16),
        axis=AX.X, apply_absolute_value=True))
    p.sync_run("vector", lambda v: v.tensor_scalar(
        scl[:R, :nb], bm[:R, :nb], 1.0 / 6.0, 1e-12, ALU.mult, ALU.max))
    p.sync_run("vector", lambda v: v.reciprocal(bm[:R, :nb], scl[:R, :nb]))
    p.sync_run("vector", lambda v: v.tensor_copy(invb[:R, :nb], bm[:R, :nb]))

    def expand_inv(v):
        iv = invb[:R, :nb].unsqueeze(2).broadcast_to([R, nb, 16])
        return v.tensor_copy(bx[:R, :W].rearrange("p (a b) -> p a b", b=16), iv)
    p.sync_run("vector", expand_inv)
    t_tgt = p.run("vector", lambda v: v.tensor_tensor(t[:R, :W], in_b, bx[:R, :W],
                                                      ALU.mult), inc="q_t")
    p.wait("scalar", "q_t", t_tgt)
    sg_tgt = p.run("scalar", lambda s: s.activation(sg[:R, :W], t[:R, :W], AF.Sign),
                   inc="q_sg")
    p.wait("vector", "q_t", t_tgt)

    p.sync_run("vector", lambda v: (
        v.tensor_tensor(ta[:R, :W], t[:R, :W], t[:R, :W], ALU.mult),
        v.tensor_copy(invb[:R, :nb], scl[:R, :nb]))[-1])
    p.sync_run("vector", lambda v: (
        v.tensor_scalar(c[:R, :W], ta[:R, :W], 0.0625, None, ALU.is_gt),
        v.tensor_scalar(d[:R, :W], ta[:R, :W], 6.25, None, ALU.is_gt),
        v.tensor_scalar(e_t[:R, :W], ta[:R, :W], 25.0, 2.0, ALU.is_gt, ALU.mult))[-1])
    p.sync_run("vector", lambda v: (
        v.scalar_tensor_tensor(c[:R, :W], ta[:R, :W], 0.5625, c[:R, :W],
                               ALU.is_gt, ALU.add),
        v.scalar_tensor_tensor(d[:R, :W], ta[:R, :W], 12.25, d[:R, :W],
                               ALU.is_gt, ALU.add))[-1])
    p.sync_run("vector", lambda v: v.scalar_tensor_tensor(
        c[:R, :W], ta[:R, :W], 1.5625, c[:R, :W], ALU.is_gt, ALU.add))
    p.sync_run("vector", lambda v: (
        v.scalar_tensor_tensor(c[:R, :W], ta[:R, :W], 3.0625, c[:R, :W],
                               ALU.is_gt, ALU.add),
        v.tensor_tensor(e_t[:R, :W], d[:R, :W], e_t[:R, :W], ALU.add))[-1])
    p.sync_run("vector", lambda v: v.scalar_tensor_tensor(
        c[:R, :W], c[:R, :W], 0.5, e_t[:R, :W], ALU.mult, ALU.add))

    def expand_scl(v):
        sv = invb[:R, :nb].unsqueeze(2).broadcast_to([R, nb, 16])
        return v.tensor_copy(bx[:R, :W].rearrange("p (a b) -> p a b", b=16), sv)
    p.sync_run("vector", expand_scl)
    p.wait("vector", "q_sg", sg_tgt)
    p.sync_run("vector", lambda v: v.tensor_tensor(c[:R, :W], c[:R, :W], sg[:R, :W],
                                                   ALU.mult))
    return p.run("vector", lambda v: v.tensor_tensor(out_b, c[:R, :W], bx[:R, :W],
                                                     ALU.mult), inc=done_sem)


def make_fp4_scratch(p, maxw=DI):
    return dict(
        bm=p.sbuf("q_bm", [128, maxw // 16], F32),
        scl=p.sbuf("q_scl", [128, maxw // 16], F32),
        invb=p.sbuf("q_invb", [128, maxw // 16], BF16),
        t=p.sbuf("q_t", [128, maxw], BF16),
        ta=p.sbuf("q_ta", [128, maxw], BF16),
        c=p.sbuf("q_c", [128, maxw], BF16),
        d=p.sbuf("q_d", [128, maxw], BF16),
        e=p.sbuf("q_e", [128, maxw], BF16),
        sg=p.sbuf("q_sg", [128, maxw], BF16),
        bx=p.sbuf("q_bx", [128, maxw], BF16),
    )


def build_a1():
    nc = bass.Bass("TRN2", target_bir_lowering=False)
    xpad = nc.dram_tensor("xpad", [TT, D], F32, kind="ExternalInput").ap()
    maskcol = nc.dram_tensor("maskcol", [128, NTT], F32, kind="ExternalInput").ap()
    wg_sh = nc.dram_tensor("wg_sh", [WQ_SH, D], F32, kind="ExternalInput").ap()
    wu_sh = nc.dram_tensor("wu_sh", [WQ_SH, D], F32, kind="ExternalInput").ap()
    wd_sh = nc.dram_tensor("wd_sh", [128, DI], F32, kind="ExternalInput").ap()
    posb = nc.dram_tensor("posb", [44, H], F32, kind="ExternalInput").ap()
    indic = nc.dram_tensor("indic", [44, 24], BF16, kind="ExternalInput").ap()

    xnb_d = nc.dram_tensor("xnb", [TT, D], BF16, kind="ExternalOutput").ap()
    xq_d = nc.dram_tensor("xq", [T, D], BF16, kind="ExternalOutput").ap()
    wqg_d = nc.dram_tensor("wqg", [WQ_SH, D], BF16, kind="ExternalOutput").ap()
    wqu_d = nc.dram_tensor("wqu", [WQ_SH, D], BF16, kind="ExternalOutput").ap()
    wqd_d = nc.dram_tensor("wqd", [128, DI], BF16, kind="ExternalOutput").ap()
    wdrev_d = nc.dram_tensor("wdrev", [H, 384], BF16, kind="ExternalOutput").ap()
    lnw8_d = nc.dram_tensor("lnw8", [H, NLO], F32, kind="ExternalOutput").ap()

    p = Prog(nc)
    xs = [p.sbuf(f"xs{i}", [128, D], F32) for i in range(2)]
    mcol = p.sbuf("mcol", [128, NTT], F32)
    stat = p.sbuf("stat", [128, 8], F32)
    epsc = p.sbuf("epsc", [128, 1], F32)
    sq = p.sbuf("sq", [128, D], F32)
    xnb = [p.sbuf(f"xnb{i}", [128, D], BF16) for i in range(2)]
    xqb = p.sbuf("xqb", [128, D], BF16)
    scr = make_fp4_scratch(p)
    wst = p.sbuf("wst", [128, DI], BF16)
    wqb = p.sbuf("wqb", [128, DI], BF16)
    pb = p.sbuf("pb", [44, H], F32)
    pbe = p.sbuf("pbe", [44, H], BF16)
    ind_s = p.sbuf("ind_s", [44, 24], BF16)
    ws16 = p.sbuf("ws16", [16, 24], F32)
    lnw = p.sbuf("lnw", [16, 24], F32)
    wdrev = p.sbuf("wdrev_s", [16, 384], BF16)
    pspb = p.psum("pspb", [16, 24], F32)

    mtot = p.dma("sync", mcol[:], maskcol, "ld_m")
    ep = p.run("vector", lambda v: v.memset(epsc[:], 1e-5), inc="v_eps")
    p.wait("scalar", "v_eps", ep)

    for ti in range(NTT):
        buf = ti % 2
        own = ti >= HALO // 128
        w = ((f"st_xnb{buf}", p.val(f"st_xnb{buf}")) if ti >= 2 else None)
        ldv = p.dma("sync", xs[buf][:], xpad[ti * 128:(ti + 1) * 128, :],
                    f"ld_x{buf}", wait=w)
        p.wait("vector", f"ld_x{buf}", ldv)
        if ti == 0:
            p.wait("vector", "ld_m", mtot)

        p.sync_run("vector", lambda v, x=xs[buf]: v.reduce_sum(
            stat[:, 0:1], x[:], axis=AX.X))
        p.sync_run("vector", lambda v: v.tensor_scalar_mul(
            stat[:, 0:1], stat[:, 0:1], 1.0 / D))
        ct = p.run("vector", lambda v, x=xs[buf]: v.tensor_scalar(
            x[:], x[:], stat[:, 0:1], None, ALU.subtract), inc="v_ln")

        p.wait("scalar", "v_ln", ct)
        p.sync_run("scalar", lambda s: s.memzero(stat[:, 2:3]))
        p.sync_run("scalar", lambda s, x=xs[buf]: s.activation(
            sq[:], x[:], AF.Square, accum_out=stat[:, 2:3]))
        if p.val("sr_vector"):
            p.wait("scalar", "sr_vector", p.val("sr_vector"))
        rt = p.run("scalar", lambda s: s.activation(
            stat[:, 4:5], stat[:, 2:3], AF.Sqrt, scale=1.0 / D, bias=epsc[:, 0:1]),
            inc="s_rs")

        p.wait("vector", "s_rs", rt)
        p.sync_run("vector", lambda v: v.reciprocal(stat[:, 5:6], stat[:, 4:5]))

        def norm(v, x=xs[buf], xo=xnb[buf], ti=ti):
            return v.tensor_scalar(xo[:], x[:], stat[:, 5:6], mcol[:, ti:ti + 1],
                                   ALU.mult, ALU.mult)
        nt_ = p.run("vector", norm, inc="v_nm")
        p.dma("gpsimd", xnb_d[ti * 128:(ti + 1) * 128, :], xnb[buf][:],
              f"st_xnb{buf}", wait=("v_nm", nt_))
        if own:
            oi = ti - HALO // 128
            if oi > 0:
                p.wait("vector", "st_xq", 16 * oi)
            qd = emit_fp4_quant(p, xnb[buf][:], xqb[:], scr, 128, D, "q_done")
            p.dma("gpsimd", xq_d[oi * 128:(oi + 1) * 128, :], xqb[:], "st_xq",
                  wait=("q_done", qd))

    wjobs = [(wg_sh, wqg_d, WQ_SH, D), (wu_sh, wqu_d, WQ_SH, D), (wd_sh, wqd_d, 128, DI)]
    nst = 0
    for wi, (src, dst, rows, W) in enumerate(wjobs):
        r0 = 0
        while r0 < rows:
            r = min(128, rows - r0)
            wt = ("st_wq", 16 * nst) if nst > 0 else None
            ldw = p.dma("gpsimd", wst[0:r, 0:W], src[r0:r0 + r, :], "ld_w", wait=wt)
            p.wait("vector", "ld_w", ldw)
            if nst > 0:
                p.wait("vector", "st_wq", 16 * nst)
            qd = emit_fp4_quant(p, wst[0:r, 0:W], wqb[0:r, 0:W], scr, r, W, "q_done")
            p.dma("gpsimd", dst[r0:r0 + r, :], wqb[0:r, 0:W], "st_wq",
                  wait=("q_done", qd))
            nst += 1
            r0 += r

    p.dma("sync", pb[:], posb, "ld_pb")
    pbt = p.dma("sync", ind_s[:], indic, "ld_pb")
    p.wait("scalar", "ld_pb", pbt)
    pe = p.run("scalar", lambda s: s.activation(pbe[:], pb[:], AF.Exp), inc="s_pb")
    p.wait("tensor", "s_pb", pe)
    mmw = p.run("tensor", lambda t: t.matmul(pspb[:], pbe[:], ind_s[:],
                                             start=True, stop=True), inc="t_pb")
    p.wait("vector", "t_pb", mmw)

    p.sync_run("vector", lambda v: v.tensor_copy(ws16[:], pspb[:]))
    p.sync_run("vector", lambda v: v.memset(wdrev[:], 0.0))

    def wsum(v):
        last = None
        for gi, off in enumerate(LOCAL_OFFS):
            last = v.tensor_copy(wdrev[:, 128 + off:129 + off], ws16[:, gi:gi + 1])
        return last
    vw = p.run("vector", wsum, inc="v_pb")
    p.wait("scalar", "v_pb", vw)
    lw = p.run("scalar", lambda s: s.activation(lnw[:, 0:24], ws16[:, 0:24], AF.Ln),
               inc="s_pb")
    p.dma("gpsimd", wdrev_d, wdrev[:], "st_pb", wait=("v_pb", vw))
    p.dma("gpsimd", lnw8_d, lnw[:, len(LOCAL_OFFS):24], "st_pb", wait=("s_pb", lw))

    for sname in ["st_xnb0", "st_xnb1", "st_xq", "st_wq", "st_pb"]:
        if p.val(sname):
            p.wait("sync", sname, p.val(sname))
    return p.finalize()


def build_a2():
    nc = bass.Bass("TRN2", target_bir_lowering=False)
    xnb_d = nc.dram_tensor("xnb", [TT, D], BF16, kind="ExternalInput").ap()
    wqkv_d = nc.dram_tensor("w_qkv", [D, 3 * D], F32, kind="ExternalInput").ap()
    wgate_d = nc.dram_tensor("w_gate", [D, D], F32, kind="ExternalInput").ap()
    bq_r = nc.dram_tensor("bq_r", [1, D], F32, kind="ExternalInput").ap()
    bk_r = nc.dram_tensor("bk_r", [1, D], F32, kind="ExternalInput").ap()
    bv_r = nc.dram_tensor("bv_r", [1, D], F32, kind="ExternalInput").ap()
    bg_r = nc.dram_tensor("bg_r", [1, D], F32, kind="ExternalInput").ap()
    vrow = nc.dram_tensor("validrow", [1, TT], F32, kind="ExternalInput").ap()

    q_d = nc.dram_tensor("q_fm", [D, T], BF16, kind="ExternalOutput").ap()
    k_d = nc.dram_tensor("k_fm", [D, TT], BF16, kind="ExternalOutput").ap()
    v_d = nc.dram_tensor("v_tm", [TT, D], BF16, kind="ExternalOutput").ap()
    g_d = nc.dram_tensor("gate_fm", [D, T], BF16, kind="ExternalOutput").ap()

    p = Prog(nc)
    wstage = p.sbuf("wstage", [128, 3 * D], F32)
    wqkv = p.sbuf("wqkv", [128, 8 * 3 * D], BF16)
    wgate = p.sbuf("wgate", [128, 8 * D], BF16)
    brow = p.sbuf("brow", [1, 4 * D], F32)
    vr = p.sbuf("vr", [1, TT], F32)
    xt = [[p.sbuf(f"xt{b}_{kc}", [128, 512], BF16) for kc in range(8)]
          for b in range(2)]
    ev = [p.sbuf(f"ev{i}", [128, 512], BF16) for i in range(4)]
    ps = [p.psum(f"ps{i}", [128, 512], F32) for i in range(4)]

    ncast = 0
    for kc in range(8):
        wt = ("cast_w", ncast) if ncast else None
        lw = p.dma("sync", wstage[:, 0:3 * D], wqkv_d[kc * 128:(kc + 1) * 128, :],
                   "ld_w", wait=wt)
        p.wait("vector", "ld_w", lw)
        p.run("vector", lambda v, kc=kc: v.tensor_copy(
            wqkv[:, kc * 3 * D:(kc + 1) * 3 * D], wstage[:, 0:3 * D]), inc="cast_w")
        ncast += 1
    for kc in range(8):
        lw = p.dma("sync", wstage[:, 0:D], wgate_d[kc * 128:(kc + 1) * 128, :],
                   "ld_w", wait=("cast_w", ncast))
        p.wait("vector", "ld_w", lw)
        p.run("vector", lambda v, kc=kc: v.tensor_copy(
            wgate[:, kc * D:(kc + 1) * D], wstage[:, 0:D]), inc="cast_w")
        ncast += 1
    wtot = p.val("cast_w")

    for i, dr in enumerate([bq_r, bk_r, bv_r, bg_r]):
        p.dma("sync", brow[0:1, i * D:(i + 1) * D], dr, "ld_b")
    p.dma("sync", vr[:], vrow, "ld_b")
    btot = p.val("ld_b")

    ev_tgt = [0, 0, 0, 0]
    mm_hist = []
    pslot = [0]

    def do_mm(mms_fn, dst_ap, func, sl=None):
        sl = pslot[0] % 4 if sl is None else sl
        pslot[0] += 1
        nev = p.val("ev_" + str(sl))
        if nev:
            p.wait("tensor", "ev_" + str(sl), nev)
        mt = p.run("tensor", lambda t: mms_fn(t, sl), inc="mm_done")
        p.wait("scalar", "mm_done", mt)
        if ev_tgt[sl]:
            p.wait("scalar", f"st_o{sl}", ev_tgt[sl])
        et = p.run("scalar", lambda s, sl=sl: s.activation(ev[sl][:], ps[sl][:], func),
                   inc="ev_" + str(sl))
        ev_tgt[sl] = p.dma("gpsimd", dst_ap, ev[sl][:], f"st_o{sl}",
                           wait=("ev_" + str(sl), et))

    for g in range(NG):
        buf = g % 2
        own = g >= HALO // 512
        tok0 = g * 512
        ot0 = tok0 - HALO
        for kc in range(8):
            wt = None
            if g >= 2 and kc == 0:
                wt = ("mm_done", mm_hist[g - 2])
            p.dma("scalar", xt[buf][kc][:],
                  xnb_d[tok0:tok0 + 512, kc * 128:(kc + 1) * 128], "ld_xt",
                  wait=wt, transpose=True)
        xtot = p.val("ld_xt")
        p.wait("tensor", "ld_xt", xtot)
        if g == 0:
            p.wait("tensor", "ld_b", btot)
            p.wait("tensor", "cast_w", wtot)

        vrs = vr[0:1, tok0:tok0 + 512]

        def fm_set(wbase, wsb, brow_off, n_out, dst_of, func):
            for oc in range(n_out):
                def mms(t, sl, oc=oc, wbase=wbase, wsb=wsb, brow_off=brow_off,
                        buf=buf, vrs=vrs):
                    for kc in range(8):
                        t.matmul(ps[sl][:],
                                 wsb[:, kc * wbase[1] + wbase[0] + oc * 128:
                                     kc * wbase[1] + wbase[0] + (oc + 1) * 128],
                                 xt[buf][kc][:], start=(kc == 0), stop=False)
                    return t.matmul(ps[sl][:],
                                    brow[0:1, brow_off + oc * 128:
                                         brow_off + (oc + 1) * 128],
                                    vrs, start=False, stop=True)
                do_mm(mms, dst_of(oc), func)

        if own:
            fm_set((0, 3 * D), wqkv, 0 * D, 8,
                   lambda oc: q_d[oc * 128:(oc + 1) * 128, ot0:ot0 + 512], AF.Copy)
        fm_set((D, 3 * D), wqkv, 1 * D, 8,
               lambda oc: k_d[oc * 128:(oc + 1) * 128, tok0:tok0 + 512], AF.Copy)
        if own:
            fm_set((0, D), wgate, 3 * D, 8,
                   lambda oc: g_d[oc * 128:(oc + 1) * 128, ot0:ot0 + 512], AF.Sigmoid)
        for tl in range(4):
            trow = tok0 + tl * 128
            for nch in range(2):
                def mmv(t, sl, tl=tl, nch=nch, buf=buf, trow=trow):
                    for kc in range(8):
                        t.matmul(ps[sl][:], xt[buf][kc][:, tl * 128:(tl + 1) * 128],
                                 wqkv[:, kc * 3 * D + 2 * D + nch * 512:
                                      kc * 3 * D + 2 * D + (nch + 1) * 512],
                                 start=(kc == 0), stop=False)
                    return t.matmul(ps[sl][:], vr[0:1, trow:trow + 128],
                                    brow[0:1, 2 * D + nch * 512:2 * D + (nch + 1) * 512],
                                    start=False, stop=True)
                do_mm(mmv, v_d[trow:trow + 128, nch * 512:(nch + 1) * 512], AF.Copy)
        mm_hist.append(p.val("mm_done"))

    for sl in range(4):
        if p.val(f"st_o{sl}"):
            p.wait("sync", f"st_o{sl}", p.val(f"st_o{sl}"))
    return p.finalize()



RING_V = 26

def build_a3(stage='all', nobias=False):
    nc = bass.Bass("TRN2", target_bir_lowering=False)
    q_dr = nc.dram_tensor("q_fm", [D, T], BF16, kind="ExternalInput").ap()
    k_dr = nc.dram_tensor("k_fm", [D, TT], BF16, kind="ExternalInput").ap()
    v_dr = nc.dram_tensor("v_tm", [TT, D], BF16, kind="ExternalInput").ap()
    wmask_dr = nc.dram_tensor("wmask_full", [128, 2 * 16 * 128], BF16,
                              kind="ExternalInput").ap()
    lnw8_dr = nc.dram_tensor("lnw8", [H, NLO], F32, kind="ExternalInput").ap()
    onehot_dr = nc.dram_tensor("onehot", [128, 16 * 16], BF16, kind="ExternalInput").ap()
    indc_dr = nc.dram_tensor("indc", [128, 8 * 16], BF16, kind="ExternalInput").ap()
    indzl_dr = nc.dram_tensor("indzl", [128, 16], F32, kind="ExternalInput").ap()
    identf_dr = nc.dram_tensor("identf", [128, 128], F32, kind="ExternalInput").ap()

    glo_dr = nc.dram_tensor("glo_fm", [D, T], BF16, kind="ExternalOutput").ap()
    olg_dr = nc.dram_tensor("olg_tm", [T, D], BF16, kind="ExternalOutput").ap()
    z_dr = nc.dram_tensor("z_all", [H, T], F32, kind="ExternalOutput").ap()

    p = Prog(nc)
    HT = HALO // 128
    a_sb2 = p.sbuf("a_sb2", [128, NW * T], F32)
    lnwm4 = p.sbuf("lnwm4", [128, NLO], F32)
    ebias = p.sbuf("ebias", [128, 1], F32)
    oneh = p.sbuf("oneh", [128, 16 * 16], BF16)
    indc = p.sbuf("indc_s", [128, 8 * 16], BF16)
    indzl = p.sbuf("indzl_s", [128, 16], F32)
    identf = p.sbuf("identf_s", [128, 128], F32)
    wmask = p.sbuf("wmask", [128, 2 * 16 * 128], BF16)
    zl_sb = p.sbuf("zl_sb", [16, T], F32)
    zrow_sb = p.sbuf("zrow_sb", [16, 128], F32)
    kes = contextlib.ExitStack()
    k_sb = [p.sbuf(f"k{c}", [128, TT], BF16, es=kes) for c in range(8)]
    qd = [[p.sbuf(f"qd{b}_{c}", [128, 512], BF16, es=kes) for c in range(8)]
          for b in range(2)]
    prod = [p.sbuf(f"prod{b}", [128, 512], BF16, es=kes) for b in range(2)]

    ps_sc = [p.psum(f"psc{i}", [128, 512], F32) for i in range(2)]
    ps_s = [p.psum(f"pss{i}", [128, 512], F32) for i in range(2)]
    ps_o = [p.psum(f"pso{i}", [128, 512], F32) for i in range(2)]
    ps_z = p.psum("psz", [16, 512], F32)
    ps_tp = p.psum("pstp", [128, 512], F32)

    import os as _os
    _skips = set(_os.environ.get("A3SKIP", "").split(","))
    if "k" not in _skips:
        for c in range(8):
            p.dma("sync", k_sb[c][:], k_dr[c * 128:(c + 1) * 128, :], "ld_k")
    if "c" not in _skips:
        p.dma("sync", oneh[:], onehot_dr, "ld_c")
        p.dma("sync", indc[:], indc_dr, "ld_c")
        p.dma("sync", indzl[:], indzl_dr, "ld_c")
        p.dma("sync", identf[:], identf_dr, "ld_c")
        p.dma("sync", wmask[:], wmask_dr, "ld_c")
    mz = p.run("vector", lambda v: v.memset(lnwm4[:], 0.0), inc="v_c")
    if "lnw" not in _skips:
        for qd_ in range(4):
            p.dma("gpsimd", lnwm4[32 * qd_:32 * qd_ + 16, 0:NLO], lnw8_dr, "ld_cg",
                  wait=(("v_c", mz) if qd_ == 0 else None))
    ctot = p.val("ld_c")
    p.wait("vector", "ld_c", ctot)
    p.wait("vector", "ld_cg", p.val("ld_cg"))
    cprep = p.run("vector", lambda v: (
        v.memset(ebias[:], ESHIFT),
        v.memset(a_sb2[:], 0.0),
        v.tensor_scalar(lnwm4[:], lnwm4[:], ESHIFT, None, ALU.add))[-1], inc="v_c")

    if stage == 'loads':
        for snm in ["ld_k", "ld_c", "ld_cg", "v_c"]:
            if p.val(snm):
                p.wait("sync", snm, p.val(snm))
        kes.close()
        return p.finalize()
    ktot = p.val("ld_k")
    p.wait("tensor", "ld_k", ktot)
    p.wait("vector", "ld_k", ktot)
    p.wait("tensor", "v_c", cprep)
    p.wait("scalar", "v_c", cprep)
    sc_hist = {}
    for w in range(NW):
        now = min(4, NLO - 4 * w)
        for n in range(4):
            bk = (w * 4 + n) % 2
            for c in range(8):
                wt = None
                if (w * 4 + n) >= 2 and c == 0:
                    wt = ("sc_mm", sc_hist[(w * 4 + n) - 2])
                p.dma("scalar", qd[bk][c][:],
                      q_dr[c * 128:(c + 1) * 128, n * 512:(n + 1) * 512],
                      f"ld_qd{bk}", wait=wt)
            qt = p.val(f"ld_qd{bk}")
            p.wait("vector", f"ld_qd{bk}", qt)
            p.wait("tensor", "sc_ev", p.val("sc_ev"))
            for oi in range(now):
                off = LARGE_OFFS[w * 4 + oi]
                for c in range(8):
                    pb = (oi * 8 + c) % 2
                    if p.val("sc_mm") >= 2:
                        p.wait("vector", "sc_mm", p.val("sc_mm") - 1)
                    def pr(v, c=c, off=off, n=n, bk=bk, pb=pb):
                        return v.tensor_tensor(
                            prod[pb][:], qd[bk][c][:],
                            k_sb[c][:, HALO - off + n * 512: HALO - off + (n + 1) * 512],
                            ALU.mult)
                    pt = p.run("vector", pr, inc="pr_v")
                    p.wait("tensor", "pr_v", pt)

                    def mm(t, c=c, oi=oi, pb=pb):
                        return t.matmul(ps_sc[0][32 * oi:32 * oi + 16, :],
                                        indc[:, c * 16:(c + 1) * 16], prod[pb][:],
                                        start=(c == 0), stop=(c == 7),
                                        tile_position=(0, 32 * oi))
                    p.run("tensor", mm, inc="sc_mm")
            sc_hist_mark = True
            sc_hist[w * 4 + n] = p.val("sc_mm")
            if stage == 'pre0':
                p.wait("sync", "sc_mm", p.val("sc_mm"))
                kes.close()
                return p.finalize()
            p.wait("scalar", "sc_mm", p.val("sc_mm"))
            for oi in range(now):
                def ex(s, w=w, n=n, oi=oi):
                    kw = {}
                    if not nobias:
                        kw["bias"] = lnwm4[32 * oi:32 * oi + 16,
                                           w * 4 + oi:w * 4 + oi + 1]
                    return s.activation(
                        a_sb2[32 * oi:32 * oi + 16,
                              w * T + n * 512:w * T + (n + 1) * 512],
                        ps_sc[0][32 * oi:32 * oi + 16, :], AF.Exp,
                        scale=ASCALE, **kw)
                p.run("scalar", ex, inc="sc_ev")

    if stage == 'pre':
        p.wait("sync", "sc_ev", p.val("sc_ev"))
        kes.close()
        return p.finalize()
    p.wait("tensor", "sc_ev", p.val("sc_ev"))
    for n in range(4):
        if p.val("zl_ev"):
            p.wait("tensor", "zl_ev", p.val("zl_ev"))
        def zmm(t, n=n):
            last = None
            for w in range(NW):
                last = t.matmul(ps_sc[1][0:16, :], indzl[:, 0:16],
                                a_sb2[:, w * T + n * 512:w * T + (n + 1) * 512],
                                start=(w == 0), stop=(w == NW - 1))
            return last
        zt = p.run("tensor", zmm, inc="zl_mm")
        p.wait("vector", "zl_mm", zt)
        p.run("vector", lambda v, n=n: v.tensor_copy(
            zl_sb[:, n * 512:(n + 1) * 512], ps_sc[1][0:16, :]), inc="zl_ev")

    if stage == 'zl':
        p.wait("sync", "zl_ev", p.val("zl_ev"))
        kes.close()
        return p.finalize()
    kes.close()
    q_sb = [p.sbuf(f"q{c}", [128, T], BF16) for c in range(8)]
    v_ring = p.sbuf("v_ring", [128, RING_V * D], BF16)
    kr = [p.sbuf(f"kr{c}", [128, 3 * 128], BF16) for c in range(8)]
    A_sb = p.sbuf("A_sb", [128, 2 * 2048], BF16)
    a_cmp = [p.sbuf(f"acmp{o}", [128, 16], BF16) for o in range(NLO)]
    stripes = p.sbuf("stripes", [128, NLO * D], BF16)
    og_tm = p.sbuf("og_tm", [128, D], BF16)
    v64_ring = p.sbuf("v64_ring", [128, 3 * D], BF16)
    glo_sb = [p.sbuf(f"glo{c}", [128, T], BF16) for c in range(8)]
    sc_fin = p.val("sc_mm")
    p.wait("sync", "sc_mm", sc_fin)
    p.wait("gpsimd", "sc_mm", sc_fin)
    for c in range(8):
        p.dma("sync", q_sb[c][:], q_dr[c * 128:(c + 1) * 128, :], "ld_q")
    for s in range(HT):
        p.dma("gpsimd", v_ring[:, (s % RING_V) * D:(s % RING_V + 1) * D],
              v_dr[s * 128:(s + 1) * 128, :], "ld_v_pre")
    for c in range(8):
        p.dma("sync", kr[c][:, (23 % 3) * 128:(23 % 3 + 1) * 128],
              k_dr[c * 128:(c + 1) * 128, 23 * 128:24 * 128], "ld_kr_pre")
    qtot = p.val("ld_q")
    p.wait("tensor", "ld_q", qtot)
    p.wait("tensor", "ld_v_pre", p.val("ld_v_pre"))
    p.wait("tensor", "ld_kr_pre", p.val("ld_kr_pre"))
    p.wait("tensor", "zl_ev", p.val("zl_ev"))

    tile_done = []
    for i in range(NT):
        it = HT + i
        wt = ("tile", tile_done[i - 2]) if i >= 2 else None
        p.dma("gpsimd", v_ring[:, (it % RING_V) * D:(it % RING_V + 1) * D],
              v_dr[it * 128:(it + 1) * 128, :], f"ld_v{i % 3}", wait=wt)
        p.dma("gpsimd", v64_ring[:, (i % 3) * D:(i % 3 + 1) * D],
              v_dr[it * 128 - 192:it * 128 - 64, :], f"ld_v{i % 3}")
        for c in range(8):
            p.dma("sync", kr[c][:, (it % 3) * 128:(it % 3 + 1) * 128],
                  k_dr[c * 128:(c + 1) * 128, it * 128:(it + 1) * 128],
                  f"ld_kr{i % 3}", wait=wt if c == 0 else None)
        p.wait("tensor", f"ld_v{i % 3}", p.val(f"ld_v{i % 3}"))
        p.wait("tensor", f"ld_kr{i % 3}", p.val(f"ld_kr{i % 3}"))

        def apos(h, cti):
            hg, j = divmod(h, 4)
            return (j % 2) * 2048 + hg * 512 + cti * 256 + (j // 2) * 128

        first_z = True
        for hg in range(4 if "locmm" not in _skips else 0):
            for par in range(2):
                p.wait("tensor", "s_ev", p.val("s_ev"))

                def smm(t, hg=hg, par=par, it=it, i=i):
                    last = None
                    pb = 64 * par
                    for cti, ct in enumerate([it - 1, it]):
                        for jj in range(2):
                            h = hg * 4 + 2 * jj + par
                            c = h // 2
                            last = t.matmul(
                                ps_s[par][:, cti * 256 + jj * 128:
                                          cti * 256 + (jj + 1) * 128],
                                kr[c][pb:pb + 64, (ct % 3) * 128:(ct % 3 + 1) * 128],
                                q_sb[c][pb:pb + 64, i * 128:(i + 1) * 128],
                                start=True, stop=True)
                    return last
                st = p.run("tensor", smm, inc="s_mm")
                if "lexp" in _skips:
                    continue
                p.wait("scalar", "s_mm", st)

                def ex(s, hg=hg, par=par):
                    return s.activation(
                        A_sb[:, par * 2048 + hg * 512: par * 2048 + (hg + 1) * 512],
                        ps_s[par][:], AF.Exp, scale=ASCALE, bias=ebias[:, 0:1])
                et = p.run("scalar", ex, inc="s_ev")
                p.wait("vector", "s_ev", et)

                def msk(v, hg=hg, par=par):
                    abase = par * 2048 + hg * 512
                    return v.tensor_tensor(A_sb[:, abase:abase + 512],
                                           A_sb[:, abase:abase + 512],
                                           wmask[:, abase:abase + 512], ALU.mult)
                mt = p.run("vector", msk, inc="s_msk")
                p.wait("tensor", "s_msk", mt)

            if "lvz" in _skips or "lexp" in _skips:
                continue

            def vmm(t, hg=hg, it=it):
                last = None
                for j in range(4):
                    h = hg * 4 + j
                    bank = h // 8
                    pb = 64 * (h % 2)
                    fs = ((h // 2) % 4) * 128
                    for cti, ct in enumerate([it - 1, it]):
                        kwargs = dict(start=(cti == 0), stop=(cti == 1))
                        if pb:
                            kwargs["tile_position"] = (0, pb)
                        last = t.matmul(
                            ps_o[bank][pb:pb + 64, fs:fs + 128],
                            v_ring[:, (ct % RING_V) * D + h * 64:
                                   (ct % RING_V) * D + (h + 1) * 64],
                            A_sb[:, apos(h, cti):apos(h, cti) + 128],
                            **kwargs)
                return last
            p.run("tensor", vmm, inc="v_mm")

            def zmm2(t, hg=hg, it=it, first_z=first_z):
                last = None
                for j in range(4):
                    h = hg * 4 + j
                    for cti in range(2):
                        last = t.matmul(
                            ps_z[0:16, 0:128], oneh[:, h * 16:(h + 1) * 16],
                            A_sb[:, apos(h, cti):apos(h, cti) + 128],
                            start=(first_z and j == 0 and cti == 0),
                            stop=(hg == 3 and j == 3 and cti == 1))
                return last
            p.run("tensor", zmm2, inc="z_mm")
            first_z = False

        skip_large = "large" in _skips
        def tpm(t, i=i):
            last = None
            for w in range(NW):
                last = t.transpose(ps_tp[:, w * 128:(w + 1) * 128],
                                   a_sb2[:, w * T + i * 128:w * T + (i + 1) * 128],
                                   identf[:])
            return last
        if skip_large:
            p.run("vector", lambda v: v.memset(og_tm[:], 0.0), inc="olg_v")
        p.wait("tensor", "a_ev", p.val("a_ev")) if not skip_large else None
        tpt = p.run("tensor", tpm, inc="tp_mm") if not skip_large else None
        p.wait("vector", "tp_mm", tpt) if not skip_large else None

        aevt = [p.inc("a_ev") for _ in range(NLO)] if not skip_large else []
        asem = p.sems["a_ev"] if not skip_large else None

        def aev(v, asem=asem):
            for o in range(NLO):
                w, oi = divmod(o, 4)
                v.tensor_copy(a_cmp[o][:],
                              ps_tp[:, w * 128 + 32 * oi:w * 128 + 32 * oi + 16]) \
                    .then_inc(asem, 1)
        if not skip_large:
            p.emit("vector", aev)
            p.wait("vector", "a_ev", aevt[-1])

        def vlg(v, it=it):
            last = None
            for o, off in enumerate(LARGE_OFFS):
                if off % 128 == 64:
                    s64 = (it - HALO // 128) % 3
                    av = a_cmp[o][:].unsqueeze(2).broadcast_to([128, 16, 64])
                    last = v.tensor_tensor(stripes[:, o * D:(o + 1) * D]
                                           .rearrange("p (a b) -> p a b", b=64),
                                           v64_ring[:, s64 * D:(s64 + 1) * D]
                                           .rearrange("p (a b) -> p a b", b=64),
                                           av, ALU.mult)
                    continue
                dl = off // 128
                slot = (it - dl) % RING_V
                av = a_cmp[o][:].unsqueeze(2).broadcast_to([128, 16, 64])
                last = v.tensor_tensor(stripes[:, o * D:(o + 1) * D]
                                       .rearrange("p (a b) -> p a b", b=64),
                                       v_ring[:, slot * D:(slot + 1) * D]
                                       .rearrange("p (a b) -> p a b", b=64),
                                       av, ALU.mult)
            return last
        if skip_large:
            pass
        else:
            p.sync_run("vector", vlg)
        if not skip_large:
            p.sync_run("vector", lambda v: v.tensor_tensor(
                stripes[:, 0:4 * D], stripes[:, 0:4 * D], stripes[:, 4 * D:8 * D],
                ALU.add))
            p.sync_run("vector", lambda v: v.tensor_tensor(
                stripes[:, 0:2 * D], stripes[:, 0:2 * D], stripes[:, 2 * D:4 * D],
                ALU.add))
            p.sync_run("vector", lambda v: v.tensor_tensor(
                stripes[:, 0:D], stripes[:, 0:D], stripes[:, D:2 * D], ALU.add))
            lt = p.run("vector", lambda v: v.tensor_tensor(
                og_tm[:], stripes[:, 0:D], stripes[:, 8 * D:9 * D], ALU.add),
                inc="olg_v")
        else:
            lt = p.val("olg_v")
        p.dma("gpsimd", olg_dr[i * 128:(i + 1) * 128, :], og_tm[:], "st_olg",
              wait=("olg_v", lt))
        p.wait("vector", "st_olg", p.val("st_olg"))

        if p.val("v_mm"):
            p.wait("vector", "v_mm", p.val("v_mm"))
        if p.val("z_mm"):
            p.wait("vector", "z_mm", p.val("z_mm"))

        def oev(v, i=i):
            if p.val("v_mm"):
                for bank in range(2):
                    for j in range(4):
                        c = bank * 4 + j
                        v.tensor_copy(glo_sb[c][:, i * 128:(i + 1) * 128],
                                      ps_o[bank][:, j * 128:(j + 1) * 128])
            if p.val("z_mm"):
                return v.tensor_copy(zrow_sb[:], ps_z[0:16, 0:128])
            return v.tensor_copy(zrow_sb[:], zrow_sb[:])
        ot = p.run("vector", oev, inc="o_ev")
        p.wait("vector", "o_ev", ot)

        def zadd(v, i=i):
            return v.tensor_tensor(zl_sb[:, i * 128:(i + 1) * 128],
                                   zl_sb[:, i * 128:(i + 1) * 128],
                                   zrow_sb[:], ALU.add)
        p.run("vector", zadd, inc="tile")
        tile_done.append(p.val("tile"))
        p.wait("tensor", "tile", tile_done[-1])
        if stage == 'tile1' and i == 0:
            break
        if stage == 'tile4' and i == 3:
            break

    fin = p.val("tile")
    for c in range(8):
        p.dma("gpsimd", glo_dr[c * 128:(c + 1) * 128, :], glo_sb[c][:], "st_out",
              wait=(("tile", fin) if c == 0 else None))
    p.dma("gpsimd", z_dr, zl_sb[:], "st_out")
    p.wait("sync", "st_out", p.val("st_out"))
    p.wait("sync", "st_olg", p.val("st_olg"))
    return p.finalize()



NI = DI // 512 + (1 if DI % 512 else 0)

def build_a4():
    nc = bass.Bass("TRN2", target_bir_lowering=False)
    xq_dr = nc.dram_tensor("xq_tm", [T, D], BF16, kind="ExternalInput").ap()
    wqg_dr = nc.dram_tensor("wqgT", [D, DI], BF16, kind="ExternalInput").ap()
    wqu_dr = nc.dram_tensor("wquT", [D, DI], BF16, kind="ExternalInput").ap()
    wqd_dr = nc.dram_tensor("wqdT", [DI, D], BF16, kind="ExternalInput").ap()
    g0_dr = nc.dram_tensor("g0_fm", [D, T], BF16, kind="ExternalOutput").ap()
    hq_dr = nc.dram_tensor("hq_tm", [T, DI], BF16, kind="Internal").ap()

    p = Prog(nc)
    xt = [p.sbuf(f"xqf{c}", [128, T], BF16) for c in range(8)]
    scr = make_fp4_scratch(p)
    h_sb = [p.sbuf(f"h{b}", [128, DI], BF16) for b in range(2)]
    sgt = p.sbuf("sgt", [128, 512], BF16)
    hq_sb = [p.sbuf(f"hq{b}", [128, DI], BF16) for b in range(2)]
    ps = [p.psum(f"ps{i}", [128, 512], F32) for i in range(4)]

    kes = contextlib.ExitStack()
    wg = [p.sbuf(f"wg{c}", [128, DI], BF16, es=kes) for c in range(8)]
    wu = [p.sbuf(f"wu{c}", [128, DI], BF16, es=kes) for c in range(8)]

    for c in range(8):
        p.dma("scalar", wg[c][:], wqg_dr[c * 128:(c + 1) * 128, :], "ld_w")
        p.dma("scalar", wu[c][:], wqu_dr[c * 128:(c + 1) * 128, :], "ld_w")
    for c in range(8):
        p.dma("sync", xt[c][:], xq_dr[:, c * 128:(c + 1) * 128], "ld_x",
              transpose=True)
    p.wait("tensor", "ld_w", p.val("ld_w"))
    p.wait("tensor", "ld_x", p.val("ld_x"))


    for ti in range(NT):
        buf = ti % 2
        if ti >= 2:
            p.wait("scalar", f"st_hq{buf}", p.val(f"st_hq{buf}"))
            p.wait("vector", f"st_hq{buf}", p.val(f"st_hq{buf}"))
        for ic in range(NI):
            iw = min(512, DI - ic * 512)
            sl = (ti * NI + ic) % 2
            nev = p.val(f"ev4_{sl}")
            if nev:
                p.wait("tensor", f"ev4_{sl}", nev)

            def mmg(t, ti=ti, ic=ic, sl=sl, iw=iw):
                for c in range(8):
                    t.matmul(ps[sl][:, 0:iw], wg[c][:, ic * 512:ic * 512 + iw],
                             xt[c][:, ti * 128:(ti + 1) * 128].rearrange(
                                 "p m -> p m") if False else
                             xt[c][:, ti * 128:(ti + 1) * 128],
                             start=(c == 0), stop=(c == 7)) if False else None
                last = None
                for c in range(8):
                    last = t.matmul(ps[sl][:, 0:iw],
                                    xt[c][:, ti * 128:(ti + 1) * 128],
                                    wg[c][:, ic * 512:ic * 512 + iw],
                                    start=(c == 0), stop=(c == 7))
                for c in range(8):
                    last = t.matmul(ps[2 + sl][:, 0:iw],
                                    xt[c][:, ti * 128:(ti + 1) * 128],
                                    wu[c][:, ic * 512:ic * 512 + iw],
                                    start=(c == 0), stop=(c == 7))
                return last
            mt = p.run("tensor", mmg, inc="mm4")
            p.wait("scalar", "mm4", mt)
            for s2 in range(2):
                if p.val(f"ev4_{s2}"):
                    p.wait("scalar", f"ev4_{s2}", p.val(f"ev4_{s2}"))
            et = p.run("scalar", lambda s, sl=sl, iw=iw: s.activation(
                sgt[:, 0:iw], ps[sl][:, 0:iw], AF.Silu), inc="sl4")
            p.wait("vector", "sl4", et)
            ht = p.run("vector", lambda v, sl=sl, ic=ic, iw=iw, buf=buf:
                       v.tensor_tensor(h_sb[buf][:, ic * 512:ic * 512 + iw],
                                       sgt[:, 0:iw], ps[2 + sl][:, 0:iw],
                                       ALU.mult), inc=f"ev4_{sl}")
        for s2 in range(2):
            if p.val(f"ev4_{s2}"):
                p.wait("vector", f"ev4_{s2}", p.val(f"ev4_{s2}"))
        qd = emit_fp4_quant(p, h_sb[buf][:], hq_sb[buf][:], scr, 128, DI,
                            f"q4_{buf}")
        p.dma("gpsimd", hq_dr[ti * 128:(ti + 1) * 128, :], hq_sb[buf][:],
              f"st_hq{buf}", wait=(f"q4_{buf}", qd))

    for b2 in range(2):
        p.wait("sync", f"st_hq{b2}", p.val(f"st_hq{b2}"))
        p.wait("scalar", f"st_hq{b2}", p.val(f"st_hq{b2}"))
    p.wait("sync", "mm4", p.val("mm4"))
    p.wait("scalar", "mm4", p.val("mm4"))
    kes.close()
    NIC = (DI + 127) // 128
    wd = [p.sbuf(f"wd{icc}", [128, D], BF16) for icc in range(NIC)]
    ht = [[p.sbuf(f"ht{b}_{icc}", [128, 512], BF16) for icc in range(NIC)]
          for b in range(2)]
    for icc in range(NIC):
        p.dma("scalar", wd[icc][:],
              wqd_dr[icc * 128:(icc + 1) * 128, :], "ld_wd")
    p.wait("tensor", "ld_wd", p.val("ld_wd"))
    g0_sb = [p.sbuf(f"g0_sb{i}", [128, 512], BF16) for i in range(2)]

    mm_hist4 = []
    for g in range(4):
        buf = g % 2
        for icc in range(NIC):
            wt = None
            if g >= 2 and icc == 0:
                wt = ("mm4", mm_hist4[g - 2])
            elif icc >= 8:
                wt = (f"ld_h{buf}", 16 * (p.val(f"ld_h{buf}") // 16 - 7))
            p.dma("scalar", ht[buf][icc][:],
                  hq_dr[g * 512:(g + 1) * 512, icc * 128:(icc + 1) * 128],
                  f"ld_h{buf}", wait=wt, transpose=True)
        p.wait("tensor", f"ld_h{buf}", p.val(f"ld_h{buf}"))
        for oc in range(8):
            sl = oc % 2
            nev = p.val(f"ev4b_{sl}")
            if nev:
                p.wait("tensor", f"ev4b_{sl}", nev)

            def mmd(t, g=g, oc=oc, sl=sl, buf=buf):
                last = None
                for icc in range(NIC):
                    last = t.matmul(ps[sl][:],
                                    wd[icc][:, oc * 128:(oc + 1) * 128],
                                    ht[buf][icc][:],
                                    start=(icc == 0), stop=(icc == NIC - 1))
                return last
            mt = p.run("tensor", mmd, inc="mm4")
            p.wait("scalar", "mm4", mt)
            if p.val(f"st_g0_{sl}"):
                p.wait("scalar", f"st_g0_{sl}", p.val(f"st_g0_{sl}"))
            et = p.run("scalar", lambda s, sl=sl: s.activation(
                g0_sb[sl][:], ps[sl][:], AF.Silu), inc=f"ev4b_{sl}")
            p.dma("gpsimd", g0_dr[oc * 128:(oc + 1) * 128, g * 512:(g + 1) * 512],
                  g0_sb[sl][:], f"st_g0_{sl}", wait=(f"ev4b_{sl}", p.val(f"ev4b_{sl}")))
        mm_hist4.append(p.val("mm4"))

    for sl in range(2):
        p.wait("sync", f"st_g0_{sl}", p.val(f"st_g0_{sl}"))
    return p.finalize()


def build_a5():
    nc = bass.Bass("TRN2", target_bir_lowering=False)
    glo_dr = nc.dram_tensor("glo_fm", [D, T], BF16, kind="ExternalInput").ap()
    olg_dr = nc.dram_tensor("olg_tm", [T, D], BF16, kind="ExternalInput").ap()
    z_dr = nc.dram_tensor("z_all", [H, T], F32, kind="ExternalInput").ap()
    gate_dr = nc.dram_tensor("gate_fm", [D, T], BF16, kind="ExternalInput").ap()
    g0_dr = nc.dram_tensor("g0_fm", [D, T], BF16, kind="ExternalInput").ap()
    wout_dr = nc.dram_tensor("w_out", [D, D], F32, kind="ExternalInput").ap()
    bout_dr = nc.dram_tensor("b_out_r", [1, D], F32, kind="ExternalInput").ap()
    ones_dr = nc.dram_tensor("ones_r", [1, 512], F32, kind="ExternalInput").ap()
    indh_dr = nc.dram_tensor("indh", [16, 8 * 128], F32, kind="ExternalInput").ap()
    x_dr = nc.dram_tensor("x_own", [T, D], F32, kind="ExternalInput").ap()
    x1_dr = nc.dram_tensor("x1", [T, D], F32, kind="ExternalOutput").ap()
    sg_dr = nc.dram_tensor("sg_fm", [D, T], BF16, kind="Internal").ap()

    p = Prog(nc)
    glo = [p.sbuf(f"glo{c}", [128, T], BF16) for c in range(8)]
    olgT = [p.sbuf(f"olgT{c}", [128, T], BF16) for c in range(8)]
    gate = [p.sbuf(f"gate{c}", [128, T], BF16) for c in range(8)]
    g0 = [p.sbuf(f"g0_{c}", [128, T], BF16) for c in range(8)]
    zi = p.sbuf("zi", [16, T], F32)
    indh = p.sbuf("indh_s", [16, 8 * 128], F32)
    wstage = p.sbuf("wstage", [128, D], F32)
    wout = p.sbuf("wout", [128, 8 * D], BF16)
    bout = p.sbuf("bout", [1, D], F32)
    onesr = p.sbuf("onesr", [1, 512], F32)
    gn = p.sbuf("gn", [128, 512], BF16)
    sev = [p.sbuf(f"sev{i}", [128, 512], BF16) for i in range(2)]
    xs = [p.sbuf(f"xs{b}", [128, D], F32) for b in range(2)]
    sgt = [p.sbuf(f"sgt{b}", [128, D], BF16) for b in range(2)]
    x1b = [p.sbuf(f"x1b{b}", [128, D], F32) for b in range(2)]
    ps_b = p.psum("psb", [128, 512], F32)
    ps_w = [p.psum(f"psw{i}", [128, 512], F32) for i in range(2)]

    for c in range(8):
        p.dma("sync", glo[c][:], glo_dr[c * 128:(c + 1) * 128, :], "ld_a")
        p.dma("sync", gate[c][:], gate_dr[c * 128:(c + 1) * 128, :], "ld_a")
        p.dma("sync", g0[c][:], g0_dr[c * 128:(c + 1) * 128, :], "ld_a")
        p.dma("scalar", olgT[c][:], olg_dr[:, c * 128:(c + 1) * 128], "ld_aT",
              transpose=True)
    p.dma("sync", zi[:], z_dr, "ld_a")
    p.dma("sync", indh[:], indh_dr, "ld_a")
    p.dma("sync", bout[:], bout_dr, "ld_a")
    p.dma("sync", onesr[:], ones_dr, "ld_a")
    ncast = 0
    for kc in range(8):
        lw = p.dma("sync", wstage[:], wout_dr[kc * 128:(kc + 1) * 128, :], "ld_w",
                   wait=(("cast_w", ncast) if ncast else None))
        p.wait("vector", "ld_w", lw)
        p.run("vector", lambda v, kc=kc: v.tensor_copy(
            wout[:, kc * D:(kc + 1) * D], wstage[:]), inc="cast_w")
        ncast += 1
    p.wait("vector", "ld_a", p.val("ld_a"))
    p.wait("vector", "ld_aT", p.val("ld_aT"))
    p.sync_run("vector", lambda v: v.reciprocal(zi[:], zi[:]))
    vz = p.run("vector", lambda v: v.tensor_copy(zi[0:1, 0:1], zi[0:1, 0:1]),
               inc="v5")
    p.wait("tensor", "v5", vz)
    p.wait("tensor", "cast_w", p.val("cast_w"))

    for c in range(8):
        for n in range(4):
            def bz(t, c=c, n=n):
                return t.matmul(ps_b[:], indh[:, c * 128:(c + 1) * 128],
                                zi[:, n * 512:(n + 1) * 512], start=True, stop=True)
            bt = p.run("tensor", bz, inc="b_mm")
            p.wait("vector", "b_mm", bt)

            sl_ = slice(n * 512, (n + 1) * 512)
            p.sync_run("vector", lambda v, c=c, sl=sl_: v.tensor_tensor(
                gn[:], glo[c][:, sl], olgT[c][:, sl], ALU.add))
            p.sync_run("vector", lambda v: v.tensor_tensor(
                gn[:], gn[:], ps_b[:], ALU.mult))
            nt_ = p.run("vector", lambda v, c=c, sl=sl_: v.tensor_tensor(
                olgT[c][:, sl], gn[:], gate[c][:, sl], ALU.mult), inc="n_ev")
            p.wait("tensor", "n_ev", nt_)
            p.wait("vector", "n_ev", nt_)

    for n in range(4):
        for oc in range(8):
            sl = oc % 2
            if p.val(f"ev5_{sl}"):
                p.wait("tensor", f"ev5_{sl}", p.val(f"ev5_{sl}"))

            def mmo(t, n=n, oc=oc, sl=sl):
                for kc in range(8):
                    t.matmul(ps_w[sl][:], wout[:, kc * D + oc * 128:
                                               kc * D + (oc + 1) * 128],
                             olgT[kc][:, n * 512:(n + 1) * 512],
                             start=(kc == 0), stop=False)
                return t.matmul(ps_w[sl][:], bout[0:1, oc * 128:(oc + 1) * 128],
                                onesr[0:1, 0:512], start=False, stop=True)
            mt = p.run("tensor", mmo, inc="mm5")
            p.wait("vector", "mm5", mt)
            if p.val(f"st5_{sl}"):
                p.wait("vector", f"st5_{sl}", p.val(f"st5_{sl}"))

            p.sync_run("vector", lambda v, sl=sl: v.tensor_copy(sev[sl][:],
                                                                 ps_w[sl][:]))
            et = p.run("vector", lambda v, n=n, oc=oc, sl=sl: v.tensor_tensor(
                sev[sl][:], sev[sl][:], g0[oc][:, n * 512:(n + 1) * 512], ALU.mult),
                inc=f"ev5_{sl}")
            p.dma("gpsimd", sg_dr[oc * 128:(oc + 1) * 128, n * 512:(n + 1) * 512],
                  sev[sl][:], f"st5_{sl}", wait=(f"ev5_{sl}", p.val(f"ev5_{sl}")))

    for sl in range(2):
        p.wait("sync", f"st5_{sl}", p.val(f"st5_{sl}"))
        p.wait("scalar", f"st5_{sl}", p.val(f"st5_{sl}"))

    for ti in range(NT):
        buf = ti % 2
        wt = (f"st_x1_{buf}", p.val(f"st_x1_{buf}")) if ti >= 2 else None
        p.dma("sync", xs[buf][:], x_dr[ti * 128:(ti + 1) * 128, :], f"ld_x{buf}",
              wait=wt)
        p.dma("scalar", sgt[buf][:], sg_dr[:, ti * 128:(ti + 1) * 128],
              f"ld_sg{buf}", wait=wt, transpose=True)
        p.wait("vector", f"ld_x{buf}", p.val(f"ld_x{buf}"))
        p.wait("vector", f"ld_sg{buf}", p.val(f"ld_sg{buf}"))
        at = p.run("vector", lambda v, buf=buf: v.tensor_tensor(
            x1b[buf][:], xs[buf][:], sgt[buf][:], ALU.add), inc="x1v")
        p.dma("gpsimd", x1_dr[ti * 128:(ti + 1) * 128, :], x1b[buf][:],
              f"st_x1_{buf}", wait=("x1v", at))
    for b in range(2):
        p.wait("sync", f"st_x1_{b}", p.val(f"st_x1_{b}"))
    return p.finalize()


def build_b():
    nc = bass.Bass("TRN2", target_bir_lowering=False)
    x1_dr = nc.dram_tensor("x1", [T, D], F32, kind="ExternalInput").ap()
    win_dr = nc.dram_tensor("W_in", [D, 3 * S], F32, kind="ExternalInput").ap()
    bin_dr = nc.dram_tensor("b_in_r", [1, 3 * S], F32, kind="ExternalInput").ap()
    wx_dr = nc.dram_tensor("W_x", [S, S], F32, kind="ExternalInput").ap()
    bx_dr = nc.dram_tensor("b_x_r", [1, S], F32, kind="ExternalInput").ap()
    wdt_dr = nc.dram_tensor("W_dt", [S, S], F32, kind="ExternalInput").ap()
    bdt_dr = nc.dram_tensor("b_dt_r", [1, S], F32, kind="ExternalInput").ap()
    wnorm_dr = nc.dram_tensor("w_norm_r", [1, D], F32, kind="ExternalInput").ap()

    xn1_dr = nc.dram_tensor("xn1_tm", [T, D], F32, kind="ExternalOutput").ap()
    xn1b_dr = nc.dram_tensor("xn1b", [T, D], BF16, kind="ExternalOutput").ap()
    xr_dr = nc.dram_tensor("xr_b", [T, D], BF16, kind="Internal").ap()
    sigg_dr = nc.dram_tensor("sigg", [S, T], BF16, kind="ExternalOutput").ap()
    hs_dr = nc.dram_tensor("hs_loc", [S, T], F32, kind="ExternalOutput").ap()
    pp_dr = nc.dram_tensor("pp", [S, T], F32, kind="ExternalOutput").ap()
    ends_dr = nc.dram_tensor("ends", [S, 2], F32, kind="ExternalOutput").ap()
    amax_dr = nc.dram_tensor("amax", [1, 1], F32, kind="ExternalOutput").ap()

    p = Prog(nc)
    xs = [p.sbuf(f"xs{b}", [128, D], F32) for b in range(2)]
    xnb = [p.sbuf(f"xnb{b}", [128, D], BF16) for b in range(2)]
    xnf = [p.sbuf(f"xnf{b}", [128, D], F32) for b in range(2)]
    stat = p.sbuf("stat", [128, 8], F32)
    sq = [p.sbuf(f"sq{b}", [128, D], F32) for b in range(2)]
    xrb = [p.sbuf(f"xrb{b}", [128, D], BF16) for b in range(2)]
    epsc = p.sbuf("epsc", [128, 1], F32)
    eps5 = p.sbuf("eps5", [128, 1], F32)
    wnr = p.sbuf("wnr", [1, D], F32)
    wnb = p.sbuf("wnb", [128, D], F32)
    amx = p.sbuf("amx", [128, 2], F32)
    amr = p.sbuf("amr", [1, 128], F32)
    identf = p.sbuf("identf2", [128, 128], F32)
    win = p.sbuf("win", [128, 8 * 3 * S], BF16)
    wx = p.sbuf("wx", [128, 2 * S], BF16)
    wdt = p.sbuf("wdt", [128, 2 * S], BF16)
    brows = p.sbuf("brows", [1, 3 * S + 2 * S], F32)
    wstage = p.sbuf("wstage", [128, 3 * S], F32)
    xnt = [[p.sbuf(f"xnt{b}_{c}", [128, 512], BF16) for c in range(8)]
           for b in range(2)]
    uzg = [p.sbuf(f"uzg{j}", [128, T], BF16) for j in range(6)]
    ut = [p.sbuf(f"ut{j}", [128, T], F32) for j in range(2)]
    dt_ = [p.sbuf(f"dt{j}", [128, T], F32) for j in range(2)]
    bb = [p.sbuf(f"bb{j}", [128, T], F32) for j in range(2)]
    hs = [p.sbuf(f"hs{j}", [128, T], F32) for j in range(2)]
    pp = [p.sbuf(f"pp{j}", [128, T], F32) for j in range(2)]
    sgg = p.sbuf("sgg", [128, T], BF16)
    ends = p.sbuf("ends_s", [128, 4], F32)
    ps = [p.psum(f"ps{i}", [128, 512], F32) for i in range(3)]

    idf_dr = nc.dram_tensor("identf", [128, 128], F32, kind="ExternalInput").ap()

    p.dma("sync", wnr[:], wnorm_dr, "ld_c")
    p.dma("sync", idf_dr_s := identf[:], idf_dr, "ld_c")
    p.dma("sync", brows[0:1, 0:3 * S], bin_dr, "ld_c")
    p.dma("sync", brows[0:1, 3 * S:4 * S], bx_dr, "ld_c")
    p.dma("sync", brows[0:1, 4 * S:5 * S], bdt_dr, "ld_c")
    ncast = 0
    for kc in range(8):
        lw = p.dma("sync", wstage[:], win_dr[kc * 128:(kc + 1) * 128, :], "ld_w",
                   wait=(("cast_w", ncast) if ncast else None))
        p.wait("vector", "ld_w", lw)
        p.run("vector", lambda v, kc=kc: v.tensor_copy(
            win[:, kc * 3 * S:(kc + 1) * 3 * S], wstage[:]), inc="cast_w")
        ncast += 1
    for kc in range(2):
        lw = p.dma("sync", wstage[:, 0:S], wx_dr[kc * 128:(kc + 1) * 128, :],
                   "ld_w", wait=("cast_w", ncast))
        p.wait("vector", "ld_w", lw)
        p.run("vector", lambda v, kc=kc: v.tensor_copy(
            wx[:, kc * S:(kc + 1) * S], wstage[:, 0:S]), inc="cast_w")
        ncast += 1
        lw = p.dma("sync", wstage[:, 0:S], wdt_dr[kc * 128:(kc + 1) * 128, :],
                   "ld_w", wait=("cast_w", ncast))
        p.wait("vector", "ld_w", lw)
        p.run("vector", lambda v, kc=kc: v.tensor_copy(
            wdt[:, kc * S:(kc + 1) * S], wstage[:, 0:S]), inc="cast_w")
        ncast += 1
    p.wait("vector", "ld_c", p.val("ld_c"))
    one_c = p.sbuf("one_c", [1, 1], F32)
    mz = p.sync_run("vector", lambda v: (v.memset(one_c[:], 1.0),
                                         v.memset(epsc[:], 1e-8),
                                         v.memset(eps5[:], 1e-5),
                                         v.memset(amx[:], 0.0),
                                         v.memset(stat[:], 0.0))[-1], name="v_c")
    p.wait("tensor", "v_c", mz)
    p.wait("scalar", "v_c", mz)
    for n in range(2):
        def wb(t, n=n):
            return t.matmul(ps[2][:], one_c[0:1, 0:1].broadcast_to([1, 128])
                            if False else one_c[0:1, 0:1],
                            wnr[0:1, n * 512:(n + 1) * 512],
                            start=True, stop=True)
        pass
    ones128 = p.sbuf("ones128", [1, 128], F32)
    ones512 = p.sbuf("ones512", [1, 512], F32)
    mo = p.sync_run("vector", lambda v: (v.memset(ones128[:], 1.0),
                                         v.memset(ones512[:], 1.0))[-1], name="v_c")
    p.wait("tensor", "v_c", mo)
    for n in range(2):
        def wb(t, n=n):
            return t.matmul(ps[2][:], ones128[0:1, :],
                            wnr[0:1, n * 512:(n + 1) * 512], start=True, stop=True)
        bt = p.run("tensor", wb, inc="wb_mm")
        p.wait("vector", "wb_mm", bt)
        vt = p.run("vector", lambda v, n=n: v.tensor_copy(
            wnb[:, n * 512:(n + 1) * 512], ps[2][:]), inc="wb_ev")
        p.wait("tensor", "wb_ev", p.val("wb_ev"))

    for ti in range(NT):
        buf = ti % 2
        wt = (f"st_xn{buf}", p.val(f"st_xn{buf}")) if ti >= 2 else None
        ldv = p.dma("sync", xs[buf][:], x1_dr[ti * 128:(ti + 1) * 128, :],
                    f"ld_x{buf}", wait=wt)
        p.wait("vector", f"ld_x{buf}", ldv)
        p.sync_run("vector", lambda v, x=xs[buf]: v.reduce_sum(
            stat[:, 0:1], x[:], axis=AX.X))
        p.sync_run("vector", lambda v: v.tensor_scalar_mul(
            stat[:, 0:1], stat[:, 0:1], 1.0 / D))
        ct = p.run("vector", lambda v, x=xs[buf]: v.tensor_scalar(
            x[:], x[:], stat[:, 0:1], None, ALU.subtract), inc="v_ln")
        p.wait("scalar", "v_ln", ct)
        if ti >= 2:
            p.wait("scalar", f"st_xr{buf}", p.val(f"st_xr{buf}"))
        p.sync_run("scalar", lambda s: s.memzero(stat[:, 2:3]))
        p.sync_run("scalar", lambda s, x=xs[buf], q=sq[buf]: s.activation(
            q[:], x[:], AF.Square, accum_out=stat[:, 2:3]))
        if p.val("sr_vector"):
            p.wait("scalar", "sr_vector", p.val("sr_vector"))
        rt = p.run("scalar", lambda s: s.activation(
            stat[:, 4:5], stat[:, 2:3], AF.Sqrt, scale=1.0 / D, bias=eps5[:, 0:1]),
            inc="s_rs")
        p.wait("vector", "s_rs", rt)
        p.sync_run("vector", lambda v: v.reciprocal(stat[:, 5:6], stat[:, 4:5]))
        p.sync_run("vector", lambda v, x=xs[buf], xf=xnf[buf]: v.tensor_scalar(
            xf[:], x[:], stat[:, 5:6], None, ALU.mult))
        at = p.run("vector", lambda v, xf=xnf[buf]: v.reduce_max(
            stat[:, 6:7], xf[:], axis=AX.X, apply_absolute_value=True), inc="v_am")
        p.wait("vector", "v_am", at)
        p.sync_run("vector", lambda v: v.tensor_tensor(
            amx[:, 0:1], amx[:, 0:1], stat[:, 6:7], ALU.max))
        nt_ = p.run("vector", lambda v, xf=xnf[buf], xb=xnb[buf]:
                    v.tensor_copy(xb[:], xf[:]), inc="v_nm")
        p.dma("gpsimd", xn1_dr[ti * 128:(ti + 1) * 128, :], xnf[buf][:],
              f"st_xn{buf}", wait=("v_nm", nt_))
        p.dma("gpsimd", xn1b_dr[ti * 128:(ti + 1) * 128, :], xnb[buf][:],
              f"st_xn{buf}")
        p.wait("scalar", "sr_vector", p.val("sr_vector"))
        p.sync_run("scalar", lambda s: s.memzero(stat[:, 3:4]))
        p.sync_run("scalar", lambda s, xf=xnf[buf], q=sq[buf]: s.activation(
            q[:], xf[:], AF.Square, accum_out=stat[:, 3:4]))
        rt2 = p.run("scalar", lambda s: s.activation(
            stat[:, 7:8], stat[:, 3:4], AF.Sqrt, scale=1.0 / D, bias=epsc[:, 0:1]),
            inc="s_rs")
        p.wait("vector", "s_rs", rt2)
        p.sync_run("vector", lambda v: v.reciprocal(stat[:, 1:2], stat[:, 7:8]))
        p.sync_run("vector", lambda v, xf=xnf[buf], q=sq[buf]: v.tensor_scalar(
            q[:], xf[:], stat[:, 1:2], None, ALU.mult))
        p.sync_run("vector", lambda v, q=sq[buf]: v.tensor_tensor(
            q[:], q[:], wnb[:], ALU.mult))
        xr_ = p.run("vector", lambda v, q=sq[buf], xb=xrb[buf]:
                    v.tensor_copy(xb[:], q[:]), inc="v_xr")
        p.dma("gpsimd", xr_dr[ti * 128:(ti + 1) * 128, :], xrb[buf][:],
              f"st_xr{buf}", wait=("v_xr", xr_))
    amt = p.val("sr_vector")
    p.wait("tensor", "sr_vector", amt)
    at2 = p.run("tensor", lambda t: t.matmul(ps[2][0:1, 0:128], amx[:, 0:1],
                                             identf[:], start=True, stop=True),
                inc="am_mm")
    p.wait("vector", "am_mm", at2)
    p.sync_run("vector", lambda v: v.reduce_max(amr[0:1, 0:1],
                                                ps[2][0:1, 0:128], axis=AX.X))
    av = p.run("vector", lambda v: v.tensor_copy(amr[0:1, 1:2], amr[0:1, 0:1]),
               inc="v_amf")
    p.dma("gpsimd", amax_dr, amr[0:1, 0:1], "st_am", wait=("v_amf", av))

    xn_hist = []
    for g in range(4):
        buf = g % 2
        if g == 0:
            for b2 in range(2):
                p.wait("tensor", f"st_xr{b2}", p.val(f"st_xr{b2}"))
                p.wait("scalar", f"st_xr{b2}", p.val(f"st_xr{b2}"))
        for c in range(8):
            wt = None
            if g >= 2 and c == 0:
                wt = ("mm_b", xn_hist[g - 2])
            p.dma("scalar", xnt[buf][c][:],
                  xr_dr[g * 512:(g + 1) * 512, c * 128:(c + 1) * 128],
                  f"ld_xt{buf}", wait=wt, transpose=True)
        p.wait("tensor", f"ld_xt{buf}", p.val(f"ld_xt{buf}"))
        p.wait("tensor", "cast_w", ncast)
        for j in range(6):
            sl = j % 2
            if p.val(f"evb_{sl}"):
                p.wait("tensor", f"evb_{sl}", p.val(f"evb_{sl}"))

            def mmi(t, g=g, j=j, sl=sl, buf=buf):
                oc = j
                for c in range(8):
                    t.matmul(ps[sl][:],
                             win[:, c * 3 * S + oc * 128:
                                 c * 3 * S + (oc + 1) * 128],
                             xnt[buf][c][:], start=(c == 0), stop=False)
                return t.matmul(ps[sl][:], brows[0:1, oc * 128:(oc + 1) * 128],
                                ones512[0:1, :], start=False, stop=True)
            mt = p.run("tensor", mmi, inc="mm_b")
            p.wait("scalar", "mm_b", mt)
            func = AF.Sigmoid if j >= 4 else AF.Copy

            def ev(s, g=g, j=j, sl=sl, func=func):
                return s.activation(uzg[j][:, g * 512:(g + 1) * 512], ps[sl][:],
                                    func)
            p.run("scalar", ev, inc=f"evb_{sl}")
        xn_hist.append(p.val("mm_b"))
    for sl in range(2):
        p.wait("tensor", f"evb_{sl}", p.val(f"evb_{sl}"))

    ev_all = [p.val("evb_0"), p.val("evb_1")]
    p.wait("gpsimd", "evb_0", ev_all[0])
    p.wait("gpsimd", "evb_1", ev_all[1])
    for j in range(2):
        p.dma("gpsimd", sigg_dr[j * 128:(j + 1) * 128, :], uzg[4 + j][:], "st_sg")

    p.wait("tensor", "sr_vector", p.val("sr_vector"))
    for g in range(4):
        for j in range(2):
            for which in range(2):
                sl = 2 if which == 0 else 0
                if p.val(f"evc_{sl}"):
                    p.wait("tensor", f"evc_{sl}", p.val(f"evc_{sl}"))

                def mmx(t, g=g, j=j, which=which, sl=sl):
                    w = wx if which == 0 else wdt
                    srcv = uzg[0:2] if which == 0 else uzg[2:4]
                    boff = (3 * S if which == 0 else 4 * S) + j * 128
                    for c in range(2):
                        t.matmul(ps[sl][:],
                                 w[:, c * S + j * 128:c * S + (j + 1) * 128],
                                 srcv[c][:, g * 512:(g + 1) * 512],
                                 start=(c == 0), stop=False)
                    return t.matmul(ps[sl][:], brows[0:1, boff:boff + 128],
                                    ones512[0:1, :], start=False, stop=True)
                mt = p.run("tensor", mmx, inc="mm_b")
                p.wait("scalar", "mm_b", mt)

                def evx(s, g=g, j=j, which=which, sl=sl):
                    if which == 0:
                        return s.activation(ut[j][:, g * 512:(g + 1) * 512],
                                            ps[sl][:], AF.Tanh)
                    return s.activation(dt_[j][:, g * 512:(g + 1) * 512],
                                        ps[sl][:], AF.Sigmoid)
                p.run("scalar", evx, inc=f"evc_{sl}")
    for sl in (0, 2):
        p.wait("vector", f"evc_{sl}", p.val(f"evc_{sl}"))

    for j in range(2):
        p.sync_run("vector", lambda v, j=j: v.tensor_tensor(
            bb[j][:], dt_[j][:], ut[j][:], ALU.mult))
        p.sync_run("vector", lambda v, j=j: v.tensor_scalar(
            dt_[j][:], dt_[j][:], -1.0, 1.0, ALU.mult, ALU.add))
        p.sync_run("vector", lambda v, j=j: v.tensor_tensor_scan(
            hs[j][:], dt_[j][:], bb[j][:], 0.0, ALU.mult, ALU.add))
        p.sync_run("vector", lambda v, j=j: v.tensor_tensor_scan(
            pp[j][:], dt_[j][:], dt_[j][:], 1.0, ALU.mult, ALU.bypass))
        et = p.run("vector", lambda v, j=j: (
            v.tensor_copy(ends[:, 2 * j:2 * j + 1], pp[j][:, T - 1:T]),
            v.tensor_copy(ends[:, 2 * j + 1:2 * j + 2], hs[j][:, T - 1:T]))[-1],
            inc="v_end")
        p.dma("gpsimd", hs_dr[j * 128:(j + 1) * 128, :], hs[j][:], "st_s",
              wait=("v_end", p.val("v_end")))
        p.dma("gpsimd", pp_dr[j * 128:(j + 1) * 128, :], pp[j][:], "st_s")
    p.dma("gpsimd", ends_dr[0:128, :], ends[:, 0:2], "st_s")
    p.dma("gpsimd", ends_dr[128:256, :], ends[:, 2:4], "st_s")
    p.wait("sync", "st_s", p.val("st_s"))
    p.wait("sync", "st_sg", p.val("st_sg"))
    p.wait("sync", "st_am", p.val("st_am"))
    for b in range(2):
        p.wait("sync", f"st_xn{b}", p.val(f"st_xn{b}"))
        p.wait("sync", f"st_xr{b}", p.val(f"st_xr{b}"))
    return p.finalize()


def build_c():
    nc = bass.Bass("TRN2", target_bir_lowering=False)
    x1_dr = nc.dram_tensor("x1", [T, D], F32, kind="ExternalInput").ap()
    xn1_dr = nc.dram_tensor("xn1_tm", [T, D], F32, kind="ExternalInput").ap()
    xn1b_dr = nc.dram_tensor("xn1b", [T, D], BF16, kind="ExternalInput").ap()
    hs_dr = nc.dram_tensor("hs_loc", [S, T], F32, kind="ExternalInput").ap()
    pp_dr = nc.dram_tensor("pp", [S, T], F32, kind="ExternalInput").ap()
    sigg_dr = nc.dram_tensor("sigg", [S, T], BF16, kind="ExternalInput").ap()
    endsp_dr = nc.dram_tensor("ends_prev", [S, 6], F32, kind="ExternalInput").ap()
    amax8_dr = nc.dram_tensor("amax8", [1, 8], F32, kind="ExternalInput").ap()
    wout1_dr = nc.dram_tensor("W_out1", [S, D], F32, kind="ExternalInput").ap()
    bout1_dr = nc.dram_tensor("b_out1_r", [1, D], F32, kind="ExternalInput").ap()
    wfT_dr = nc.dram_tensor("wf_T", [D, D], F32, kind="ExternalInput").ap()
    bf_dr = nc.dram_tensor("b_f_r", [1, D], F32, kind="ExternalInput").ap()
    ones_dr = nc.dram_tensor("ones_r", [1, 512], F32, kind="ExternalInput").ap()
    out_dr = nc.dram_tensor("out", [T, D], F32, kind="ExternalOutput").ap()
    xdq_dr = nc.dram_tensor("xdq_tm", [T, D], BF16, kind="Internal").ap()
    sg1_dr = nc.dram_tensor("sg1_fm", [D, T], BF16, kind="Internal").ap()

    p = Prog(nc)
    hs = [p.sbuf(f"hs{j}", [128, T], F32) for j in range(2)]
    pp = [p.sbuf(f"pp{j}", [128, T], F32) for j in range(2)]
    sgg = [p.sbuf(f"sgg{j}", [128, T], BF16) for j in range(2)]
    yy = [p.sbuf(f"yy{j}", [128, T], BF16) for j in range(2)]
    endsp = p.sbuf("endsp", [128, 12], F32)
    hin = p.sbuf("hin", [128, 2], F32)
    am8 = p.sbuf("am8", [1, 8], F32)
    scl8 = p.sbuf("scl8", [1, 4], F32)
    scol = p.sbuf("scol", [128, 2], F32)
    wstage = p.sbuf("wstage", [128, D], F32)
    wo1 = p.sbuf("wo1", [128, 2 * D], BF16)
    wfq = [p.sbuf(f"wfq{c}", [128, D], BF16) for c in range(8)]
    bo1 = p.sbuf("bo1", [1, D], F32)
    bfr = p.sbuf("bfr", [1, D], F32)
    onesr = p.sbuf("onesr", [1, 512], F32)
    onesc = p.sbuf("onesc", [1, 128], F32)
    s1f = [p.sbuf(f"s1f{c}", [128, T], BF16) for c in range(8)]
    xnf1 = [p.sbuf(f"xnf1{c}", [128, T], BF16) for c in range(8)]
    xst = [p.sbuf(f"xst{b}", [128, D], F32) for b in range(2)]
    f8t = p.sbuf("f8t", [128, D], F8)
    xdqb = [p.sbuf(f"xdqb{b}", [128, D], BF16) for b in range(2)]
    wst8 = p.sbuf("wst8", [128, D], F32)
    amw = p.sbuf("amw", [128, 2], F32)
    amwr = p.sbuf("amwr", [1, 128], F32)
    identf = p.sbuf("identfc", [128, 128], F32)
    xq8 = [[p.sbuf(f"xq8{b}_{c}", [128, 512], BF16) for c in range(8)]
           for b in range(2)]
    g1ev = [p.sbuf(f"g1ev{i}", [128, 512], BF16) for i in range(2)]
    sg1t = [p.sbuf(f"sg1t{b}", [128, D], BF16) for b in range(2)]
    outb = [p.sbuf(f"outb{b}", [128, D], F32) for b in range(2)]
    idf_dr = nc.dram_tensor("identf", [128, 128], F32, kind="ExternalInput").ap()
    ps = [p.psum(f"ps{i}", [128, 512], F32) for i in range(3)]

    for j in range(2):
        p.dma("sync", hs[j][:], hs_dr[j * 128:(j + 1) * 128, :], "ld_a")
        p.dma("sync", pp[j][:], pp_dr[j * 128:(j + 1) * 128, :], "ld_a")
        p.dma("sync", sgg[j][:], sigg_dr[j * 128:(j + 1) * 128, :], "ld_a")
    p.dma("sync", endsp[0:128, 0:6], endsp_dr[0:128, :], "ld_a")
    p.dma("sync", endsp[0:128, 6:12], endsp_dr[128:256, :], "ld_a")
    p.dma("sync", am8[:], amax8_dr, "ld_a")
    p.dma("sync", bo1[:], bout1_dr, "ld_a")
    p.dma("sync", bfr[:], bf_dr, "ld_a")
    p.dma("sync", onesr[:], ones_dr, "ld_a")
    p.dma("sync", identf[:], idf_dr, "ld_a")
    ncast = 0
    for kc in range(2):
        lw = p.dma("sync", wstage[:], wout1_dr[kc * 128:(kc + 1) * 128, :], "ld_w",
                   wait=(("cast_w", ncast) if ncast else None))
        p.wait("vector", "ld_w", lw)
        p.run("vector", lambda v, kc=kc: v.tensor_copy(
            wo1[:, kc * D:(kc + 1) * D], wstage[:]), inc="cast_w")
        ncast += 1
    p.wait("vector", "ld_a", p.val("ld_a"))
    mz = p.sync_run("vector", lambda v: (v.memset(onesc[:], 1.0),
                                         v.memset(amw[:], 0.0))[-1])

    for j in range(2):
        sl6 = slice(6 * j, 6 * j + 6)
        e = endsp[:, 6 * j:6 * j + 6]
        p.sync_run("vector", lambda v, j=j, e=e: v.tensor_copy(
            hin[:, j:j + 1], e[:, 5:6]))
        p.sync_run("vector", lambda v, j=j, e=e: v.tensor_tensor(
            hin[:, j:j + 1], hin[:, j:j + 1], e[:, 2:3], ALU.mult))
        p.sync_run("vector", lambda v, j=j, e=e: v.tensor_tensor(
            hin[:, j:j + 1], hin[:, j:j + 1], e[:, 3:4], ALU.add))
        p.sync_run("vector", lambda v, j=j, e=e: v.tensor_tensor(
            hin[:, j:j + 1], hin[:, j:j + 1], e[:, 0:1], ALU.mult))
        p.sync_run("vector", lambda v, j=j, e=e: v.tensor_tensor(
            hin[:, j:j + 1], hin[:, j:j + 1], e[:, 1:2], ALU.add))
        p.sync_run("vector", lambda v, j=j: v.scalar_tensor_tensor(
            hs[j][:], pp[j][:], hin[:, j:j + 1], hs[j][:], ALU.mult, ALU.add))
        yt = p.run("vector", lambda v, j=j: v.tensor_tensor(
            yy[j][:], hs[j][:], sgg[j][:], ALU.mult), inc="v_y")
    p.wait("tensor", "v_y", p.val("v_y"))
    p.wait("tensor", "cast_w", ncast)

    for c in range(8):
        p.dma("scalar", xnf1[c][:], xn1b_dr[:, c * 128:(c + 1) * 128], "ld_xT",
              transpose=True)
    p.wait("vector", "ld_xT", p.val("ld_xT"))
    for n in range(4):
        for oc in range(8):
            sl = oc % 2
            if p.val(f"evo_{sl}"):
                p.wait("tensor", f"evo_{sl}", p.val(f"evo_{sl}"))

            def mmo(t, n=n, oc=oc, sl=sl):
                for j in range(2):
                    t.matmul(ps[sl][:], wo1[:, j * D + oc * 128:
                                            j * D + (oc + 1) * 128],
                             yy[j][:, n * 512:(n + 1) * 512],
                             start=(j == 0), stop=False)
                return t.matmul(ps[sl][:], bo1[0:1, oc * 128:(oc + 1) * 128],
                                onesr[0:1, 0:512], start=False, stop=True)
            mt = p.run("tensor", mmo, inc="mm_c")
            p.wait("vector", "mm_c", mt)
            et = p.run("vector", lambda v, n=n, oc=oc, sl=sl: v.tensor_tensor(
                s1f[oc][:, n * 512:(n + 1) * 512], ps[sl][:],
                xnf1[oc][:, n * 512:(n + 1) * 512], ALU.add), inc=f"evo_{sl}")

    p.sync_run("vector", lambda v: v.reduce_max(scl8[0:1, 0:1], am8[0:1, :],
                                                axis=AX.X))
    p.sync_run("vector", lambda v: v.tensor_scalar(
        scl8[0:1, 1:2], scl8[0:1, 0:1], 1e-12, None, ALU.max))
    p.sync_run("vector", lambda v: v.reciprocal(scl8[0:1, 2:3], scl8[0:1, 1:2]))
    p.sync_run("vector", lambda v: v.tensor_scalar(
        scl8[0:1, 2:3], scl8[0:1, 2:3], 224.0, None, ALU.mult))
    sc8 = p.sync_run("vector", lambda v: v.tensor_scalar(
        scl8[0:1, 3:4], scl8[0:1, 1:2], 1.0 / 224.0, None, ALU.mult))
    p.wait("tensor", "sr_vector", sc8)
    bt = p.run("tensor", lambda t: t.matmul(ps[2][:, 0:2], onesc[0:1, :],
                                            scl8[0:1, 2:4], start=True, stop=True),
               inc="mm_c")
    p.wait("vector", "mm_c", p.val("mm_c"))
    p.sync_run("vector", lambda v: v.tensor_copy(scol[:], ps[2][:, 0:2]))

    for ti in range(NT):
        buf = ti % 2
        wt = (f"st_xd{buf}", p.val(f"st_xd{buf}")) if ti >= 2 else None
        ldv = p.dma("sync", xst[buf][:], xn1_dr[ti * 128:(ti + 1) * 128, :],
                    f"ld_xq{buf}", wait=wt)
        p.wait("vector", f"ld_xq{buf}", ldv)
        if p.val("v_q8"):
            p.wait("vector", "v_q8", p.val("v_q8"))
        p.sync_run("vector", lambda v, buf=buf: v.tensor_scalar(
            f8t[:], xst[buf][:], scol[:, 0:1], None, ALU.mult))
        qt = p.run("vector", lambda v, buf=buf: v.tensor_scalar(
            xdqb[buf][:], f8t[:], scol[:, 1:2], None, ALU.mult), inc="v_q8")
        p.dma("gpsimd", xdq_dr[ti * 128:(ti + 1) * 128, :], xdqb[buf][:],
              f"st_xd{buf}", wait=("v_q8", qt))

    for kc in range(8):
        lw = p.dma("sync", wst8[:], wfT_dr[kc * 128:(kc + 1) * 128, :], "ld_w8",
                   wait=(("w8_ev", p.val("w8_ev")) if kc else None))
        p.wait("vector", "ld_w8", lw)
        p.sync_run("vector", lambda v: v.reduce_max(amw[:, 1:2], wst8[:],
                                                    axis=AX.X,
                                                    apply_absolute_value=True))
        p.run("vector", lambda v: v.tensor_tensor(
            amw[:, 0:1], amw[:, 0:1], amw[:, 1:2], ALU.max), inc="w8_ev")
        p.wait("vector", "w8_ev", p.val("w8_ev"))
    aw = p.val("w8_ev")
    p.wait("tensor", "w8_ev", aw)
    p.run("tensor", lambda t: t.matmul(ps[2][0:1, 0:128], amw[:, 0:1], identf[:],
                                       start=True, stop=True), inc="mm_c")
    p.wait("vector", "mm_c", p.val("mm_c"))
    p.sync_run("vector", lambda v: v.reduce_max(scl8[0:1, 0:1],
                                                ps[2][0:1, 0:128], axis=AX.X))
    p.sync_run("vector", lambda v: v.tensor_scalar(
        scl8[0:1, 1:2], scl8[0:1, 0:1], 1e-12, None, ALU.max))
    p.sync_run("vector", lambda v: v.reciprocal(scl8[0:1, 2:3], scl8[0:1, 1:2]))
    p.sync_run("vector", lambda v: v.tensor_scalar(
        scl8[0:1, 2:3], scl8[0:1, 2:3], 224.0, None, ALU.mult))
    sw8 = p.sync_run("vector", lambda v: v.tensor_scalar(
        scl8[0:1, 3:4], scl8[0:1, 1:2], 1.0 / 224.0, None, ALU.mult))
    p.wait("tensor", "sr_vector", sw8)
    p.run("tensor", lambda t: t.matmul(ps[2][:, 0:2], onesc[0:1, :],
                                       scl8[0:1, 2:4], start=True, stop=True),
          inc="mm_c")
    p.wait("vector", "mm_c", p.val("mm_c"))
    p.sync_run("vector", lambda v: v.tensor_copy(scol[:], ps[2][:, 0:2]))
    for kc in range(8):
        wt8 = ("w8q_ev", p.val("w8q_ev")) if kc else ("w8_ev", p.val("w8_ev"))
        lw = p.dma("sync", wst8[:], wfT_dr[kc * 128:(kc + 1) * 128, :], "ld_w8b",
                   wait=wt8)
        p.wait("vector", "ld_w8b", lw)
        if kc == 0 and p.val("v_q8"):
            p.wait("vector", "v_q8", p.val("v_q8"))
        if p.val("w8q_ev"):
            p.wait("vector", "w8q_ev", p.val("w8q_ev"))
        p.sync_run("vector", lambda v: v.tensor_scalar(
            f8t[:], wst8[:], scol[:, 0:1], None, ALU.mult))
        p.run("vector", lambda v, kc=kc: v.tensor_scalar(
            wfq[kc][:], f8t[:], scol[:, 1:2], None, ALU.mult), inc="w8q_ev")
    p.wait("tensor", "w8q_ev", p.val("w8q_ev"))

    xq_hist = []
    for g in range(4):
        buf = g % 2
        if g == 0:
            for b2 in range(2):
                p.wait("tensor", f"st_xd{b2}", p.val(f"st_xd{b2}"))
                p.wait("scalar", f"st_xd{b2}", p.val(f"st_xd{b2}"))
        for c in range(8):
            wt = None
            if g >= 2 and c == 0:
                wt = ("mm_c", xq_hist[g - 2])
            p.dma("scalar", xq8[buf][c][:],
                  xdq_dr[g * 512:(g + 1) * 512, c * 128:(c + 1) * 128],
                  f"ld_q8{buf}", wait=wt, transpose=True)
        p.wait("tensor", f"ld_q8{buf}", p.val(f"ld_q8{buf}"))
        for oc in range(8):
            sl = oc % 2
            if p.val(f"evg_{sl}"):
                p.wait("tensor", f"evg_{sl}", p.val(f"evg_{sl}"))

            def mmf(t, g=g, oc=oc, sl=sl, buf=buf):
                for c in range(8):
                    t.matmul(ps[sl][:], wfq[c][:, oc * 128:(oc + 1) * 128],
                             xq8[buf][c][:], start=(c == 0), stop=False)
                return t.matmul(ps[sl][:], bfr[0:1, oc * 128:(oc + 1) * 128],
                                onesr[0:1, 0:512], start=False, stop=True)
            mt = p.run("tensor", mmf, inc="mm_c")
            p.wait("scalar", "mm_c", mt)
            if p.val(f"st_sg1_{sl}"):
                p.wait("scalar", f"st_sg1_{sl}", p.val(f"st_sg1_{sl}"))
            et = p.run("scalar", lambda s, sl=sl: s.activation(
                g1ev[sl][:], ps[sl][:], AF.Silu), inc=f"evg_{sl}")
            p.wait("vector", f"evg_{sl}", p.val(f"evg_{sl}"))
            st2 = p.run("vector", lambda v, g=g, oc=oc, sl=sl: v.tensor_tensor(
                g1ev[sl][:], g1ev[sl][:], s1f[oc][:, g * 512:(g + 1) * 512],
                ALU.mult), inc=f"evg_{sl}")
            p.dma("gpsimd", sg1_dr[oc * 128:(oc + 1) * 128, g * 512:(g + 1) * 512],
                  g1ev[sl][:], f"st_sg1_{sl}", wait=(f"evg_{sl}", st2))
        xq_hist.append(p.val("mm_c"))
    for sl in range(2):
        p.wait("sync", f"st_sg1_{sl}", p.val(f"st_sg1_{sl}"))
        p.wait("scalar", f"st_sg1_{sl}", p.val(f"st_sg1_{sl}"))

    for ti in range(NT):
        buf = ti % 2
        wt = (f"st_o{buf}", p.val(f"st_o{buf}")) if ti >= 2 else None
        p.dma("sync", xst[buf][:], x1_dr[ti * 128:(ti + 1) * 128, :],
              f"ld_f{buf}", wait=wt)
        p.dma("scalar", sg1t[buf][:], sg1_dr[:, ti * 128:(ti + 1) * 128],
              f"ld_g{buf}", wait=wt, transpose=True)
        p.wait("vector", f"ld_f{buf}", p.val(f"ld_f{buf}"))
        p.wait("vector", f"ld_g{buf}", p.val(f"ld_g{buf}"))
        at = p.run("vector", lambda v, buf=buf: v.tensor_tensor(
            outb[buf][:], xst[buf][:], sg1t[buf][:], ALU.add), inc="v_o")
        p.dma("gpsimd", out_dr[ti * 128:(ti + 1) * 128, :], outb[buf][:],
              f"st_o{buf}", wait=("v_o", at))
    for b in range(2):
        p.wait("sync", f"st_o{b}", p.val(f"st_o{b}"))
    return p.finalize()


def host_prep(inp):
    x = np.asarray(inp["L0_attn_x"])
    groups = LOCAL_OFFS + LARGE_OFFS
    perm = []
    indic = np.zeros((44, 24), np.float32)
    all_offs = [(1 << j) * t for j in range(11) for t in range(4)]
    r = 0
    for gi, off in enumerate(groups):
        for k, o in enumerate(all_offs):
            if o == off:
                perm.append(k)
                indic[r, gi] = 1.0
                r += 1
    assert r == 44
    posb_perm = np.ascontiguousarray(np.asarray(inp["L0_attn_pos_bias"])[perm])
    indic_b = indic.astype(ml_dtypes.bfloat16)

    wg = np.asarray(inp["L0_ffn_w_gate"])
    wu = np.asarray(inp["L0_ffn_w_up"])
    wd = np.asarray(inp["L0_ffn_w_down"])
    b_qkv = np.asarray(inp["L0_attn_b_qkv"])
    b_gate = np.asarray(inp["L0_attn_b_gate"])

    per_core = []
    for c in range(NCORES):
        seq, chunk = divmod(c, 4)
        start = chunk * T
        vstart = max(0, start - HALO)
        pad = HALO - (start - vstart)
        xp = np.zeros((TT, D), np.float32)
        xp[pad:] = x[seq, vstart:start + T]
        valid = np.zeros(TT, np.float32)
        valid[pad:] = 1.0
        per_core.append(dict(
            xpad=xp,
            maskcol=np.ascontiguousarray(valid.reshape(NTT, 128).T),
            validrow=valid.reshape(1, TT).copy(),
            wg_sh=np.ascontiguousarray(wg[c * WQ_SH:(c + 1) * WQ_SH]),
            wu_sh=np.ascontiguousarray(wu[c * WQ_SH:(c + 1) * WQ_SH]),
            wd_sh=np.ascontiguousarray(wd[c * 128:(c + 1) * 128]),
            posb=posb_perm,
            indic=indic_b,
            bq_r=np.ascontiguousarray(b_qkv[None, 0:D]),
            bk_r=np.ascontiguousarray(b_qkv[None, D:2 * D]),
            bv_r=np.ascontiguousarray(b_qkv[None, 2 * D:3 * D]),
            bg_r=np.ascontiguousarray(b_gate[None, :]),
        ))
    return per_core


def _make_consts():
    onehot = np.zeros((128, 16 * 16), np.float32)
    for h in range(16):
        onehot[:, h * 16 + h] = 1.0
    indc = np.zeros((128, 8 * 16), np.float32)
    for c in range(8):
        indc[0:64, c * 16 + 2 * c] = 1.0
        indc[64:128, c * 16 + 2 * c + 1] = 1.0
    indzl = np.zeros((128, 16), np.float32)
    for q in range(4):
        for h in range(16):
            indzl[32 * q + h, h] = 1.0
    identf = np.eye(128, dtype=np.float32)
    indh = np.zeros((16, 8 * 128), np.float32)
    for c in range(8):
        for h in (2 * c, 2 * c + 1):
            for dd in range(64):
                indh[h, c * 128 + (h % 2) * 64 + dd] = 1.0
    return dict(onehot=onehot.astype(ml_dtypes.bfloat16),
                indc=indc.astype(ml_dtypes.bfloat16), indzl=indzl,
                identf=identf, indh=indh)


def _wmask_from_wdrev(wdrev):
    wm = np.zeros((128, 2 * 16 * 128), dtype=wdrev.dtype)
    cc, mm_ = np.meshgrid(np.arange(128), np.arange(128), indexing="ij")
    for h in range(16):
        hg, j = divmod(h, 4)
        par, jj = j % 2, j // 2
        for cti, base in [(0, 256), (1, 128)]:
            idx = np.clip(base + mm_ - cc, 0, 383)
            pos = par * 2048 + hg * 512 + cti * 256 + jj * 128
            wm[:, pos:pos + 128] = wdrev[h][idx]
    return wm


def kernel(**inputs):
    from concourse.bass_utils import run_bass_kernel_spmd
    inp = inputs
    C = _make_consts()
    pc = host_prep(inp)
    cores = list(range(8))

    def launch(build, maps):
        return run_bass_kernel_spmd(build(), maps, core_ids=cores).results

    r1 = launch(build_a1,
                [{k: pc[c][k] for k in ["xpad", "maskcol", "wg_sh", "wu_sh",
                                        "wd_sh", "posb", "indic"]}
                 for c in range(8)])
    wq = np.ascontiguousarray(np.asarray(inp["L0_attn_w_qkv"], np.float32))
    wg = np.ascontiguousarray(np.asarray(inp["L0_attn_w_gate"], np.float32))
    r2 = launch(build_a2,
                [dict(xnb=r1[c]["xnb"], w_qkv=wq, w_gate=wg,
                      bq_r=pc[c]["bq_r"], bk_r=pc[c]["bk_r"],
                      bv_r=pc[c]["bv_r"], bg_r=pc[c]["bg_r"],
                      validrow=pc[c]["validrow"]) for c in range(8)])
    r3 = launch(build_a3,
                [dict(q_fm=r2[c]["q_fm"], k_fm=r2[c]["k_fm"], v_tm=r2[c]["v_tm"],
                      wmask_full=_wmask_from_wdrev(r1[c]["wdrev"]),
                      lnw8=r1[c]["lnw8"], onehot=C["onehot"], indc=C["indc"],
                      indzl=C["indzl"], identf=C["identf"]) for c in range(8)])
    wqgT = np.ascontiguousarray(np.concatenate([r1[c]["wqg"] for c in range(8)], 0).T)
    wquT = np.ascontiguousarray(np.concatenate([r1[c]["wqu"] for c in range(8)], 0).T)
    wqdT = np.ascontiguousarray(np.concatenate([r1[c]["wqd"] for c in range(8)], 0).T)
    r4 = launch(build_a4,
                [dict(xq_tm=r1[c]["xq"], wqgT=wqgT, wquT=wquT, wqdT=wqdT)
                 for c in range(8)])
    ones512 = np.ones((1, 512), np.float32)
    x = np.asarray(inp["L0_attn_x"], np.float32)
    xown = [np.ascontiguousarray(x[c // 4, (c % 4) * T:(c % 4 + 1) * T])
            for c in range(8)]
    w_out0 = np.ascontiguousarray(np.asarray(inp["L0_attn_w_out"], np.float32))
    b_out0 = np.asarray(inp["L0_attn_b_out"], np.float32)
    r5 = launch(build_a5,
                [dict(glo_fm=r3[c]["glo_fm"], olg_tm=r3[c]["olg_tm"],
                      z_all=r3[c]["z_all"], gate_fm=r2[c]["gate_fm"],
                      g0_fm=r4[c]["g0_fm"], w_out=w_out0,
                      b_out_r=b_out0[None, :],
                      ones_r=ones512, indh=C["indh"], x_own=xown[c])
                 for c in range(8)])
    rB = launch(build_b,
                [dict(x1=r5[c]["x1"],
                      W_in=np.ascontiguousarray(np.asarray(inp["L1_attn_W_in"], np.float32)),
                      b_in_r=np.asarray(inp["L1_attn_b_in"], np.float32)[None, :],
                      W_x=np.ascontiguousarray(np.asarray(inp["L1_attn_W_x"], np.float32)),
                      b_x_r=np.asarray(inp["L1_attn_b_x"], np.float32)[None, :],
                      W_dt=np.ascontiguousarray(np.asarray(inp["L1_attn_W_dt"], np.float32)),
                      b_dt_r=np.asarray(inp["L1_attn_b_dt"], np.float32)[None, :],
                      w_norm_r=np.asarray(inp["L1_attn_w_norm"], np.float32)[None, :],
                      identf=C["identf"]) for c in range(8)])
    amax8 = np.array([[rB[c]["amax"][0, 0] for c in range(8)]], np.float32)
    maps_c = []
    for c in range(8):
        seq, ch = divmod(c, 4)
        ends_prev = np.zeros((S, 6), np.float32)
        for jprev in range(1, 4):
            cp = c - jprev
            if cp // 4 == seq and cp >= 0:
                ends_prev[:, 2 * (jprev - 1):2 * jprev] = rB[cp]["ends"]
        maps_c.append(dict(
            x1=r5[c]["x1"], xn1_tm=rB[c]["xn1_tm"], xn1b=rB[c]["xn1b"],
            hs_loc=rB[c]["hs_loc"], pp=rB[c]["pp"], sigg=rB[c]["sigg"],
            ends_prev=ends_prev, amax8=amax8,
            W_out1=np.ascontiguousarray(np.asarray(inp["L1_attn_W_out"], np.float32)),
            b_out1_r=np.asarray(inp["L1_attn_b_out"], np.float32)[None, :],
            wf_T=np.ascontiguousarray(np.asarray(inp["L1_ffn_weight"], np.float32).T),
            b_f_r=np.asarray(inp["L1_ffn_bias"], np.float32)[None, :],
            ones_r=ones512, identf=C["identf"]))
    rC = launch(build_c, maps_c)
    out = np.zeros((2, 8192, 1024), np.float32)
    for c in range(8):
        seq, ch = divmod(c, 4)
        out[seq, ch * T:(ch + 1) * T] = rC[c]["out"]
    return out
